# version 03
# baseline (speedup 1.0000x reference)
import os
import numpy as np
from contextlib import ExitStack

import concourse.bass as bass
import concourse.mybir as mybir
from concourse.bass_utils import run_bass_kernel_spmd
from concourse.alu_op_type import AluOpType as A
from concourse.expressions import binary_op, make_scalar_value, RuntimeValue

F32 = mybir.dt.float32
BF16 = mybir.dt.bfloat16
I32 = mybir.dt.int32
AF = mybir.ActivationFunctionType
AX = mybir.AxisListType

S = 4096
HS = 2048
D = 1024
DH = 1024
NST = 16
DTR = 64
TB = 512
NBLK = 8
OBLK = 4
NG = 8
FFI = 2816
NFI = 22
EPS_LN = 1e-5
EPS_RMS = 1e-6

GROUPS_ALL = [list(range(8))]
GROUPS_PAIR = [[0, 1], [2, 3], [4, 5], [6, 7]]


def _rv(expr, mx=(1 << 30)):
    return RuntimeValue(expr, min_val=0, max_val=mx)


def build_nc(debug=()):
    nc = bass.Bass("TRN2", target_bir_lowering=False, debug=False)
    es = ExitStack()

    def din(name, shape, dt=F32):
        return nc.dram_tensor(name, list(shape), dt, kind="ExternalInput").ap()

    def dint(name, shape, dt=F32, shared=False):
        kw = {"kind": "Internal"}
        if shared:
            kw["addr_space"] = "Shared"
        return nc.dram_tensor(name, list(shape), dt, **kw).ap()

    x_half = din("x_half", (HS, D))
    w_in = din("w_in", (D, 2 * DH))
    conv_w = din("conv_w", (128, NG, 4))
    conv_b = din("conv_b", (128, NG))
    w_x = din("w_x", (D, 96))
    w_dt = din("w_dt", (DTR, DH))
    b_dt = din("b_dt", (128, NG))
    a_log = din("a_log", (128, NG, NST))
    d_vec = din("d_vec", (128, NG))
    w_out = din("w_out", (D, D))
    anorm_w = din("anorm_w", (128, NG))
    fnorm_w = din("fnorm_w", (128, NG))
    wqT = din("wqT", (D, 512))
    wkT = din("wkT", (D, 128))
    wvT = din("wvT", (D, 128))
    woT = din("woT", (D, D))
    gate_wT = din("gate_wT", (D, 8))
    expert_wT = din("expert_wT", (8, D, D))
    l1_wq = din("l1_wq", (D, 512))
    l1_wk = din("l1_wk", (D, 512))
    l1_wv = din("l1_wv", (D, 512))
    l1_wo = din("l1_wo", (D, D))
    l1_wg = din("l1_wg", (FFI, D))
    l1_wu = din("l1_wu", (FFI, D))
    l1_wd = din("l1_wd", (NG, 128, FFI))
    ident_in = din("ident_in", (128, 128))
    ones_in = din("ones_in", (128, 128))
    tri_in = din("tri_in", (128, 128))
    hot8_in = din("hot8_in", (8, 1024))
    sel_in = din("sel_in", (64, 256))
    hot16_in = din("hot16_in", (16, 2048))

    y_out = nc.dram_tensor("y_out", [HS, D], F32, kind="ExternalOutput").ap()

    dbg = {}
    def dbg_out(name, shape, dt=F32):
        if name in debug:
            dbg[name] = nc.dram_tensor("dbg_" + name, list(shape), dt,
                                       kind="ExternalOutput").ap()
        return dbg.get(name)

    cc_xn_in = dint("cc_xn_in", (D, HS), BF16)
    cc_xn_out = dint("cc_xn_out", (8 * D, HS), BF16, shared=True)
    cc_rs_in = dint("cc_rs_in", (1, HS))
    cc_rs_out = dint("cc_rs_out", (8, HS), shared=True)
    u_spill = dint("u_spill", (128, NG, S), BF16)
    res_spill = dint("res_spill", (128, NG, S), BF16)
    cc_dbc_in = dint("cc_dbc_in", (NBLK, 96, TB))
    cc_dbc_out = dint("cc_dbc_out", (NBLK, 96, TB))
    cc_s_in = dint("cc_s_in", (128, NG, S), BF16)
    cc_s_out = dint("cc_s_out", (8 * 128, NG, S), BF16, shared=True)
    cc_ao_in = dint("cc_ao_in", (128, 4, S), BF16)
    cc_ao_out = dint("cc_ao_out", (8 * 128, 4, S), BF16, shared=True)
    cc_xn1_in = dint("cc_xn1_in", (D, HS), BF16)
    cc_xn1_out = dint("cc_xn1_out", (8 * D, HS), BF16, shared=True)
    xn1f_spill = dint("xn1f_spill", (128, 16, D))
    x1_spill = dint("x1_spill", (128, 16, D))
    cc_ao1_in = dint("cc_ao1_in", (128, 4, S), BF16)
    cc_ao1_out = dint("cc_ao1_out", (8 * 128, 4, S), BF16, shared=True)
    ao_my = dint("ao_my", (128, 4, HS), BF16)
    ao_pt = dint("ao_pt", (128, 4, HS), BF16)
    s_my = dint("s_my", (128, NG, HS), BF16)
    s_pt = dint("s_pt", (128, NG, HS), BF16)
    xn1_loc = dint("xn1_loc", (D, S), BF16)
    ao1_my = dint("ao1_my", (128, 4, HS), BF16)
    ao1_pt = dint("ao1_pt", (128, 4, HS), BF16)
    wgq_spill = dint("wgq_spill", (128, NG, FFI), BF16)
    wuq_spill = dint("wuq_spill", (128, NG, FFI), BF16)
    wdq_spill = dint("wdq_spill", (128, NFI, D), BF16)

    def sb(name, shape, dt=F32):
        return es.enter_context(nc.sbuf_tensor(name, list(shape), dt))

    ident = sb("ident", (128, 128), BF16)
    identf = sb("identf", (128, 128), F32)
    onesb = sb("onesb", (128, 128), BF16)
    onesf = sb("onesf", (128, 128), F32)
    trib = sb("trib", (128, 128), BF16)
    hot8f = sb("hot8f", (8, 1024))
    hot8b = sb("hot8b", (8, 1024), BF16)
    hot16b = sb("hot16b", (16, 2048), BF16)
    selb = sb("selb", (64, 256), BF16)
    conv_w_sb = sb("conv_w_sb", (128, NG, 4))
    conv_b_sb = sb("conv_b_sb", (128, NG))
    b_dt_sb = sb("b_dt_sb", (128, NG))
    a_sb = sb("a_sb", (128, NG, NST))
    d_sb = sb("d_sb", (128, NG))
    anw_sb = sb("anw_sb", (128, NG))
    fnw_sb = sb("fnw_sb", (128, NG))
    w_x_sb = sb("w_x_sb", (128, NG, 96), BF16)
    w_dt_sb = sb("w_dt_sb", (DTR, DH), BF16)
    gw_sb = sb("gw_sb", (128, NG, 8), BF16)
    rn_rstd = sb("rn_rstd", (NBLK, TB))
    state = sb("state", (128, NG, NST))

    class Sem:
        def __init__(self, name):
            self.h = es.enter_context(nc.semaphore(name))
            self.n = 0

        def inc(self, instr, k):
            instr.then_inc(self.h, k)
            self.n += k

        def bump(self, eng, k=1):
            eng.sem_inc(self.h, k)
            self.n += k

        def tag(self, instr, k=1):
            instr.then_inc(self.h, k)
            self.n += k

        def edge(self, eng, instr, k=1):
            self.tag(instr, k)
            self.wait(eng)

        def wait(self, eng, n=None):
            eng.wait_ge(self.h, self.n if n is None else n)

    sWg = Sem("sWg")
    sWs = Sem("sWs")
    sA = Sem("sA")
    sV = Sem("sV")
    sS = Sem("sS")
    sT = Sem("sT")
    sG = Sem("sG")
    sD = Sem("sD")
    sO = Sem("sO")

    pid_expr = [None]

    def pid(eng):
        return eng.partition_id()

    def rv_role(eng):
        return _rv(binary_op(pid(eng), make_scalar_value(1), A.bitwise_and))

    def rv_mul(val, k, mx=(1 << 30)):
        return _rv(binary_op(val, make_scalar_value(k), A.mult), mx)

    def rv_xor1(eng):
        return _rv(binary_op(pid(eng), make_scalar_value(1), A.bitwise_xor))

    ds = bass.ds

    with es, nc.Block() as block:
        _eng_sem = {"vector": sV, "scalar": sS, "tensor": sT}

        def phase_barrier():
            sems = [sWg, sWs, sA, sV, sS, sT, sG, sO, sD]
            for engname in ["vector", "scalar", "tensor", "gpsimd", "sync"]:
                def bar(eng):
                    for sm in sems:
                        if sm.n:
                            eng.wait_ge(sm.h, sm.n)
                getattr(block, engname)(bar)

        def _emit(engname, fn):
            sem = _eng_sem.get(engname)
            n = sem.n if sem else 0

            def wrapped(eng):
                if sem is not None and n > 0:
                    eng.wait_ge(sem.h, n)
                fn(eng)
            getattr(block, engname)(wrapped)

        def g_loads(g):
            for dst, src in [(ident, ident_in), (onesb, ones_in), (trib, tri_in),
                             (hot16b, hot16_in), (selb, sel_in), (hot8b, hot8_in),
                             (w_x_sb, w_x.rearrange("(g p) c -> p g c", p=128)), (w_dt_sb, w_dt),
                             (gw_sb, gate_wT.rearrange("(g p) c -> p g c", p=128))]:
                sWg.inc(g.dma_start(out=dst[:], in_=src), 16)
        block.gpsimd(g_loads)

        def sy_loads(sy):
            for dst, src in [(identf, ident_in), (onesf, ones_in),
                             (conv_w_sb, conv_w), (conv_b_sb, conv_b),
                             (b_dt_sb, b_dt), (d_sb, d_vec),
                             (anw_sb, anorm_w), (fnw_sb, fnorm_w),
                             (a_sb, a_log), (hot8f, hot8_in)]:
                sWs.inc(sy.dma_start(out=dst[:], in_=src), 16)
        block.sync(sy_loads)

        def s_a(s):
            sWs.wait(s)
            sS.tag(s.activation(out=a_sb[:], in_=a_sb[:], func=AF.Exp))
        _emit("scalar", s_a)

        def v_a(v):
            sS.wait(v)
            v.tensor_scalar(out=a_sb[:], in0=a_sb[:], scalar1=-1.0, scalar2=None, op0=A.mult)
            sV.tag(v.memset(state[:], 0.0))
        _emit("vector", v_a)

        with nc.sbuf_tensor("xh", [128, 16, D], F32) as xh, \
             nc.sbuf_tensor("xnb", [128, 16, D], BF16) as xnb, \
             nc.sbuf_tensor("sq", [128, 16, 8], F32) as sq, \
             nc.sbuf_tensor("xnf", [128, NG, HS], BF16) as xnf, \
             nc.sbuf_tensor("rsh", [128, 16], F32) as rsh, \
             nc.sbuf_tensor("rs_t", [16, 128], F32) as rs_t, \
             nc.psum_tensor("prs", [16, 128], F32) as prs, \
             nc.psum_tensor("ptr", [128, 1024], BF16) as ptr, \
             nc.psum_tensor("ptr2", [128, 1024], BF16) as ptr2:

            def p0_load(sy):
                sA.inc(sy.dma_start(out=xh[:], in_=x_half.rearrange("(n p) d -> p n d", p=128)), 16)
            block.sync(p0_load)

            def p0_sq(s):
                sA.wait(s)
                for i in range(16):
                    r = s.activation(out=xnb[:, i, :], in_=xh[:, i, :], func=AF.Square,
                                     accum_out=sq[:, i, 1:2])
                sS.tag(r)
            _emit("scalar", p0_sq)

            def p0_v1(v):
                sA.wait(v)
                for i in range(16):
                    r = v.reduce_sum(out=sq[:, i, 0:1], in_=xh[:, i, :], axis=AX.X)
                sV.edge(v, r)
                sS.wait(v)
                for i in range(16):
                    v.tensor_scalar(out=sq[:, i, 2:3], in0=sq[:, i, 0:1], scalar1=1.0 / D, scalar2=None, op0=A.mult)
                    r = v.tensor_scalar(out=sq[:, i, 3:4], in0=sq[:, i, 1:2], scalar1=1.0 / D, scalar2=None, op0=A.mult)
                sV.edge(v, r)
                for i in range(16):
                    r = v.tensor_tensor(out=sq[:, i, 4:5], in0=sq[:, i, 2:3], in1=sq[:, i, 2:3], op=A.mult)
                sV.edge(v, r)
                for i in range(16):
                    r = v.tensor_tensor(out=sq[:, i, 4:5], in0=sq[:, i, 3:4], in1=sq[:, i, 4:5], op=A.subtract)
                sV.edge(v, r)
                for i in range(16):
                    r = v.tensor_scalar(out=sq[:, i, 5:6], in0=sq[:, i, 4:5], scalar1=EPS_LN, scalar2=None, op0=A.add)
                sV.tag(r)
            _emit("vector", p0_v1)

            def p0_s2(s):
                sV.wait(s)
                for i in range(16):
                    r = s.activation(out=sq[:, i, 6:7], in_=sq[:, i, 5:6], func=AF.Sqrt)
                sS.tag(r)
            _emit("scalar", p0_s2)

            def p0_v2(v):
                sS.wait(v)
                for i in range(16):
                    r = v.reciprocal(out=sq[:, i, 7:8], in_=sq[:, i, 6:7])
                sV.edge(v, r)
                for i in range(16):
                    v.tensor_scalar(out=xnb[:, i, :], in0=xh[:, i, :],
                                    scalar1=sq[:, i, 2:3], scalar2=sq[:, i, 7:8],
                                    op0=A.subtract, op1=A.mult)
                for i in range(16):
                    r = v.reciprocal(out=sq[:, i, 0:1], in_=sq[:, i, 5:6])
                sV.edge(v, r)
                for i in range(16):
                    r = v.tensor_tensor(out=sq[:, i, 3:4], in0=sq[:, i, 4:5], in1=sq[:, i, 0:1], op=A.mult)
                sV.edge(v, r)
                for i in range(16):
                    r = v.tensor_scalar(out=sq[:, i, 3:4], in0=sq[:, i, 3:4], scalar1=EPS_RMS, scalar2=None, op0=A.add)
                sV.tag(r)
            _emit("vector", p0_v2)

            def p0_s3(s):
                sV.wait(s)
                for i in range(16):
                    r = s.activation(out=sq[:, i, 5:6], in_=sq[:, i, 3:4], func=AF.Sqrt)
                sS.tag(r)
            _emit("scalar", p0_s3)

            def p0_v3(v):
                sS.wait(v)
                for i in range(16):
                    r = v.reciprocal(out=rsh[:, i:i + 1], in_=sq[:, i, 5:6])
                sV.tag(r)
            _emit("vector", p0_v3)

            def p0_trs(t):
                sV.wait(t)
                sWs.wait(t)
                sWg.wait(t)
                sT.tag(t.transpose(prs[:], rsh[:], identf[:]))
            _emit("tensor", p0_trs)

            def p0_evrs(v):
                sT.wait(v)
                sV.tag(v.tensor_copy(out=rs_t[:], in_=prs[:]))
            _emit("vector", p0_evrs)

            ev_cnt = [sV.n]
            for i in range(16):
                pt = ptr if i % 2 == 0 else ptr2

                def p0_t(t, i=i, pt=pt):
                    if i == 0:
                        sV.wait(t)
                        sWg.wait(t)
                    if i >= 2:
                        t.wait_ge(sV.h, ev_cnt[0] + i - 1)
                    for jj in range(NG):
                        r = t.transpose(pt[:, jj * 128:(jj + 1) * 128],
                                        xnb[:, i, jj * 128:(jj + 1) * 128], ident[:])
                    sT.tag(r)
                _emit("tensor", p0_t)
                tcur = sT.n

                def p0_ev(v, i=i, pt=pt, tcur=tcur):
                    v.wait_ge(sT.h, tcur)
                    for jj in range(NG):
                        r = v.tensor_copy(out=xnf[:, jj, i * 128:(i + 1) * 128],
                                          in_=pt[:, jj * 128:(jj + 1) * 128])
                    sV.tag(r)
                _emit("vector", p0_ev)

            def p0_st(sy):
                sV.wait(sy)
                sO.inc(sy.dma_start(out=cc_xn_in.rearrange("(g p) t -> p g t", p=128), in_=xnf[:]), 16)
                sO.inc(sy.dma_start(out=cc_rs_in[0:1, :].rearrange("o (n p) -> (o n) p", p=128), in_=rs_t[:]), 16)
            block.sync(p0_st)

            def p0_cc(g):
                sO.wait(g)
                sG.inc(g.collective_compute("AllGather", A.bypass, replica_groups=GROUPS_ALL,
                                            ins=[cc_xn_in[:]], outs=[cc_xn_out[:]]), 1)
                sG.inc(g.collective_compute("AllGather", A.bypass, replica_groups=GROUPS_ALL,
                                            ins=[cc_rs_in[:]], outs=[cc_rs_out[:]]), 1)
            block.gpsimd(p0_cc)

        xn_cm = nc.sbuf_tensor("xn", [128, NG, S], BF16)
        xn = xn_cm.__enter__()

        def p0_ld2(sy):
            sG.wait(sy)
            base = rv_mul(_rv(binary_op(pid(sy), make_scalar_value(~1 & 0xff), A.bitwise_and), 6), D, 6 * D)
            src0 = cc_xn_out[ds(base, D), :].rearrange("(g p) t -> p g t", p=128)
            sA.inc(sy.dma_start(out=xn[:, :, 0:HS], in_=src0), 16)
            base1 = rv_mul(_rv(binary_op(pid(sy), make_scalar_value(1), A.bitwise_or), 7), D, 7 * D)
            src1 = cc_xn_out[ds(base1, D), :].rearrange("(g p) t -> p g t", p=128)
            sA.inc(sy.dma_start(out=xn[:, :, HS:S], in_=src1), 16)
            b0 = _rv(binary_op(pid(sy), make_scalar_value(~1 & 0xff), A.bitwise_and), 6)
            sA.inc(sy.dma_start(out=rn_rstd[0:4, :], in_=cc_rs_out[ds(b0, 1), :].rearrange("o (n t) -> (o n) t", t=TB)), 16)
            b1 = _rv(binary_op(pid(sy), make_scalar_value(1), A.bitwise_or), 7)
            sA.inc(sy.dma_start(out=rn_rstd[4:8, :], in_=cc_rs_out[ds(b1, 1), :].rearrange("o (n t) -> (o n) t", t=TB)), 16)
        block.sync(p0_ld2)


        if dbg_out("xn", (128, NG, S), BF16):
            def dxn(sy):
                sA.wait(sy)
                sO.inc(sy.dma_start(out=dbg["xn"][:], in_=xn[:]), 16)
            block.sync(dxn)

        phase_barrier()
        with nc.sbuf_tensor("w_in_sb", [128, NG, 2 * DH], BF16) as w_in_sb, \
             nc.sbuf_tensor("xi_ext", [128, NG, TB + 3], F32) as xi_ext, \
             nc.sbuf_tensor("u_blk", [128, NG, TB], BF16) as u_blk, \
             nc.sbuf_tensor("res_blk", [128, NG, TB], BF16) as res_blk, \
             nc.sbuf_tensor("cacc", [128, NG, TB], F32) as cacc, \
             nc.sbuf_tensor("dbc_ev", [96, TB], F32) as dbc_ev, \
             nc.psum_tensor("ps_mm", [128, TB], F32) as ps_mm, \
             nc.psum_tensor("ps_mm2", [128, TB], F32) as ps_mm2, \
             nc.psum_tensor("ps_dbc", [96, TB], F32) as ps_dbc:

            def p1_wload(g):
                sWg.inc(g.dma_start(out=w_in_sb[:], in_=w_in.rearrange("(g p) c -> p g c", p=128)), 16)
            block.gpsimd(p1_wload)

            def p1_init(v):
                sV.tag(v.memset(xi_ext[:, :, 0:3], 0.0))
            _emit("vector", p1_init)

            p1_evs = []
            for blk in range(NBLK):
                t0 = blk * TB

                for grp in range(16):
                    is_res = grp >= NG
                    oc = grp % NG
                    gidx = blk * 16 + grp
                    pm = ps_mm if grp % 2 == 0 else ps_mm2

                    def p1_t(t, blk=blk, t0=t0, grp=grp, oc=oc, pm=pm, is_res=is_res, gidx=gidx):
                        if gidx == 0:
                            sWg.wait(t)
                            sWs.wait(t)
                            sA.wait(t)
                            sV.wait(t)
                        if gidx >= 2:
                            t.wait_ge(sV.h, p1_evs[gidx - 2])
                        col0 = (DH if is_res else 0) + oc * 128
                        for g in range(NG):
                            r = t.matmul(pm[:], lhsT=w_in_sb[:, g, col0:col0 + 128],
                                         rhs=xn[:, g, t0:t0 + TB],
                                         start=(g == 0), stop=(g == NG - 1))
                        sT.tag(r)
                    _emit("tensor", p1_t)
                    tcur = sT.n

                    def p1_ev(v, grp=grp, oc=oc, pm=pm, is_res=is_res, tcur=tcur,
                              so_n=sO.n, ss_n=sS.n):
                        v.wait_ge(sT.h, tcur)
                        if grp == 0:
                            v.wait_ge(sO.h, so_n)
                            v.wait_ge(sS.h, ss_n)
                        if is_res:
                            r = v.tensor_copy(out=res_blk[:, oc, :], in_=pm[:])
                        else:
                            r = v.tensor_copy(out=xi_ext[:, oc, 3:3 + TB], in_=pm[:])
                        sV.tag(r)
                    _emit("vector", p1_ev)
                    p1_evs.append(sV.n)

                def p1_conv(v, blk=blk, ss_n=sS.n):
                    v.wait_ge(sS.h, ss_n)
                    for oc in range(NG):
                        r = v.tensor_scalar(out=cacc[:, oc, :], in0=xi_ext[:, oc, 0:TB],
                                            scalar1=conv_w_sb[:, oc, 0:1], scalar2=None, op0=A.mult)
                    sV.edge(v, r)
                    for kk in range(1, 4):
                        for oc in range(NG):
                            r = v.scalar_tensor_tensor(out=cacc[:, oc, :], in0=xi_ext[:, oc, kk:kk + TB],
                                                       scalar=conv_w_sb[:, oc, kk:kk + 1],
                                                       in1=cacc[:, oc, :], op0=A.mult, op1=A.add)
                        sV.edge(v, r)
                    for oc in range(NG):
                        r = v.tensor_copy(out=xi_ext[:, oc, 0:3], in_=xi_ext[:, oc, TB:TB + 3])
                    sV.tag(r)
                _emit("vector", p1_conv)

                def p1_s(s, blk=blk, sv_n=sV.n, st_n=sT.n, so_n=sO.n):
                    s.wait_ge(sV.h, sv_n)
                    s.wait_ge(sT.h, st_n)
                    s.wait_ge(sO.h, so_n)
                    for oc in range(NG):
                        r = s.activation(out=u_blk[:, oc, :], in_=cacc[:, oc, :], func=AF.Silu,
                                         bias=conv_b_sb[:, oc:oc + 1], scale=1.0)
                    sS.tag(r)
                _emit("scalar", p1_s)

                def p1_t2(t, blk=blk, t0=t0):
                    sS.wait(t)
                    for g in range(NG):
                        r = t.matmul(ps_dbc[:], lhsT=w_x_sb[:, g, :], rhs=u_blk[:, g, :],
                                     start=(g == 0), stop=(g == NG - 1))
                    sT.tag(r)
                _emit("tensor", p1_t2)

                def p1_ev3(v, blk=blk, so_n=sO.n):
                    sT.wait(v)
                    v.wait_ge(sO.h, so_n)
                    sV.tag(v.tensor_copy(out=dbc_ev[:], in_=ps_dbc[:]))
                _emit("vector", p1_ev3)

                def p1_st(sy, blk=blk, t0=t0):
                    sV.wait(sy)
                    sS.wait(sy)
                    sO.inc(sy.dma_start(out=u_spill[:, :, t0:t0 + TB], in_=u_blk[:]), 16)
                    sO.inc(sy.dma_start(out=res_spill[:, :, t0:t0 + TB], in_=res_blk[:]), 16)
                    sO.inc(sy.dma_start(out=cc_dbc_in[blk], in_=dbc_ev[:]), 16)
                block.sync(p1_st)

            def p1_cc(g):
                sO.wait(g)
                sG.inc(g.collective_compute("AllReduce", A.add, replica_groups=GROUPS_PAIR,
                                            ins=[cc_dbc_in[:]], outs=[cc_dbc_out[:]]), 1)
            block.gpsimd(p1_cc)

            if dbg_out("u", (128, NG, S), BF16):
                def du_(sy):
                    sO.wait(sy)
                    sO.inc(sy.dma_start(out=dbg["u"][:], in_=u_spill[:]), 16)
                block.sync(du_)
            if dbg_out("res", (128, NG, S), BF16):
                def dr_(sy):
                    sO.wait(sy)
                    sO.inc(sy.dma_start(out=dbg["res"][:], in_=res_spill[:]), 16)
                block.sync(dr_)
            if dbg_out("dbc", (NBLK, 96, TB)):
                def dd_(sy):
                    sG.wait(sy)
                    sO.inc(sy.dma_start(out=dbg["dbc"][:], in_=cc_dbc_out[:]), 16)
                block.sync(dd_)

        phase_barrier()
        kscale_l0 = 0.125
        es2 = ExitStack()
        def sb2(name, shape, dt=F32):
            return es2.enter_context(nc.sbuf_tensor(name, list(shape), dt))
        wq_sb = sb2("wq_sb", (128, NG, 512), BF16)
        wk_sb = sb2("wk_sb", (128, NG, 128), BF16)
        wv_sb = sb2("wv_sb", (128, NG, 128), BF16)
        k_fm = sb2("k_fm", (128, S), BF16)
        v_tok = sb2("v_tok", (128, 32, 130), BF16)
        rnb = sb2("rnb", (128, NG, TB), BF16)
        qb = sb2("qb", (128, 4, TB), BF16)
        atb = sb2("atb", (128, 2, 2, TB), BF16)
        recd = sb2("recd", (128, 2, TB))
        ib2 = sb2("ib2", (64, 2, TB))
        nb = sb2("nb", (64, 2, TB), BF16)
        aob = sb2("aob", (128, 4, TB), BF16)
        psA = es2.enter_context(nc.psum_tensor("psA", [128, TB], F32))
        psB = es2.enter_context(nc.psum_tensor("psB", [128, TB], F32))
        psC = es2.enter_context(nc.psum_tensor("psC", [128, TB], F32))
        psD = es2.enter_context(nc.psum_tensor("psD", [128, TB], F32))
        av0 = es2.enter_context(nc.psum_tensor("av0", [128, TB], F32))
        av1 = es2.enter_context(nc.psum_tensor("av1", [128, TB], F32))
        psM = es2.enter_context(nc.psum_tensor("psM", [128, TB], F32))
        with es2:

            def p2_init(g):
                sWg.inc(g.dma_start(out=wq_sb[:], in_=wqT.rearrange("(g p) c -> p g c", p=128)), 16)
                sWg.inc(g.dma_start(out=wk_sb[:], in_=wkT.rearrange("(g p) c -> p g c", p=128)), 16)
                sWg.inc(g.dma_start(out=wv_sb[:], in_=wvT.rearrange("(g p) c -> p g c", p=128)), 16)
                g.memset(v_tok[:, :, 64:65], 1.0)
                r = g.memset(v_tok[:, :, 129:130], 1.0)
                sG.tag(r)
            block.gpsimd(p2_init)

            for i in range(NBLK):
                t0 = i * TB

                def p2_rn_t(t, i=i, sv_n=sV.n):
                    t.wait_ge(sV.h, sv_n)
                    r = t.matmul(psM[:], lhsT=hot8f[:, i * 128:(i + 1) * 128], rhs=rn_rstd[:],
                                 start=True, stop=True)
                    sT.tag(r)
                _emit("tensor", p2_rn_t)

                def p2_rn_v(v, i=i, st_n=sT.n):
                    v.wait_ge(sT.h, st_n)
                    for g in range(NG):
                        r = v.scalar_tensor_tensor(out=rnb[:, g, :], in0=xn[:, g, t0:t0 + TB],
                                                   scalar=anw_sb[:, g:g + 1], in1=psM[:],
                                                   op0=A.mult, op1=A.mult)
                    sV.tag(r)
                _emit("vector", p2_rn_v)
                rn_done = sV.n

                for hp in range(4):
                    pq = psA if hp % 2 == 0 else psB
                    def p2_q_t(t, hp=hp, pq=pq, rn_done=rn_done, ss_n=sS.n, swg=sWg.n):
                        t.wait_ge(sWg.h, swg)
                        t.wait_ge(sV.h, rn_done)
                        t.wait_ge(sS.h, ss_n)
                        for g in range(NG):
                            r = t.matmul(pq[:], lhsT=wq_sb[:, g, hp * 128:(hp + 1) * 128],
                                         rhs=rnb[:, g, :], start=(g == 0), stop=(g == NG - 1))
                        sT.tag(r)
                    _emit("tensor", p2_q_t)

                    def p2_q_s(s, hp=hp, pq=pq, st_n=sT.n):
                        s.wait_ge(sT.h, st_n)
                        sS.tag(s.activation(out=qb[:, hp, :], in_=pq[:], func=AF.Copy))
                    _emit("scalar", p2_q_s)

                def p2_kv_t(t, i=i, rn_done=rn_done, ss_n=sS.n):
                    t.wait_ge(sS.h, ss_n)
                    for g in range(NG):
                        r = t.matmul(psC[:], lhsT=wk_sb[:, g, :], rhs=rnb[:, g, :],
                                     start=(g == 0), stop=(g == NG - 1))
                    sT.tag(r)
                    for sub in range(4):
                        for g in range(NG):
                            r = t.matmul(psD[:, sub * 128:(sub + 1) * 128],
                                         lhsT=rnb[:, g, sub * 128:(sub + 1) * 128],
                                         rhs=wv_sb[:, g, :], start=(g == 0), stop=(g == NG - 1))
                    sT.tag(r)
                _emit("tensor", p2_kv_t)

                def p2_kv_s(s, i=i, t0=t0, st_n=sT.n):
                    s.wait_ge(sT.h, st_n)
                    s.activation(out=k_fm[:, t0:t0 + TB], in_=psC[:], func=AF.Copy, scale=kscale_l0)
                    for sub in range(4):
                        s.activation(out=v_tok[:, i * 4 + sub, 0:64],
                                     in_=psD[:, sub * 128 + 0:sub * 128 + 64], func=AF.Copy)
                        r = s.activation(out=v_tok[:, i * 4 + sub, 65:129],
                                         in_=psD[:, sub * 128 + 64:sub * 128 + 128], func=AF.Copy)
                    sS.tag(r)
                _emit("scalar", p2_kv_s)
                kv_done = sS.n

                nkt = 4 * i + 4
                for ht in range(4):
                    exp_done = {}
                    msk_done = {}
                    for kt in range(nkt):
                        par = kt % 2
                        scA = psA if par == 0 else psC
                        scB = psB if par == 0 else psD
                        diag = (kt >= 4 * i)

                        def p2_sc(t, i=i, ht=ht, kt=kt, par=par, scA=scA, scB=scB,
                                  kv_done=kv_done, ss_n=sS.n, exp_done=exp_done):
                            t.wait_ge(sS.h, kv_done)
                            if kt >= 2:
                                t.wait_ge(sS.h, exp_done[kt - 2])
                            elif kt < 2:
                                t.wait_ge(sS.h, ss_n)
                            r = t.matmul(scA[:], lhsT=k_fm[0:64, kt * 128:(kt + 1) * 128],
                                         rhs=qb[0:64, ht, :], start=True, stop=True)
                            r = t.matmul(scB[:], lhsT=k_fm[64:128, kt * 128:(kt + 1) * 128],
                                         rhs=qb[64:128, ht, :], start=True, stop=True)
                            sT.tag(r)
                        _emit("tensor", p2_sc)
                        sc_n = sT.n

                        def p2_exp(s, kt=kt, par=par, scA=scA, scB=scB, sc_n=sc_n,
                                   tav_need=None, exp_done=exp_done):
                            s.wait_ge(sT.h, sc_n)
                            s.activation(out=atb[:, 0, par, :], in_=scA[:], func=AF.Exp)
                            r = s.activation(out=atb[:, 1, par, :], in_=scB[:], func=AF.Exp)
                            sS.tag(r)
                            exp_done[kt] = sS.n
                        _emit("scalar", p2_exp)

                        if diag:
                            j = kt - 4 * i

                            def p2_msk(v, kt=kt, par=par, j=j, e_n=sS.n, msk_done=msk_done):
                                v.wait_ge(sS.h, e_n)
                                for hh in range(2):
                                    if j > 0:
                                        v.memset(atb[:, hh, par, 0:j * 128], 0.0)
                                    r = v.tensor_tensor(out=atb[:, hh, par, j * 128:(j + 1) * 128],
                                                        in0=atb[:, hh, par, j * 128:(j + 1) * 128],
                                                        in1=trib[:], op=A.mult)
                                sV.tag(r)
                                msk_done[kt] = sV.n
                            _emit("vector", p2_msk)

                        def p2_av(t, i=i, ht=ht, kt=kt, par=par, diag=diag,
                                  exp_done=exp_done, msk_done=msk_done, sg_n=sG.n):
                            if kt == 0:
                                t.wait_ge(sG.h, sg_n)
                            t.wait_ge(sS.h, exp_done[kt])
                            if diag:
                                t.wait_ge(sV.h, msk_done[kt])
                            r = t.matmul(av0[0:65, :], lhsT=v_tok[:, kt, 0:65],
                                         rhs=atb[:, 0, par, :], start=(kt == 0), stop=(kt == nkt - 1))
                            r = t.matmul(av1[0:65, :], lhsT=v_tok[:, kt, 65:130],
                                         rhs=atb[:, 1, par, :], start=(kt == 0), stop=(kt == nkt - 1))
                            sT.tag(r)
                        _emit("tensor", p2_av)

                    def p2_ep_v(v, ht=ht, st_n=sT.n, so_n=sO.n):
                        v.wait_ge(sT.h, st_n)
                        if ht == 0:
                            v.wait_ge(sO.h, so_n)
                        v.reciprocal(out=recd[64:65, 0, :], in_=av0[64:65, :])
                        r = v.reciprocal(out=recd[64:65, 1, :], in_=av1[64:65, :])
                        sV.tag(r)
                    _emit("vector", p2_ep_v)

                    def p2_ep_t(t, ht=ht, sv_n=sV.n):
                        t.wait_ge(sV.h, sv_n)
                        t.matmul(psM[0:64, :], lhsT=onesf[64:65, 0:64], rhs=recd[64:65, 0, :],
                                 start=True, stop=True)
                        r = t.matmul(psC[0:64, :], lhsT=onesf[64:65, 0:64], rhs=recd[64:65, 1, :],
                                     start=True, stop=True)
                        sT.tag(r)
                    _emit("tensor", p2_ep_t)

                    def p2_ep_sx(s, ht=ht, st_n=sT.n):
                        s.wait_ge(sT.h, st_n)
                        s.activation(out=ib2[:, 0, :], in_=psM[0:64, :], func=AF.Copy)
                        r = s.activation(out=ib2[:, 1, :], in_=psC[0:64, :], func=AF.Copy)
                        sS.tag(r)
                    _emit("scalar", p2_ep_sx)

                    def p2_ep_v2(v, ht=ht, ss_n=sS.n):
                        v.wait_ge(sS.h, ss_n)
                        v.tensor_tensor(out=nb[:, 0, :], in0=av0[0:64, :], in1=ib2[:, 0, :], op=A.mult)
                        r = v.tensor_tensor(out=nb[:, 1, :], in0=av1[0:64, :], in1=ib2[:, 1, :], op=A.mult)
                        sV.tag(r)
                    _emit("vector", p2_ep_v2)

                    def p2_ep_t2(t, ht=ht, sv_n=sV.n):
                        t.wait_ge(sV.h, sv_n)
                        t.matmul(psB[:], lhsT=selb[:, 0:128], rhs=nb[:, 0, :], start=True, stop=False)
                        r = t.matmul(psB[:], lhsT=selb[:, 128:256], rhs=nb[:, 1, :], start=False, stop=True)
                        sT.tag(r)
                    _emit("tensor", p2_ep_t2)

                    def p2_ep_s2(s, ht=ht, st_n=sT.n):
                        s.wait_ge(sT.h, st_n)
                        sS.tag(s.activation(out=aob[:, ht, :], in_=psB[:], func=AF.Copy))
                    _emit("scalar", p2_ep_s2)

                def p2_st(sy, i=i, t0=t0, ss_n=sS.n):
                    sy.wait_ge(sS.h, ss_n)
                    sO.inc(sy.dma_start(out=cc_ao_in[:, :, t0:t0 + TB], in_=aob[:]), 16)
                block.sync(p2_st)

            def p2_cc(g):
                sO.wait(g)
                sG.inc(g.collective_compute("AllGather", A.bypass, replica_groups=GROUPS_ALL,
                                            ins=[cc_ao_in[:]], outs=[cc_ao_out[:]]), 1)
            block.gpsimd(p2_cc)

            def p2_restage(sc):
                sG.wait(sc)
                p_ = pid(sc)
                role_ = _rv(binary_op(p_, make_scalar_value(1), A.bitwise_and), 1)
                hoff = _rv(binary_op(role_, make_scalar_value(HS), A.mult), HS)
                mybase = _rv(binary_op(p_, make_scalar_value(128), A.mult), 7 * 128)
                pbase = _rv(binary_op(_rv(binary_op(p_, make_scalar_value(1), A.bitwise_xor), 7),
                                      make_scalar_value(128), A.mult), 7 * 128)
                sD.inc(sc.dma_start(out=ao_my[:], in_=cc_ao_out[ds(mybase, 128), :, ds(hoff, HS)]), 16)
                sD.inc(sc.dma_start(out=ao_pt[:], in_=cc_ao_out[ds(pbase, 128), :, ds(hoff, HS)]), 16)
            block.scalar(p2_restage)

        xn_cm.__exit__(None, None, None)
        phase_barrier()

        es3 = ExitStack()
        def sb3(name, shape, dt=F32):
            return es3.enter_context(nc.sbuf_tensor(name, list(shape), dt))
        scr = sb3("scr", (128, 16, TB))
        delta = sb3("delta", (128, NG, TB))
        du = sb3("du", (128, NG, TB), BF16)
        bcB = sb3("bcB", (128, NST, TB), BF16)
        bcC = sb3("bcC", (128, NST, TB), BF16)
        dbu = sb3("dbu", (128, NST, TB), BF16)
        hbuf = sb3("hbuf", (128, NST, TB), BF16)
        u3 = sb3("u3", (128, NG, TB), BF16)
        res3 = sb3("res3", (128, NG, TB), BF16)
        ygc = sb3("ygc", (128, NG, TB), BF16)
        spb = sb3("spb", (128, NG, TB), BF16)
        dbc3 = sb3("dbc3", (64, TB), BF16)
        w_out_sb = sb3("w_out_sb", (128, NG, D), BF16)
        dbc3B = sb3("dbc3B", (16, TB), BF16)
        dbc3C = sb3("dbc3C", (16, TB), BF16)
        pw0 = es3.enter_context(nc.psum_tensor("pw0", [128, TB], F32))
        pw1 = es3.enter_context(nc.psum_tensor("pw1", [128, TB], F32))
        pby = es3.enter_context(nc.psum_tensor("pby", [128, TB], F32))
        pbs = es3.enter_context(nc.psum_tensor("pbs", [128, TB], F32))
        pbB = es3.enter_context(nc.psum_tensor("pbB", [128, TB], F32))
        pbC = es3.enter_context(nc.psum_tensor("pbC", [128, TB], F32))
        with es3:

            for blk in range(NBLK):
                t0 = blk * TB

                def p3_ld(sy, blk=blk, t0=t0, sv_n=sV.n, st_n=sT.n, ss_n=sS.n):
                    sy.wait_ge(sV.h, sv_n)
                    sy.wait_ge(sS.h, ss_n)
                    sA.inc(sy.dma_start(out=u3[:], in_=u_spill[:, :, t0:t0 + TB]), 16)
                    sA.inc(sy.dma_start(out=res3[:], in_=res_spill[:, :, t0:t0 + TB]), 16)
                block.sync(p3_ld)

                def p3_ldg(g, blk=blk, sv_n=sV.n, st_n=sT.n):
                    if blk == 0:
                        sWg.inc(g.dma_start(out=w_out_sb[:], in_=w_out.rearrange("(g p) c -> p g c", p=128)), 16)
                    sG.wait(g)
                    g.wait_ge(sT.h, st_n)
                    sWg.inc(g.dma_start(out=dbc3[:], in_=cc_dbc_out[blk, 0:64, :]), 16)
                    sWg.inc(g.dma_start(out=dbc3B[:], in_=cc_dbc_out[blk, 64:80, :]), 16)
                    sWg.inc(g.dma_start(out=dbc3C[:], in_=cc_dbc_out[blk, 80:96, :]), 16)
                block.gpsimd(p3_ldg)

                for n in range(NST):
                    def p3_bc_t(t, n=n, swg=sWg.n, sv_n=sV.n):
                        t.wait_ge(sWg.h, swg)
                        if n >= 1:
                            t.wait_ge(sV.h, sv_n)
                        t.matmul(pbB[:], lhsT=hot16b[:, n * 128:(n + 1) * 128], rhs=dbc3B[:],
                                 start=True, stop=True)
                        r = t.matmul(pbC[:], lhsT=hot16b[:, n * 128:(n + 1) * 128], rhs=dbc3C[:],
                                     start=True, stop=True)
                        sT.tag(r)
                    _emit("tensor", p3_bc_t)

                    def p3_bc_v(v, n=n, st_n=sT.n):
                        v.wait_ge(sT.h, st_n)
                        v.tensor_copy(out=bcB[:, n, :], in_=pbB[:])
                        r = v.tensor_copy(out=bcC[:, n, :], in_=pbC[:])
                        sV.tag(r)
                    _emit("vector", p3_bc_v)

                for g in range(NG):
                    pw = pw0 if g % 2 == 0 else pw1
                    def p3_d_t(t, g=g, pw=pw, swg=sWg.n, sv_n=sV.n):
                        t.wait_ge(sWg.h, swg)
                        if g >= 2:
                            t.wait_ge(sV.h, sv_n)
                        r = t.matmul(pw[:], lhsT=w_dt_sb[:, g * 128:(g + 1) * 128],
                                     rhs=dbc3[0:64, :], start=True, stop=True)
                        sT.tag(r)
                    _emit("tensor", p3_d_t)

                    def p3_d_v(v, g=g, pw=pw, st_n=sT.n):
                        v.wait_ge(sT.h, st_n)
                        r = v.tensor_scalar(out=scr[:, g, :], in0=pw[:],
                                            scalar1=b_dt_sb[:, g:g + 1], scalar2=None, op0=A.add)
                        sV.tag(r)
                    _emit("vector", p3_d_v)

                def p3_relu(v):
                    for g in range(NG):
                        r = v.tensor_scalar(out=delta[:, g, :], in0=scr[:, g, :],
                                            scalar1=0.0, scalar2=None, op0=A.max)
                    sV.tag(r)
                _emit("vector", p3_relu)

                def p3_sp_s(s, sv_n=sV.n):
                    s.wait_ge(sV.h, sv_n)
                    for g in range(NG):
                        r = s.activation(out=scr[:, 8 + g, :], in_=scr[:, g, :], func=AF.Abs)
                    sS.edge(s, r)
                    for g in range(NG):
                        r = s.activation(out=scr[:, g, :], in_=scr[:, 8 + g, :], func=AF.Exp, scale=-1.0)
                    sS.edge(s, r)
                    for g in range(NG):
                        r = s.activation(out=scr[:, 8 + g, :], in_=scr[:, g, :], func=AF.Ln, bias=1.0)
                    sS.tag(r)
                _emit("scalar", p3_sp_s)

                def p3_dfin(v, ss_n=sS.n, sa_n=sA.n):
                    v.wait_ge(sS.h, ss_n)
                    v.wait_ge(sA.h, sa_n)
                    for g in range(NG):
                        r = v.tensor_tensor(out=delta[:, g, :], in0=delta[:, g, :],
                                            in1=scr[:, 8 + g, :], op=A.add)
                    sV.edge(v, r)
                    for g in range(NG):
                        r = v.tensor_tensor(out=du[:, g, :], in0=delta[:, g, :],
                                            in1=u3[:, g, :], op=A.mult)
                    sV.tag(r)
                _emit("vector", p3_dfin)

                for g in range(NG):
                    def p3_dA(s, g=g, sv_n=sV.n):
                        s.wait_ge(sV.h, sv_n)
                        for n in range(NST):
                            r = s.activation(out=scr[:, n, :], in_=delta[:, g, :], func=AF.Exp,
                                             scale=a_sb[:, g, n:n + 1])
                        sS.tag(r)
                    _emit("scalar", p3_dA)

                    def p3_dbu(v, g=g, ss_n=sS.n):
                        v.wait_ge(sS.h, ss_n)
                        for n in range(NST):
                            r = v.tensor_tensor(out=dbu[:, n, :], in0=du[:, g, :],
                                                in1=bcB[:, n, :], op=A.mult)
                        sV.edge(v, r)
                        for n in range(NST):
                            r = v.tensor_tensor_scan(out=hbuf[:, n, :], data0=scr[:, n, :],
                                                     data1=dbu[:, n, :],
                                                     initial=state[:, g, n:n + 1],
                                                     op0=A.mult, op1=A.add)
                        sV.edge(v, r)
                        for n in range(NST):
                            r = v.tensor_copy(out=state[:, g, n:n + 1], in_=hbuf[:, n, 511:512])
                        sV.edge(v, r)
                        for n in range(NST):
                            r = v.tensor_tensor(out=hbuf[:, n, :], in0=hbuf[:, n, :],
                                                in1=bcC[:, n, :], op=A.mult)
                        sV.tag(r)
                    _emit("vector", p3_dbu)

                    def p3_y_t(t, g=g, sv_n=sV.n):
                        t.wait_ge(sV.h, sv_n)
                        for n in range(NST):
                            r = t.matmul(pby[:], lhsT=ident[:], rhs=hbuf[:, n, :],
                                         start=(n == 0), stop=(n == NST - 1))
                        sT.tag(r)
                    _emit("tensor", p3_y_t)

                    def p3_y_v(v, g=g, st_n=sT.n, ss_n=sS.n):
                        v.wait_ge(sT.h, st_n)
                        r = v.scalar_tensor_tensor(out=scr[:, 0, :], in0=u3[:, g, :],
                                                   scalar=d_sb[:, g:g + 1], in1=pby[:],
                                                   op0=A.mult, op1=A.add)
                        sV.tag(r)
                    _emit("vector", p3_y_v)

                    def p3_y_s(s, g=g, sa_n=sA.n, sv_n=sV.n):
                        s.wait_ge(sA.h, sa_n)
                        s.wait_ge(sV.h, sv_n)
                        sS.tag(s.activation(out=scr[:, 1, :], in_=res3[:, g, :], func=AF.Silu))
                    _emit("scalar", p3_y_s)

                    def p3_ygc(v, g=g, ss_n=sS.n):
                        v.wait_ge(sS.h, ss_n)
                        r = v.tensor_tensor(out=ygc[:, g, :], in0=scr[:, 0, :],
                                            in1=scr[:, 1, :], op=A.mult)
                        sV.tag(r)
                    _emit("vector", p3_ygc)

                for oc in range(NG):
                    def p3_s_t(t, oc=oc, sv_n=sV.n, ss_n=sS.n):
                        t.wait_ge(sV.h, sv_n)
                        if oc >= 1:
                            t.wait_ge(sS.h, ss_n)
                        for g in range(NG):
                            r = t.matmul(pbs[:], lhsT=w_out_sb[:, g, oc * 128:(oc + 1) * 128],
                                         rhs=ygc[:, g, :], start=(g == 0), stop=(g == NG - 1))
                        sT.tag(r)
                    _emit("tensor", p3_s_t)

                    def p3_s_s(s, oc=oc, st_n=sT.n, so_n=sO.n):
                        s.wait_ge(sT.h, st_n)
                        if oc == 0:
                            s.wait_ge(sO.h, so_n)
                        sS.tag(s.activation(out=spb[:, oc, :], in_=pbs[:], func=AF.Copy))
                    _emit("scalar", p3_s_s)

                def p3_st(sy, blk=blk, t0=t0, ss_n=sS.n):
                    sy.wait_ge(sS.h, ss_n)
                    sO.inc(sy.dma_start(out=cc_s_in[:, :, t0:t0 + TB], in_=spb[:]), 16)
                block.sync(p3_st)

            def p3_cc(g):
                sO.wait(g)
                sG.inc(g.collective_compute("AllGather", A.bypass, replica_groups=GROUPS_ALL,
                                            ins=[cc_s_in[:]], outs=[cc_s_out[:]]), 1)
            block.gpsimd(p3_cc)

            def p3_restage(sc):
                sG.wait(sc)
                p_ = pid(sc)
                role_ = _rv(binary_op(p_, make_scalar_value(1), A.bitwise_and), 1)
                hoff = _rv(binary_op(role_, make_scalar_value(HS), A.mult), HS)
                pbase = _rv(binary_op(_rv(binary_op(p_, make_scalar_value(1), A.bitwise_xor), 7),
                                      make_scalar_value(128), A.mult), 7 * 128)
                sD.inc(sc.dma_start(out=s_my[:], in_=cc_s_in[:, :, ds(hoff, HS)]), 16)
                sD.inc(sc.dma_start(out=s_pt[:], in_=cc_s_out[ds(pbase, 128), :, ds(hoff, HS)]), 16)
            block.scalar(p3_restage)

        phase_barrier()
        dbg_out("x1", (128, 16, D))
        dbg_out("xn1", (128, 16, D))
        es4 = ExitStack()
        def sb4(name, shape, dt=F32):
            return es4.enter_context(nc.sbuf_tensor(name, list(shape), dt))
        wo_sb = sb4("wo_sb", (128, NG, D), BF16)
        xnob = sb4("xnob", (128, NG, TB), BF16)
        ao8a = sb4("ao8a", (128, 4, TB), BF16)
        ao8b = sb4("ao8b", (128, 4, TB), BF16)
        smine = sb4("smine", (128, NG, TB), BF16)
        spart = sb4("spart", (128, NG, TB), BF16)
        hb4 = sb4("hb4", (128, NG, TB))
        rn2 = sb4("rn2", (128, NG, TB), BF16)
        sc4 = sb4("sc4", (128, 2, TB))
        strow = sb4("strow", (1, TB))
        lg4 = sb4("lg4", (128, 4, 8))
        cmb = sb4("cmb", (128, 4, 8))
        cmbw = sb4("cmbw", (128, 4, 4))
        combr = sb4("combr", (8, TB), BF16)
        cbsb = sb4("cbsb", (128, TB), BF16)
        ew = sb4("ew", (128, 2, NG, D), BF16)
        macc = sb4("macc", (128, NG, TB))
        gs4 = sb4("gs4", (128, NG, TB), BF16)
        sg4 = sb4("sg4", (128, NG, TB), BF16)
        x0t = sb4("x0t", (128, 4, D))
        x1t = sb4("x1t", (128, D))
        xn1t = sb4("xn1t", (128, D), BF16)
        st4 = sb4("st4", (128, 8))
        ph0 = es4.enter_context(nc.psum_tensor("ph0", [128, TB], F32))
        ph1 = es4.enter_context(nc.psum_tensor("ph1", [128, TB], F32))
        pst4 = es4.enter_context(nc.psum_tensor("pst4", [128, TB], F32))
        ppk4 = es4.enter_context(nc.psum_tensor("ppk4", [128, 1024], BF16))
        plg4 = es4.enter_context(nc.psum_tensor("plg4", [128, 64], F32))
        pct4 = es4.enter_context(nc.psum_tensor("pct4", [16, 128], F32))
        with es4:
            def p4_w(g):
                sWg.inc(g.dma_start(out=wo_sb[:], in_=woT.rearrange("(g p) c -> p g c", p=128)), 16)
            block.gpsimd(p4_w)

            for ob in range(OBLK):
                ot = ob * TB

                def p4_ld(sy, ob=ob, ot=ot, sv_n=sV.n, ss_n=sS.n, st_n=sT.n, sd_n=sD.n, so_n=sO.n):
                    sy.wait_ge(sV.h, sv_n)
                    sy.wait_ge(sS.h, ss_n)
                    sy.wait_ge(sT.h, st_n)
                    sy.wait_ge(sD.h, sd_n)
                    sy.wait_ge(sO.h, so_n)
                    sA.inc(sy.dma_start(out=xnob[:], in_=cc_xn_in.rearrange(
                        "(g p) t -> p g t", p=128)[:, :, ot:ot + TB]), 16)
                    sA.inc(sy.dma_start(out=ao8a[:], in_=ao_my[:, :, ot:ot + TB]), 16)
                    sA.inc(sy.dma_start(out=ao8b[:], in_=ao_pt[:, :, ot:ot + TB]), 16)
                    sA.inc(sy.dma_start(out=smine[:], in_=s_my[:, :, ot:ot + TB]), 16)
                    sA.inc(sy.dma_start(out=spart[:], in_=s_pt[:, :, ot:ot + TB]), 16)
                    sA.inc(sy.dma_start(out=x0t[:], in_=x_half.rearrange(
                        "(n p) d -> p n d", p=128)[:, ob * 4:(ob + 1) * 4, :]), 16)
                block.sync(p4_ld)

                for g in range(NG):
                    ph = ph0 if g % 2 == 0 else ph1
                    def p4_h_t(t, g=g, ph=ph, sa_n=sA.n, swg=sWg.n, sv_n=sV.n):
                        t.wait_ge(sA.h, sa_n)
                        t.wait_ge(sWg.h, swg)
                        if g >= 2:
                            t.wait_ge(sV.h, sv_n)
                        for j in range(NG):
                            rhs_ = ao8a[:, j, :] if j < 4 else ao8b[:, j - 4, :]
                            r = t.matmul(ph[:], lhsT=wo_sb[:, j, g * 128:(g + 1) * 128],
                                         rhs=rhs_, start=(j == 0), stop=(j == NG - 1))
                        sT.tag(r)
                    _emit("tensor", p4_h_t)

                    def p4_h_v(v, g=g, ph=ph, st_n=sT.n):
                        v.wait_ge(sT.h, st_n)
                        r = v.tensor_tensor(out=hb4[:, g, :], in0=ph[:], in1=xnob[:, g, :], op=A.add)
                        sV.tag(r)
                    _emit("vector", p4_h_v)

                for g in range(NG):
                    def p4_sq_s(s, g=g, sv_n=sV.n, st_n=sT.n):
                        s.wait_ge(sV.h, sv_n)
                        if g >= 2:
                            s.wait_ge(sT.h, st_n)
                        sS.tag(s.activation(out=sc4[:, g % 2, :], in_=hb4[:, g, :], func=AF.Square))
                    _emit("scalar", p4_sq_s)

                    def p4_st_tg(t, g=g, ss_n=sS.n):
                        t.wait_ge(sS.h, ss_n)
                        r = t.matmul(pst4[0:1, :], lhsT=onesf[:, 0:1], rhs=sc4[:, g % 2, :],
                                     start=(g == 0), stop=(g == NG - 1))
                        sT.tag(r)
                    _emit("tensor", p4_st_tg)

                def p4_st_v(v, st_n=sT.n):
                    v.wait_ge(sT.h, st_n)
                    r = v.tensor_scalar(out=strow[0:1, :], in0=pst4[0:1, :],
                                        scalar1=1.0 / D, scalar2=EPS_RMS, op0=A.mult, op1=A.add)
                    sV.tag(r)
                _emit("vector", p4_st_v)

                def p4_st_s(s, sv_n=sV.n):
                    s.wait_ge(sV.h, sv_n)
                    sS.tag(s.activation(out=strow[0:1, :], in_=strow[0:1, :], func=AF.Sqrt))
                _emit("scalar", p4_st_s)

                def p4_st_v2(v, ss_n=sS.n):
                    v.wait_ge(sS.h, ss_n)
                    sV.tag(v.reciprocal(out=strow[0:1, :], in_=strow[0:1, :]))
                _emit("vector", p4_st_v2)

                def p4_r2_t(t, sv_n=sV.n):
                    t.wait_ge(sV.h, sv_n)
                    r = t.matmul(pst4[:], lhsT=onesf[0:1, :], rhs=strow[0:1, :],
                                 start=True, stop=True)
                    sT.tag(r)
                _emit("tensor", p4_r2_t)

                def p4_rn2_v(v, st_n=sT.n):
                    v.wait_ge(sT.h, st_n)
                    for g in range(NG):
                        r = v.scalar_tensor_tensor(out=rn2[:, g, :], in0=hb4[:, g, :],
                                                   scalar=fnw_sb[:, g:g + 1], in1=pst4[:],
                                                   op0=A.mult, op1=A.mult)
                    sV.tag(r)
                _emit("vector", p4_rn2_v)

                for sub in range(4):
                    def p4_lg_t(t, sub=sub, sv_n=sV.n):
                        t.wait_ge(sV.h, sv_n)
                        for g in range(NG):
                            r = t.matmul(plg4[:, sub * 8:(sub + 1) * 8],
                                         lhsT=rn2[:, g, sub * 128:(sub + 1) * 128],
                                         rhs=gw_sb[:, g, :], start=(g == 0), stop=(g == NG - 1))
                        sT.tag(r)
                    _emit("tensor", p4_lg_t)

                def p4_lg_v(v, st_n=sT.n):
                    v.wait_ge(sT.h, st_n)
                    for sub in range(4):
                        r = v.tensor_copy(out=lg4[:, sub, :], in_=plg4[:, sub * 8:(sub + 1) * 8])
                    sV.edge(v, r)
                    r = v.tensor_reduce(out=cmbw[:, :, 0:1], in_=lg4[:], axis=AX.X, op=A.max)
                    sV.edge(v, r)
                    r = v.tensor_tensor(out=cmb[:], in0=lg4[:],
                                        in1=cmbw[:, :, 0:1].broadcast_to((128, 4, 8)), op=A.is_equal)
                    sV.edge(v, r)
                    r = v.scalar_tensor_tensor(out=sc4[:, 0, 0:32].rearrange("p (a b) -> p a b", b=8),
                                               in0=cmb[:], scalar=-1e30, in1=lg4[:],
                                               op0=A.mult, op1=A.add)
                    sV.edge(v, r)
                    r = v.tensor_reduce(out=cmbw[:, :, 1:2],
                                        in_=sc4[:, 0, 0:32].rearrange("p (a b) -> p a b", b=8),
                                        axis=AX.X, op=A.max)
                    sV.edge(v, r)
                    r = v.tensor_tensor(out=cmbw[:, :, 2:3], in0=cmbw[:, :, 0:1],
                                        in1=cmbw[:, :, 1:2], op=A.subtract)
                    sV.tag(r)
                _emit("vector", p4_lg_v)

                def p4_sig_s(s, sv_n=sV.n):
                    s.wait_ge(sV.h, sv_n)
                    sS.tag(s.activation(out=cmbw[:, :, 2:3], in_=cmbw[:, :, 2:3], func=AF.Sigmoid))
                _emit("scalar", p4_sig_s)

                def p4_cmb_v(v, ss_n=sS.n):
                    v.wait_ge(sS.h, ss_n)
                    r = v.tensor_scalar(out=cmbw[:, :, 3:4], in0=cmbw[:, :, 2:3],
                                        scalar1=-1.0, scalar2=1.0, op0=A.mult, op1=A.add)
                    sV.edge(v, r)
                    r = v.tensor_tensor(out=sc4[:, 1, 0:32].rearrange("p (a b) -> p a b", b=8),
                                        in0=lg4[:], in1=cmbw[:, :, 1:2].broadcast_to((128, 4, 8)),
                                        op=A.is_equal)
                    sV.edge(v, r)
                    r = v.tensor_tensor(out=cmb[:], in0=cmb[:],
                                        in1=cmbw[:, :, 2:3].broadcast_to((128, 4, 8)), op=A.mult)
                    sV.edge(v, r)
                    r = v.tensor_tensor(out=sc4[:, 1, 0:32].rearrange("p (a b) -> p a b", b=8),
                                        in0=sc4[:, 1, 0:32].rearrange("p (a b) -> p a b", b=8),
                                        in1=cmbw[:, :, 3:4].broadcast_to((128, 4, 8)), op=A.mult)
                    sV.edge(v, r)
                    r = v.tensor_tensor(out=cmb[:], in0=cmb[:],
                                        in1=sc4[:, 1, 0:32].rearrange("p (a b) -> p a b", b=8), op=A.add)
                    sV.tag(r)
                _emit("vector", p4_cmb_v)

                for sub in range(4):
                    def p4_ct_t(t, sub=sub, sv_n=sV.n):
                        t.wait_ge(sV.h, sv_n)
                        sT.tag(t.transpose(pct4[0:8, :], cmb[:, sub, :], identf[:]))
                    _emit("tensor", p4_ct_t)

                    def p4_ct_v(v, sub=sub, st_n=sT.n):
                        v.wait_ge(sT.h, st_n)
                        sV.tag(v.tensor_copy(out=combr[:, sub * 128:(sub + 1) * 128],
                                             in_=pct4[0:8, :]))
                    _emit("vector", p4_ct_v)

                for e in range(8):
                    def p4_ew(g2, e=e, ob=ob, st_n=sT.n):
                        g2.wait_ge(sT.h, st_n)
                        sWg.inc(g2.dma_start(out=ew[:, e % 2, :, :], in_=expert_wT.rearrange(
                            "e (g p) c -> e p g c", p=128)[e]), 16)
                    block.gpsimd(p4_ew)

                    def p4_cb_t(t, e=e, sv_n=sV.n, ss_n=sS.n):
                        t.wait_ge(sV.h, sv_n)
                        if e >= 1:
                            t.wait_ge(sS.h, ss_n)
                        r = t.matmul(pst4[:], lhsT=hot8b[:, e * 128:(e + 1) * 128], rhs=combr[:],
                                     start=True, stop=True)
                        sT.tag(r)
                    _emit("tensor", p4_cb_t)

                    def p4_cb_s(s, e=e, st_n=sT.n):
                        s.wait_ge(sT.h, st_n)
                        sS.tag(s.activation(out=cbsb[:], in_=pst4[:], func=AF.Copy))
                    _emit("scalar", p4_cb_s)

                    for oc in range(NG):
                        ph = ph0 if oc % 2 == 0 else ph1
                        def p4_e_t(t, e=e, oc=oc, ph=ph, swg=sWg.n, sv_n=sV.n):
                            t.wait_ge(sWg.h, swg)
                            if e + oc > 0:
                                t.wait_ge(sV.h, sv_n)
                            for g in range(NG):
                                r = t.matmul(ph[:], lhsT=ew[:, e % 2, g, oc * 128:(oc + 1) * 128],
                                             rhs=rn2[:, g, :], start=(g == 0), stop=(g == NG - 1))
                            sT.tag(r)
                        _emit("tensor", p4_e_t)

                        def p4_e_v(v, e=e, oc=oc, ph=ph, st_n=sT.n, ss_n=sS.n):
                            v.wait_ge(sT.h, st_n)
                            v.wait_ge(sS.h, ss_n)
                            if e == 0:
                                r = v.tensor_tensor(out=macc[:, oc, :], in0=ph[:], in1=cbsb[:], op=A.mult)
                            else:
                                r = v.tensor_tensor(out=sc4[:, 0, :], in0=ph[:], in1=cbsb[:], op=A.mult)
                                sV.edge(v, r)
                                r = v.tensor_tensor(out=macc[:, oc, :], in0=macc[:, oc, :],
                                                    in1=sc4[:, 0, :], op=A.add)
                            sV.tag(r)
                        _emit("vector", p4_e_v)

                def p4_out_v(v):
                    for g in range(NG):
                        r = v.tensor_tensor(out=macc[:, g, :], in0=macc[:, g, :], in1=hb4[:, g, :], op=A.add)
                    sV.tag(r)
                _emit("vector", p4_out_v)

                def p4_gs_s(s, sv_n=sV.n):
                    s.wait_ge(sV.h, sv_n)
                    for g in range(NG):
                        r = s.activation(out=gs4[:, g, :], in_=macc[:, g, :], func=AF.Silu)
                    sS.tag(r)
                _emit("scalar", p4_gs_s)

                def p4_sg_v(v, ss_n=sS.n, sa_n=sA.n):
                    v.wait_ge(sS.h, ss_n)
                    v.wait_ge(sA.h, sa_n)
                    for g in range(NG):
                        r = v.tensor_tensor(out=macc[:, g, :], in0=smine[:, g, :], in1=spart[:, g, :], op=A.add)
                    sV.edge(v, r)
                    for g in range(NG):
                        r = v.tensor_tensor(out=sg4[:, g, :], in0=macc[:, g, :], in1=gs4[:, g, :], op=A.mult)
                    sV.tag(r)
                _emit("vector", p4_sg_v)

                for sub in range(4):
                    def p4_tr_t(t, sub=sub, sv_n=sV.n, ss_n=sS.n):
                        t.wait_ge(sV.h, sv_n)
                        if sub > 0:
                            t.wait_ge(sS.h, ss_n)
                        for g in range(NG):
                            r = t.transpose(ppk4[:, g * 128:(g + 1) * 128],
                                            sg4[:, g, sub * 128:(sub + 1) * 128], ident[:])
                        sT.tag(r)
                    _emit("tensor", p4_tr_t)

                    def p4_x1_v(v, sub=sub, ob=ob, st_n=sT.n, so_n=sO.n):
                        v.wait_ge(sT.h, st_n)
                        v.wait_ge(sO.h, so_n)
                        r = v.tensor_tensor(out=x1t[:], in0=x0t[:, sub, :], in1=ppk4[:], op=A.add)
                        sV.edge(v, r)
                        r = v.reduce_sum(out=st4[:, 0:1], in_=x1t[:], axis=AX.X)
                        sV.tag(r)
                    _emit("vector", p4_x1_v)

                    def p4_ln_s(s, sub=sub, sv_n=sV.n):
                        s.wait_ge(sV.h, sv_n)
                        sS.tag(s.activation(out=xn1t[:], in_=x1t[:], func=AF.Square,
                                            accum_out=st4[:, 1:2]))
                    _emit("scalar", p4_ln_s)

                    def p4_ln_v(v, sub=sub, ss_n=sS.n):
                        v.wait_ge(sS.h, ss_n)
                        r = v.tensor_scalar(out=st4[:, 2:3], in0=st4[:, 0:1], scalar1=1.0 / D, scalar2=None, op0=A.mult)
                        sV.edge(v, r)
                        r = v.tensor_scalar(out=st4[:, 3:4], in0=st4[:, 1:2], scalar1=1.0 / D, scalar2=None, op0=A.mult)
                        sV.edge(v, r)
                        r = v.tensor_tensor(out=st4[:, 4:5], in0=st4[:, 2:3], in1=st4[:, 2:3], op=A.mult)
                        sV.edge(v, r)
                        r = v.tensor_tensor(out=st4[:, 4:5], in0=st4[:, 3:4], in1=st4[:, 4:5], op=A.subtract)
                        sV.edge(v, r)
                        r = v.tensor_scalar(out=st4[:, 4:5], in0=st4[:, 4:5], scalar1=EPS_LN, scalar2=None, op0=A.add)
                        sV.tag(r)
                    _emit("vector", p4_ln_v)

                    def p4_ln_s2(s, sv_n=sV.n):
                        s.wait_ge(sV.h, sv_n)
                        sS.tag(s.activation(out=st4[:, 5:6], in_=st4[:, 4:5], func=AF.Sqrt))
                    _emit("scalar", p4_ln_s2)

                    def p4_ln_v2(v, ss_n=sS.n):
                        v.wait_ge(sS.h, ss_n)
                        r = v.reciprocal(out=st4[:, 6:7], in_=st4[:, 5:6])
                        sV.edge(v, r)
                        r = v.tensor_scalar(out=xn1t[:], in0=x1t[:], scalar1=st4[:, 2:3],
                                            scalar2=st4[:, 6:7], op0=A.subtract, op1=A.mult)
                        sV.edge(v, r)
                        r = v.tensor_scalar(out=x0t[:, sub, :], in0=x1t[:], scalar1=st4[:, 2:3],
                                            scalar2=st4[:, 6:7], op0=A.subtract, op1=A.mult)
                        sV.tag(r)
                    _emit("vector", p4_ln_v2)

                    def p4_xn1_t(t, sub=sub, ob=ob, sv_n=sV.n, ss_n=sS.n):
                        t.wait_ge(sV.h, sv_n)
                        for g in range(NG):
                            r = t.transpose(ppk4[:, g * 128:(g + 1) * 128],
                                            xn1t[:, g * 128:(g + 1) * 128], ident[:])
                        sT.tag(r)
                    _emit("tensor", p4_xn1_t)

                    def p4_xn1_s(s, sub=sub, ob=ob, st_n=sT.n, so_n=sO.n):
                        s.wait_ge(sT.h, st_n)
                        s.wait_ge(sO.h, so_n)
                        r = s.activation(out=gs4[:, 0:2, :].rearrange("p g t -> p (g t)"),
                                         in_=ppk4[:], func=AF.Copy)
                        sS.tag(r)
                    _emit("scalar", p4_xn1_s)

                    def p4_st_dma(sy, sub=sub, ob=ob, ss_n=sS.n, sv_n=sV.n):
                        sy.wait_ge(sS.h, ss_n)
                        sy.wait_ge(sV.h, sv_n)
                        tt = ob * 4 + sub
                        if "x1" in debug:
                            sO.inc(sy.dma_start(out=dbg["x1"][:, tt, :], in_=x1t[:]), 16)
                        if "xn1" in debug:
                            sO.inc(sy.dma_start(out=dbg["xn1"][:, tt, :], in_=x0t[:, sub, :]), 16)
                        sO.inc(sy.dma_start(out=x1_spill[:, tt, :], in_=x1t[:]), 16)
                        sO.inc(sy.dma_start(out=xn1f_spill[:, tt, :], in_=x0t[:, sub, :]), 16)
                        sO.inc(sy.dma_start(out=cc_xn1_in.rearrange("(g p) t -> p g t", p=128)[
                            :, :, ob * TB + sub * 128:ob * TB + (sub + 1) * 128],
                            in_=gs4[:, 0:2, :].rearrange("p g t -> p (g t)").rearrange(
                                "p (g c) -> p g c", c=128)), 16)
                    block.sync(p4_st_dma)

            def p4_cc(g):
                sO.wait(g)
                sG.inc(g.collective_compute("AllGather", A.bypass, replica_groups=GROUPS_ALL,
                                            ins=[cc_xn1_in[:]], outs=[cc_xn1_out[:]]), 1)
            block.gpsimd(p4_cc)

        phase_barrier()
        def emit_quant(get_bufs, in_ap, out_ap, F, sv0=None):
            B = get_bufs
            nb = F // 16

            def qv1(v, sv0=sv0):
                if sv0 is not None:
                    v.wait_ge(sV.h, sv0)
                r = v.tensor_reduce(out=B["bm"], in_=in_ap.rearrange("p (b s) -> p b s", s=16),
                                    axis=AX.X, op=A.max, apply_absolute_value=True)
                sV.edge(v, r)
                r = v.reciprocal(out=B["bmr"], in_=B["bm"])
                sV.edge(v, r)
                r = v.tensor_scalar(out=B["bmr"], in0=B["bmr"], scalar1=6.0, scalar2=1e12,
                                    op0=A.mult, op1=A.min)
                r2 = v.tensor_scalar(out=B["bmd"], in0=B["bm"], scalar1=1.0 / 6.0, scalar2=1e-12,
                                     op0=A.mult, op1=A.max)
                sV.edge(v, r2)
                r = v.tensor_tensor(out=B["xs"].rearrange("p (b s) -> p b s", s=16),
                                    in0=in_ap.rearrange("p (b s) -> p b s", s=16),
                                    in1=B["bmr"].unsqueeze(2).broadcast_to((128, nb, 16)),
                                    op=A.mult)
                sV.tag(r)
            _emit("vector", qv1)

            def qs1(s, sv_n=sV.n):
                s.wait_ge(sV.h, sv_n)
                r = s.activation(out=B["a2"], in_=B["xs"], func=AF.Abs, scale=2.0)
                sS.edge(s, r)
                r = s.activation(out=B["sgn"], in_=B["xs"], func=AF.Sign)
                sS.tag(r)
            _emit("scalar", qs1)

            def qv2(v, ss_n=sS.n):
                v.wait_ge(sS.h, ss_n)
                r = v.tensor_scalar(out=B["q1"], in0=B["a2"], scalar1=0.5, scalar2=2.0, op0=A.mult, op1=A.add)
                r = v.tensor_scalar(out=B["q2"], in0=B["a2"], scalar1=0.25, scalar2=4.0, op0=A.mult, op1=A.add)
                sV.edge(v, r)
                r = v.tensor_tensor(out=B["q1"], in0=B["a2"], in1=B["q1"], op=A.min)
                sV.edge(v, r)
                r = v.tensor_tensor(out=B["q2"], in0=B["q1"], in1=B["q2"], op=A.min)
                sV.edge(v, r)
                r = v.tensor_scalar(out=B["li"], in0=B["q2"], scalar1=1.0, scalar2=None, op0=A.mult)
                sV.edge(v, r)
                r = v.tensor_copy(out=B["xs"], in_=B["li"])
                sV.edge(v, r)
                r = v.tensor_tensor(out=B["sgn2"], in0=B["q2"], in1=B["xs"], op=A.subtract)
                sV.edge(v, r)
                r = v.tensor_scalar(out=B["sgn2"], in0=B["sgn2"], scalar1=0.5, scalar2=None, op0=A.is_ge)
                sV.edge(v, r)
                r = v.tensor_tensor(out=B["xs"], in0=B["xs"], in1=B["sgn2"], op=A.add)
                sV.edge(v, r)
                r = v.tensor_scalar(out=B["q1"], in0=B["xs"], scalar1=0.5, scalar2=None, op0=A.mult)
                r = v.tensor_scalar(out=B["q2"], in0=B["xs"], scalar1=1.0, scalar2=-2.0, op0=A.mult, op1=A.add)
                r = v.tensor_scalar(out=B["a2"], in0=B["xs"], scalar1=2.0, scalar2=-8.0, op0=A.mult, op1=A.add)
                sV.edge(v, r)
                r = v.tensor_tensor(out=B["q1"], in0=B["q1"], in1=B["q2"], op=A.max)
                sV.edge(v, r)
                r = v.tensor_tensor(out=B["q1"], in0=B["q1"], in1=B["a2"], op=A.max)
                sV.edge(v, r)
                r = v.tensor_tensor(out=B["q1"], in0=B["q1"], in1=B["sgn"], op=A.mult)
                sV.edge(v, r)
                r = v.tensor_tensor(out=out_ap.rearrange("p (b s) -> p b s", s=16),
                                    in0=B["q1"].rearrange("p (b s) -> p b s", s=16),
                                    in1=B["bmd"].unsqueeze(2).broadcast_to((128, nb, 16)),
                                    op=A.mult)
                sV.tag(r)
            _emit("vector", qv2)

        es6 = ExitStack()
        def sb6(name, shape, dt=F32):
            return es6.enter_context(nc.sbuf_tensor(name, list(shape), dt))
        qw_in = sb6("qw_in", (128, FFI))
        qw_xs = sb6("qw_xs", (128, FFI))
        qw_a2 = sb6("qw_a2", (128, FFI))
        qw_q1 = sb6("qw_q1", (128, FFI))
        qw_q2 = sb6("qw_q2", (128, FFI))
        qw_sgn = sb6("qw_sgn", (128, FFI))
        qw_sg2 = sb6("qw_sg2", (128, FFI))
        qw_li = sb6("qw_li", (128, FFI), I32)
        qw_bm = sb6("qw_bm", (128, FFI // 16))
        qw_bmr = sb6("qw_bmr", (128, FFI // 16))
        qw_bmd = sb6("qw_bmd", (128, FFI // 16))
        qw_out = sb6("qw_out", (128, FFI), BF16)
        qw_tr = sb6("qw_tr", (128, FFI), BF16)
        pq6 = es6.enter_context(nc.psum_tensor("pq6", [128, 1024], BF16))
        pq6b = es6.enter_context(nc.psum_tensor("pq6b", [128, 1024], BF16))
        with es6:
            def qbufs(F):
                return {"xs": qw_xs[:, 0:F], "a2": qw_a2[:, 0:F], "q1": qw_q1[:, 0:F],
                        "q2": qw_q2[:, 0:F], "sgn": qw_sgn[:, 0:F], "sgn2": qw_sg2[:, 0:F],
                        "li": qw_li[:, 0:F],
                        "bm": qw_bm[:, 0:F // 16], "bmr": qw_bmr[:, 0:F // 16],
                        "bmd": qw_bmd[:, 0:F // 16]}

            for wsrc, wdst in ((l1_wg, wgq_spill), (l1_wu, wuq_spill)):
                for it in range(NFI):
                    def q6_ld(sy, wsrc=wsrc, it=it, sv_n=sV.n):
                        sy.wait_ge(sV.h, sv_n)
                        sA.inc(sy.dma_start(out=qw_in[:, 0:D], in_=wsrc[it * 128:(it + 1) * 128, :]), 16)
                    block.sync(q6_ld)

                    def q6_w(v, sa_n=sA.n):
                        v.wait_ge(sA.h, sa_n)
                        v.engine_nop() if hasattr(v, "engine_nop") else None
                        sV.bump(v)
                    _emit("vector", q6_w)
                    emit_quant(qbufs(D), qw_in[:, 0:D], qw_out[:, 0:D], D)

                    def q6_t(t, it=it, sv_n=sV.n, ss_n=sS.n):
                        t.wait_ge(sV.h, sv_n)
                        t.wait_ge(sS.h, ss_n)
                        for g in range(NG):
                            r = t.transpose(pq6[:, g * 128:(g + 1) * 128],
                                            qw_out[:, g * 128:(g + 1) * 128], ident[:])
                        sT.tag(r)
                    _emit("tensor", q6_t)

                    def q6_s(s, st_n=sT.n, so_n=sO.n):
                        s.wait_ge(sT.h, st_n)
                        s.wait_ge(sO.h, so_n)
                        r = s.activation(out=qw_tr[:, 0:1024], in_=pq6[:], func=AF.Copy)
                        sS.tag(r)
                    _emit("scalar", q6_s)

                    def q6_st(sy, wdst=wdst, it=it, ss_n=sS.n):
                        sy.wait_ge(sS.h, ss_n)
                        sO.inc(sy.dma_start(
                            out=wdst.rearrange("p g c -> p g c")[:, :, it * 128:(it + 1) * 128],
                            in_=qw_tr[:, 0:1024].rearrange("p (g c) -> p g c", c=128)), 16)
                    block.sync(q6_st)

            for it in range(NG):
                def q6_ld2(sy, it=it, sv_n=sV.n):
                    sy.wait_ge(sV.h, sv_n)
                    sA.inc(sy.dma_start(out=qw_in[:], in_=l1_wd[it]), 16)
                block.sync(q6_ld2)

                def q6_w2(v, sa_n=sA.n):
                    v.wait_ge(sA.h, sa_n)
                    sV.bump(v)
                _emit("vector", q6_w2)
                emit_quant(qbufs(FFI), qw_in[:], qw_out[:], FFI)

                for half in range(2):
                    def q6_t2(t, it=it, half=half, sv_n=sV.n, ss_n=sS.n):
                        t.wait_ge(sV.h, sv_n)
                        t.wait_ge(sS.h, ss_n)
                        nn = 11
                        for g in range(nn):
                            r = t.transpose(pq6[:, (g % 8) * 128:(g % 8 + 1) * 128] if False else
                                            pq6[:, g * 128:(g + 1) * 128] if g < 8 else pq6b[:, (g - 8) * 128:(g - 7) * 128],
                                            qw_out[:, (half * 11 + g) * 128:(half * 11 + g + 1) * 128], ident[:])
                        sT.tag(r)
                    _emit("tensor", q6_t2)

                    def q6_s2(s, it=it, half=half, st_n=sT.n, so_n=sO.n):
                        s.wait_ge(sT.h, st_n)
                        s.wait_ge(sO.h, so_n)
                        s.activation(out=qw_tr[:, 0:1024], in_=pq6[:], func=AF.Copy)
                        r = s.activation(out=qw_tr[:, 1024:1408], in_=pq6b[:, 0:384], func=AF.Copy)
                        sS.tag(r)
                    _emit("scalar", q6_s2)

                    def q6_st2(sy, it=it, half=half, ss_n=sS.n):
                        sy.wait_ge(sS.h, ss_n)
                        sO.inc(sy.dma_start(
                            out=wdq_spill[:, half * 11:(half + 1) * 11, it * 128:(it + 1) * 128],
                            in_=qw_tr[:, 0:1408].rearrange("p (g c) -> p g c", c=128)), 16)
                    block.sync(q6_st2)

        phase_barrier()
        es7 = ExitStack()
        def sb7(name, shape, dt=F32):
            return es7.enter_context(nc.sbuf_tensor(name, list(shape), dt))
        l1q_sb = sb7("l1q_sb", (128, NG, 512), BF16)
        l1k_sb = sb7("l1k_sb", (128, NG, 512), BF16)
        l1v_sb = sb7("l1v_sb", (128, NG, 512), BF16)
        xn1b = sb7("xn1b", (128, NG, TB), BF16)
        k1 = sb7("k1", (128, 4, S), BF16)
        v1 = sb7("v1", (128, 32, 520), BF16)
        qb7 = sb7("qb7", (128, 4, TB), BF16)
        atb7 = sb7("atb7", (128, 2, 2, TB), BF16)
        recd7 = sb7("recd7", (128, 2, TB))
        ib7 = sb7("ib7", (64, 2, TB))
        nb7 = sb7("nb7", (64, 2, TB), BF16)
        aob7 = sb7("aob7", (128, 4, TB), BF16)
        psA7 = es7.enter_context(nc.psum_tensor("psA7", [128, TB], F32))
        psB7 = es7.enter_context(nc.psum_tensor("psB7", [128, TB], F32))
        psC7 = es7.enter_context(nc.psum_tensor("psC7", [128, TB], F32))
        psD7 = es7.enter_context(nc.psum_tensor("psD7", [128, TB], F32))
        av07 = es7.enter_context(nc.psum_tensor("av07", [128, TB], F32))
        av17 = es7.enter_context(nc.psum_tensor("av17", [128, TB], F32))
        psM7 = es7.enter_context(nc.psum_tensor("psM7", [128, TB], F32))
        with es7:
            def p7_init(g):
                sWg.inc(g.dma_start(out=l1q_sb[:], in_=l1_wq.rearrange("(g p) c -> p g c", p=128)), 16)
                sWg.inc(g.dma_start(out=l1k_sb[:], in_=l1_wk.rearrange("(g p) c -> p g c", p=128)), 16)
                sWg.inc(g.dma_start(out=l1v_sb[:], in_=l1_wv.rearrange("(g p) c -> p g c", p=128)), 16)
                for ht in range(4):
                    g.memset(v1[:, :, ht * 130 + 64:ht * 130 + 65], 1.0)
                    r = g.memset(v1[:, :, ht * 130 + 129:ht * 130 + 130], 1.0)
                p_ = pid(g)
                b0 = _rv(binary_op(_rv(binary_op(p_, make_scalar_value(~1 & 0xff), A.bitwise_and), 6),
                                   make_scalar_value(D), A.mult), 6 * D)
                b1 = _rv(binary_op(_rv(binary_op(p_, make_scalar_value(1), A.bitwise_or), 7),
                                   make_scalar_value(D), A.mult), 7 * D)
                sWg.inc(g.dma_start(out=xn1_loc[:, 0:HS], in_=cc_xn1_out[ds(b0, D), :]), 16)
                sWg.inc(g.dma_start(out=xn1_loc[:, HS:S], in_=cc_xn1_out[ds(b1, D), :]), 16)
                sG.tag(r)
            block.gpsimd(p7_init)

            for i in range(NBLK):
                t0 = i * TB

                def p7_ld(sy, i=i, t0=t0, sv_n=sV.n, ss_n=sS.n, swg=sWg.n):
                    sy.wait_ge(sV.h, sv_n)
                    sy.wait_ge(sS.h, ss_n)
                    sy.wait_ge(sWg.h, swg)
                    sA.inc(sy.dma_start(out=xn1b[:], in_=xn1_loc.rearrange(
                        "(g p) t -> p g t", p=128)[:, :, t0:t0 + TB]), 16)
                block.sync(p7_ld)

                for hp in range(4):
                    pq = psA7 if hp % 2 == 0 else psB7
                    def p7_q_t(t, hp=hp, pq=pq, sa_n=sA.n, ss_n=sS.n, swg=sWg.n):
                        t.wait_ge(sA.h, sa_n)
                        t.wait_ge(sWg.h, swg)
                        t.wait_ge(sS.h, ss_n)
                        for g in range(NG):
                            r = t.matmul(pq[:], lhsT=l1q_sb[:, g, hp * 128:(hp + 1) * 128],
                                         rhs=xn1b[:, g, :], start=(g == 0), stop=(g == NG - 1))
                        sT.tag(r)
                    _emit("tensor", p7_q_t)

                    def p7_q_s(s, hp=hp, pq=pq, st_n=sT.n):
                        s.wait_ge(sT.h, st_n)
                        sS.tag(s.activation(out=qb7[:, hp, :], in_=pq[:], func=AF.Copy))
                    _emit("scalar", p7_q_s)

                for hp in range(4):
                    pq = psA7 if hp % 2 == 0 else psB7
                    def p7_k_t(t, hp=hp, pq=pq, ss_n=sS.n):
                        t.wait_ge(sS.h, ss_n)
                        for g in range(NG):
                            r = t.matmul(pq[:], lhsT=l1k_sb[:, g, hp * 128:(hp + 1) * 128],
                                         rhs=xn1b[:, g, :], start=(g == 0), stop=(g == NG - 1))
                        sT.tag(r)
                    _emit("tensor", p7_k_t)

                    def p7_k_s(s, hp=hp, pq=pq, t0=t0, st_n=sT.n):
                        s.wait_ge(sT.h, st_n)
                        sS.tag(s.activation(out=k1[:, hp, t0:t0 + TB], in_=pq[:], func=AF.Copy, scale=0.125))
                    _emit("scalar", p7_k_s)

                for sub in range(4):
                    pv = psC7 if sub % 2 == 0 else psD7
                    def p7_v_t(t, i=i, sub=sub, pv=pv, ss_n=sS.n):
                        t.wait_ge(sS.h, ss_n)
                        for hp in range(4):
                            for g in range(NG):
                                r = t.matmul(pv[:, hp * 128:(hp + 1) * 128],
                                             lhsT=xn1b[:, g, sub * 128:(sub + 1) * 128],
                                             rhs=l1v_sb[:, g, hp * 128:(hp + 1) * 128],
                                             start=(g == 0), stop=(g == NG - 1))
                        sT.tag(r)
                    _emit("tensor", p7_v_t)

                    def p7_v_s(s, i=i, sub=sub, pv=pv, st_n=sT.n):
                        s.wait_ge(sT.h, st_n)
                        kt = i * 4 + sub
                        for hp in range(4):
                            s.activation(out=v1[:, kt, hp * 130:hp * 130 + 64],
                                         in_=pv[:, hp * 128:hp * 128 + 64], func=AF.Copy)
                            r = s.activation(out=v1[:, kt, hp * 130 + 65:hp * 130 + 129],
                                             in_=pv[:, hp * 128 + 64:hp * 128 + 128], func=AF.Copy)
                        sS.tag(r)
                    _emit("scalar", p7_v_s)
                kv_done7 = sS.n

                nkt = 4 * i + 4
                for ht in range(4):
                    exp_done = {}
                    msk_done = {}
                    for kt in range(nkt):
                        par = kt % 2
                        scA = psA7 if par == 0 else psC7
                        scB = psB7 if par == 0 else psD7
                        diag = (kt >= 4 * i)

                        def p7_sc(t, i=i, ht=ht, kt=kt, scA=scA, scB=scB,
                                  kv_done7=kv_done7, ss_n=sS.n, exp_done=exp_done):
                            t.wait_ge(sS.h, kv_done7)
                            if kt >= 2:
                                t.wait_ge(sS.h, exp_done[kt - 2])
                            else:
                                t.wait_ge(sS.h, ss_n)
                            r = t.matmul(scA[:], lhsT=k1[0:64, ht, kt * 128:(kt + 1) * 128],
                                         rhs=qb7[0:64, ht, :], start=True, stop=True)
                            r = t.matmul(scB[:], lhsT=k1[64:128, ht, kt * 128:(kt + 1) * 128],
                                         rhs=qb7[64:128, ht, :], start=True, stop=True)
                            sT.tag(r)
                        _emit("tensor", p7_sc)
                        sc_n = sT.n

                        def p7_exp(s, kt=kt, par=par, scA=scA, scB=scB, sc_n=sc_n, exp_done=exp_done):
                            s.wait_ge(sT.h, sc_n)
                            s.activation(out=atb7[:, 0, par, :], in_=scA[:], func=AF.Exp)
                            r = s.activation(out=atb7[:, 1, par, :], in_=scB[:], func=AF.Exp)
                            sS.tag(r)
                            exp_done[kt] = sS.n
                        _emit("scalar", p7_exp)

                        if diag:
                            j = kt - 4 * i

                            def p7_msk(v, kt=kt, par=par, j=j, e_n=sS.n, msk_done=msk_done):
                                v.wait_ge(sS.h, e_n)
                                for hh in range(2):
                                    if j > 0:
                                        v.memset(atb7[:, hh, par, 0:j * 128], 0.0)
                                    r = v.tensor_tensor(out=atb7[:, hh, par, j * 128:(j + 1) * 128],
                                                        in0=atb7[:, hh, par, j * 128:(j + 1) * 128],
                                                        in1=trib[:], op=A.mult)
                                sV.tag(r)
                                msk_done[kt] = sV.n
                            _emit("vector", p7_msk)

                        def p7_av(t, ht=ht, kt=kt, par=par, diag=diag, nkt=nkt,
                                  exp_done=exp_done, msk_done=msk_done, sg_n=sG.n):
                            if kt == 0:
                                t.wait_ge(sG.h, sg_n)
                            t.wait_ge(sS.h, exp_done[kt])
                            if diag:
                                t.wait_ge(sV.h, msk_done[kt])
                            r = t.matmul(av07[0:65, :], lhsT=v1[:, kt, ht * 130:ht * 130 + 65],
                                         rhs=atb7[:, 0, par, :], start=(kt == 0), stop=(kt == nkt - 1))
                            r = t.matmul(av17[0:65, :], lhsT=v1[:, kt, ht * 130 + 65:ht * 130 + 130],
                                         rhs=atb7[:, 1, par, :], start=(kt == 0), stop=(kt == nkt - 1))
                            sT.tag(r)
                        _emit("tensor", p7_av)

                    def p7_ep_v(v, ht=ht, st_n=sT.n, so_n=sO.n):
                        v.wait_ge(sT.h, st_n)
                        if ht == 0:
                            v.wait_ge(sO.h, so_n)
                        v.reciprocal(out=recd7[64:65, 0, :], in_=av07[64:65, :])
                        r = v.reciprocal(out=recd7[64:65, 1, :], in_=av17[64:65, :])
                        sV.tag(r)
                    _emit("vector", p7_ep_v)

                    def p7_ep_t(t, ht=ht, sv_n=sV.n):
                        t.wait_ge(sV.h, sv_n)
                        t.matmul(psM7[0:64, :], lhsT=onesf[64:65, 0:64], rhs=recd7[64:65, 0, :],
                                 start=True, stop=True)
                        r = t.matmul(psC7[0:64, :], lhsT=onesf[64:65, 0:64], rhs=recd7[64:65, 1, :],
                                     start=True, stop=True)
                        sT.tag(r)
                    _emit("tensor", p7_ep_t)

                    def p7_ep_sx(s, ht=ht, st_n=sT.n):
                        s.wait_ge(sT.h, st_n)
                        s.activation(out=ib7[:, 0, :], in_=psM7[0:64, :], func=AF.Copy)
                        r = s.activation(out=ib7[:, 1, :], in_=psC7[0:64, :], func=AF.Copy)
                        sS.tag(r)
                    _emit("scalar", p7_ep_sx)

                    def p7_ep_v2(v, ht=ht, ss_n=sS.n):
                        v.wait_ge(sS.h, ss_n)
                        v.tensor_tensor(out=nb7[:, 0, :], in0=av07[0:64, :], in1=ib7[:, 0, :], op=A.mult)
                        r = v.tensor_tensor(out=nb7[:, 1, :], in0=av17[0:64, :], in1=ib7[:, 1, :], op=A.mult)
                        sV.tag(r)
                    _emit("vector", p7_ep_v2)

                    def p7_ep_t2(t, ht=ht, sv_n=sV.n):
                        t.wait_ge(sV.h, sv_n)
                        t.matmul(psB7[:], lhsT=selb[:, 0:128], rhs=nb7[:, 0, :], start=True, stop=False)
                        r = t.matmul(psB7[:], lhsT=selb[:, 128:256], rhs=nb7[:, 1, :], start=False, stop=True)
                        sT.tag(r)
                    _emit("tensor", p7_ep_t2)

                    def p7_ep_s2(s, ht=ht, st_n=sT.n):
                        s.wait_ge(sT.h, st_n)
                        sS.tag(s.activation(out=aob7[:, ht, :], in_=psB7[:], func=AF.Copy))
                    _emit("scalar", p7_ep_s2)

                def p7_st(sy, i=i, t0=t0, ss_n=sS.n):
                    sy.wait_ge(sS.h, ss_n)
                    sO.inc(sy.dma_start(out=cc_ao1_in[:, :, t0:t0 + TB], in_=aob7[:]), 16)
                block.sync(p7_st)

            def p7_cc(g):
                sO.wait(g)
                sG.inc(g.collective_compute("AllGather", A.bypass, replica_groups=GROUPS_ALL,
                                            ins=[cc_ao1_in[:]], outs=[cc_ao1_out[:]]), 1)
            block.gpsimd(p7_cc)

            def p7_restage(sc):
                sG.wait(sc)
                p_ = pid(sc)
                role_ = _rv(binary_op(p_, make_scalar_value(1), A.bitwise_and), 1)
                hoff = _rv(binary_op(role_, make_scalar_value(HS), A.mult), HS)
                mybase = _rv(binary_op(p_, make_scalar_value(128), A.mult), 7 * 128)
                pbase = _rv(binary_op(_rv(binary_op(p_, make_scalar_value(1), A.bitwise_xor), 7),
                                      make_scalar_value(128), A.mult), 7 * 128)
                sD.inc(sc.dma_start(out=ao1_my[:], in_=cc_ao1_out[ds(mybase, 128), :, ds(hoff, HS)]), 16)
                sD.inc(sc.dma_start(out=ao1_pt[:], in_=cc_ao1_out[ds(pbase, 128), :, ds(hoff, HS)]), 16)
            block.scalar(p7_restage)

        phase_barrier()
        dbg_out("g1", (128, NG, HS), BF16)
        dbg_out("xq", (128, 16, D), BF16)
        es8 = ExitStack()
        def sb8(name, shape, dt=F32):
            return es8.enter_context(nc.sbuf_tensor(name, list(shape), dt))
        l1wo_sb = sb8("l1wo_sb", (128, NG, D), BF16)
        wdq_sb = sb8("wdq_sb", (128, 2, NFI, 128), BF16)
        xin8 = sb8("xin8", (128, D))
        q8_xs = sb8("q8_xs", (128, D))
        q8_a2 = sb8("q8_a2", (128, D))
        q8_q1 = sb8("q8_q1", (128, D))
        q8_q2 = sb8("q8_q2", (128, D))
        q8_sgn = sb8("q8_sgn", (128, D))
        q8_sg2 = sb8("q8_sg2", (128, D))
        q8_li = sb8("q8_li", (128, D), I32)
        q8_bm = sb8("q8_bm", (128, D // 16))
        q8_bmr = sb8("q8_bmr", (128, D // 16))
        q8_bmd = sb8("q8_bmd", (128, D // 16))
        xq8 = sb8("xq8", (128, 4, D), BF16)
        xqf = sb8("xqf", (128, NG, TB), BF16)
        wg8 = sb8("wg8", (128, 2, NG, TB), BF16)
        wu8 = sb8("wu8", (128, 2, NG, TB), BF16)
        h8s = sb8("h8s", (128, TB))
        hq8 = sb8("hq8", (128, TB), BF16)
        hqf = sb8("hqf", (128, NFI, TB), BF16)
        g18 = sb8("g18", (128, NG, TB), BF16)
        ao18 = sb8("ao18", (128, NG, TB), BF16)
        xn1fm8 = sb8("xn1fm8", (128, NG, TB), BF16)
        s1s = sb8("s1s", (128, TB))
        sg18 = sb8("sg18", (128, NG, TB), BF16)
        x1t8 = sb8("x1t8", (128, D))
        x2t8 = sb8("x2t8", (128, D))
        pg8 = es8.enter_context(nc.psum_tensor("pg8", [128, TB], F32))
        pu8 = es8.enter_context(nc.psum_tensor("pu8", [128, TB], F32))
        pt8 = es8.enter_context(nc.psum_tensor("pt8", [128, TB], BF16))
        pd8 = es8.enter_context(nc.psum_tensor("pd8", [128, TB], F32))
        pd8b = es8.enter_context(nc.psum_tensor("pd8b", [128, TB], F32))
        ppk8 = es8.enter_context(nc.psum_tensor("ppk8", [128, 1024], BF16))
        with es8:
            def q8bufs(F):
                return {"xs": q8_xs[:, 0:F], "a2": q8_a2[:, 0:F], "q1": q8_q1[:, 0:F],
                        "q2": q8_q2[:, 0:F], "sgn": q8_sgn[:, 0:F], "sgn2": q8_sg2[:, 0:F],
                        "li": q8_li[:, 0:F],
                        "bm": q8_bm[:, 0:F // 16], "bmr": q8_bmr[:, 0:F // 16],
                        "bmd": q8_bmd[:, 0:F // 16]}

            def p8_init(g):
                sWg.inc(g.dma_start(out=l1wo_sb[:], in_=l1_wo.rearrange("(g p) c -> p g c", p=128)), 16)
            block.gpsimd(p8_init)

            FBLK = [(0, 512), (512, 512), (1024, 512), (1536, 512), (2048, 512), (2560, 256)]
            for ob in range(OBLK):
                ot = ob * TB

                def p8_ld(sy, ob=ob, ot=ot, sv_n=sV.n, ss_n=sS.n, st_n=sT.n, so_n=sO.n):
                    sy.wait_ge(sV.h, sv_n)
                    sy.wait_ge(sS.h, ss_n)
                    sy.wait_ge(sT.h, st_n)
                    sy.wait_ge(sO.h, so_n)
                    sA.inc(sy.dma_start(out=ao18[:, 0:4, :], in_=ao1_my[:, :, ot:ot + TB]), 16)
                    sA.inc(sy.dma_start(out=ao18[:, 4:8, :], in_=ao1_pt[:, :, ot:ot + TB]), 16)
                    sA.inc(sy.dma_start(out=xn1fm8[:], in_=cc_xn1_in.rearrange(
                        "(g p) t -> p g t", p=128)[:, :, ot:ot + TB]), 16)
                block.sync(p8_ld)

                def p8_ldw(g, ob=ob, st_n=sT.n):
                    g.wait_ge(sT.h, st_n)
                    pass
                block.gpsimd(p8_ldw)

                for sub in range(4):
                    def p8_qld(sy, ob=ob, sub=sub, sv_n=sV.n):
                        sy.wait_ge(sV.h, sv_n)
                        sA.inc(sy.dma_start(out=xin8[:], in_=xn1f_spill[:, ob * 4 + sub, :]), 16)
                    block.sync(p8_qld)

                    def p8_qw(v, sub=sub, sa_n=sA.n):
                        v.wait_ge(sA.h, sa_n)
                        sV.bump(v)
                    _emit("vector", p8_qw)
                    emit_quant(q8bufs(D), xin8[:], xq8[:, sub, :], D)

                    if "xq" in debug:
                        def dbg_xq(sy, ob=ob, sub=sub, sv_n=sV.n):
                            sy.wait_ge(sV.h, sv_n)
                            sO.inc(sy.dma_start(out=dbg["xq"][:, ob * 4 + sub, :], in_=xq8[:, sub, :]), 16)
                        block.sync(dbg_xq)

                    def p8_xt(t, sub=sub, sv_n=sV.n, ss_n=sS.n):
                        t.wait_ge(sV.h, sv_n)
                        t.wait_ge(sS.h, ss_n)
                        for g in range(NG):
                            r = t.transpose(ppk8[:, g * 128:(g + 1) * 128],
                                            xq8[:, sub, g * 128:(g + 1) * 128], ident[:])
                        sT.tag(r)
                    _emit("tensor", p8_xt)

                    def p8_xs(s, sub=sub, st_n=sT.n):
                        s.wait_ge(sT.h, st_n)
                        for g in range(NG):
                            r = s.activation(out=xqf[:, g, sub * 128:(sub + 1) * 128],
                                             in_=ppk8[:, g * 128:(g + 1) * 128], func=AF.Copy)
                        sS.tag(r)
                    _emit("scalar", p8_xs)

                for sub in range(4):
                    for fb, (f0, fw) in enumerate(FBLK):
                        def p8_gw(g2, ob=ob, sub=sub, fb=fb, f0=f0, fw=fw, st_n=sT.n):
                            g2.wait_ge(sT.h, st_n)
                            sWg.inc(g2.dma_start(out=wg8[:, fb % 2, :, 0:fw],
                                                 in_=wgq_spill[:, :, f0:f0 + fw]), 16)
                            sWg.inc(g2.dma_start(out=wu8[:, fb % 2, :, 0:fw],
                                                 in_=wuq_spill[:, :, f0:f0 + fw]), 16)
                        block.gpsimd(p8_gw)

                        def p8_g_t(t, sub=sub, fb=fb, f0=f0, fw=fw, swg=sWg.n, ss_n=sS.n, sv_n=sV.n):
                            t.wait_ge(sWg.h, swg)
                            t.wait_ge(sS.h, ss_n)
                            t.wait_ge(sV.h, sv_n)
                            for g in range(NG):
                                t.matmul(pg8[:, 0:fw], lhsT=xqf[:, g, sub * 128:(sub + 1) * 128],
                                         rhs=wg8[:, fb % 2, g, 0:fw], start=(g == 0), stop=(g == NG - 1))
                            for g in range(NG):
                                r = t.matmul(pu8[:, 0:fw], lhsT=xqf[:, g, sub * 128:(sub + 1) * 128],
                                             rhs=wu8[:, fb % 2, g, 0:fw], start=(g == 0), stop=(g == NG - 1))
                            sT.tag(r)
                        _emit("tensor", p8_g_t)

                        def p8_sil(s, fb=fb, fw=fw, st_n=sT.n):
                            s.wait_ge(sT.h, st_n)
                            sS.tag(s.activation(out=h8s[:, 0:fw], in_=pg8[:, 0:fw], func=AF.Silu))
                        _emit("scalar", p8_sil)

                        def p8_hm(v, fb=fb, fw=fw, ss_n=sS.n):
                            v.wait_ge(sS.h, ss_n)
                            r = v.tensor_tensor(out=h8s[:, 0:fw], in0=h8s[:, 0:fw], in1=pu8[:, 0:fw], op=A.mult)
                            sV.tag(r)
                        _emit("vector", p8_hm)
                        emit_quant(q8bufs(fw), h8s[:, 0:fw], hq8[:, 0:fw], fw)

                        nsubt = fw // 128
                        def p8_ht(t, sub=sub, fb=fb, f0=f0, nsubt=nsubt, sv_n=sV.n, ss_n=sS.n):
                            t.wait_ge(sV.h, sv_n)
                            t.wait_ge(sS.h, ss_n)
                            for cch in range(nsubt):
                                r = t.transpose(pt8[:, cch * 128:(cch + 1) * 128],
                                                hq8[:, cch * 128:(cch + 1) * 128], ident[:])
                            sT.tag(r)
                        _emit("tensor", p8_ht)

                        def p8_hs(s, sub=sub, fb=fb, f0=f0, nsubt=nsubt, st_n=sT.n):
                            s.wait_ge(sT.h, st_n)
                            for cch in range(nsubt):
                                r = s.activation(out=hqf[:, f0 // 128 + cch, sub * 128:(sub + 1) * 128],
                                                 in_=pt8[:, cch * 128:(cch + 1) * 128], func=AF.Copy)
                            sS.tag(r)
                        _emit("scalar", p8_hs)

                for oc in range(NG):
                    pdd = pd8 if oc % 2 == 0 else pd8b
                    def p8_dw(g2, oc=oc, st_n=sT.n):
                        g2.wait_ge(sT.h, st_n)
                        sWg.inc(g2.dma_start(out=wdq_sb[:, oc % 2, :, :],
                                             in_=wdq_spill[:, :, oc * 128:(oc + 1) * 128]), 16)
                    block.gpsimd(p8_dw)

                    def p8_d_t(t, oc=oc, pdd=pdd, ss_n=sS.n, swg=sWg.n):
                        t.wait_ge(sS.h, ss_n)
                        t.wait_ge(sWg.h, swg)
                        for g in range(NFI):
                            r = t.matmul(pdd[:], lhsT=wdq_sb[:, oc % 2, g, :],
                                         rhs=hqf[:, g, :], start=(g == 0), stop=(g == NFI - 1))
                        sT.tag(r)
                    _emit("tensor", p8_d_t)

                    def p8_d_s(s, oc=oc, pdd=pdd, st_n=sT.n):
                        s.wait_ge(sT.h, st_n)
                        sS.tag(s.activation(out=g18[:, oc, :], in_=pdd[:], func=AF.Silu))
                    _emit("scalar", p8_d_s)

                if "g1" in debug:
                    def dbg_g1(sy, ob=ob, ot=ot, ss_n=sS.n):
                        sy.wait_ge(sS.h, ss_n)
                        sO.inc(sy.dma_start(out=dbg["g1"][:, :, ot:ot + TB], in_=g18[:]), 16)
                    block.sync(dbg_g1)

                for g in range(NG):
                    pdd = pd8 if g % 2 == 0 else pd8b
                    def p8_s1_t(t, g=g, pdd=pdd, sa_n=sA.n, swg=sWg.n, sv_n=sV.n, ss_n=sS.n):
                        t.wait_ge(sA.h, sa_n)
                        t.wait_ge(sWg.h, swg)
                        t.wait_ge(sS.h, ss_n)
                        if g >= 2:
                            t.wait_ge(sV.h, sv_n)
                        for j in range(NG):
                            r = t.matmul(pdd[:], lhsT=l1wo_sb[:, j, g * 128:(g + 1) * 128],
                                         rhs=ao18[:, j, :], start=(j == 0), stop=(j == NG - 1))
                        sT.tag(r)
                    _emit("tensor", p8_s1_t)

                    def p8_s1_v(v, g=g, pdd=pdd, st_n=sT.n, ss_n=sS.n):
                        v.wait_ge(sT.h, st_n)
                        v.wait_ge(sS.h, ss_n)
                        r = v.tensor_tensor(out=s1s[:], in0=pdd[:], in1=xn1fm8[:, g, :], op=A.add)
                        sV.edge(v, r)
                        r = v.tensor_tensor(out=sg18[:, g, :], in0=s1s[:], in1=g18[:, g, :], op=A.mult)
                        sV.tag(r)
                    _emit("vector", p8_s1_v)

                for sub in range(4):
                    def p8_tr_t(t, sub=sub, sv_n=sV.n, so_n=sO.n):
                        t.wait_ge(sV.h, sv_n)
                        for g in range(NG):
                            r = t.transpose(ppk8[:, g * 128:(g + 1) * 128],
                                            sg18[:, g, sub * 128:(sub + 1) * 128], ident[:])
                        sT.tag(r)
                    _emit("tensor", p8_tr_t)

                    def p8_x1ld(sy, ob=ob, sub=sub, sv_n=sV.n):
                        sy.wait_ge(sV.h, sv_n)
                        sA.inc(sy.dma_start(out=x1t8[:], in_=x1_spill[:, ob * 4 + sub, :]), 16)
                    block.sync(p8_x1ld)

                    def p8_x2_v(v, sub=sub, st_n=sT.n, so_n=sO.n, sa_n=sA.n):
                        v.wait_ge(sT.h, st_n)
                        v.wait_ge(sO.h, so_n)
                        v.wait_ge(sA.h, sa_n)
                        r = v.tensor_tensor(out=x2t8[:], in0=x1t8[:], in1=ppk8[:], op=A.add)
                        sV.tag(r)
                    _emit("vector", p8_x2_v)

                    def p8_y(sy, ob=ob, sub=sub, sv_n=sV.n):
                        sy.wait_ge(sV.h, sv_n)
                        tt = ob * 4 + sub
                        sO.inc(sy.dma_start(out=y_out.rearrange("(n p) d -> p n d", p=128)[:, tt, :],
                                            in_=x2t8[:]), 16)
                    block.sync(p8_y)

        phase_barrier()
        def fin(sy):
            sO.wait(sy)
            sG.wait(sy)
        block.sync(fin)

    return nc, dbg



def _shard_inputs(inputs, c):
    b, r = c // 2, c % 2
    I = inputs
    t = lambda a: np.ascontiguousarray(a)
    x = I["L0_attn_x"][b]
    m = {}
    m["x_half"] = t(x[r * HS:(r + 1) * HS])
    ocols = list(range(r * DH, (r + 1) * DH)) + list(range(2048 + r * DH, 2048 + (r + 1) * DH))
    m["w_in"] = t(I["L0_attn_W_in"][:, ocols])
    m["conv_w"] = t(I["L0_attn_conv_w"][r * DH:(r + 1) * DH, 0, :].reshape(NG, 128, 4).transpose(1, 0, 2))
    m["conv_b"] = t(I["L0_attn_conv_b"][r * DH:(r + 1) * DH].reshape(NG, 128).T)
    m["w_x"] = t(I["L0_attn_W_x"][r * DH:(r + 1) * DH])
    m["w_dt"] = t(I["L0_attn_W_dt"][:, r * DH:(r + 1) * DH])
    m["b_dt"] = t(I["L0_attn_b_dt"][r * DH:(r + 1) * DH].reshape(NG, 128).T)
    m["a_log"] = t(I["L0_attn_A_log"][r * DH:(r + 1) * DH].reshape(NG, 128, NST).transpose(1, 0, 2))
    m["d_vec"] = t(I["L0_attn_D"][r * DH:(r + 1) * DH].reshape(NG, 128).T)
    m["w_out"] = t(I["L0_attn_W_out"][r * DH:(r + 1) * DH])
    m["anorm_w"] = t(I["L0_ffn_attn_norm_w"].reshape(NG, 128).T)
    m["fnorm_w"] = t(I["L0_ffn_ffn_norm_w"].reshape(NG, 128).T)
    pair_cols = []
    for ht in range(4):
        for hl in (ht, ht + 4):
            gh = 8 * r + hl
            pair_cols.extend(range(gh * 64, (gh + 1) * 64))
    m["wqT"] = t(I["L0_ffn_wq"].T[:, pair_cols])
    m["wkT"] = t(I["L0_ffn_wk"].T[:, r * 128:(r + 1) * 128])
    m["wvT"] = t(I["L0_ffn_wv"].T[:, r * 128:(r + 1) * 128])
    woTfull = I["L0_ffn_wo"].T
    pair_rows = []
    for rr in (r, 1 - r):
        for ht in range(4):
            for hl in (ht, ht + 4):
                gh = 8 * rr + hl
                pair_rows.extend(range(gh * 64, (gh + 1) * 64))
    m["woT"] = t(woTfull[pair_rows])
    m["gate_wT"] = t(I["L0_ffn_gate_w"].T)
    m["expert_wT"] = t(np.stack([I["L0_ffn_expert_w"][e].T for e in range(8)]))
    m["l1_wq"] = t(I["L1_attn_Wq"][:, pair_cols])
    m["l1_wk"] = t(I["L1_attn_Wk"][:, pair_cols])
    m["l1_wv"] = t(I["L1_attn_Wv"][:, pair_cols])
    m["l1_wo"] = t(I["L1_attn_Wo"][pair_rows])
    m["l1_wg"] = t(I["L1_ffn_w_gate"])
    m["l1_wu"] = t(I["L1_ffn_w_up"])
    m["l1_wd"] = t(I["L1_ffn_w_down"].reshape(NG, 128, FFI))
    m["ident_in"] = np.eye(128, dtype=np.float32)
    m["ones_in"] = np.ones((128, 128), np.float32)
    kk, qq = np.meshgrid(np.arange(128), np.arange(128), indexing="ij")
    m["tri_in"] = (qq >= kk).astype(np.float32)
    h8 = np.zeros((8, 1024), np.float32)
    for j in range(8):
        h8[j, j * 128:(j + 1) * 128] = 1.0
    m["hot8_in"] = h8
    sel = np.zeros((64, 256), np.float32)
    sel[:, 0:64] = np.eye(64)
    sel[:, 128 + 64:256] = np.eye(64)
    m["sel_in"] = sel
    h16 = np.zeros((16, 2048), np.float32)
    for j in range(16):
        h16[j, j * 128:(j + 1) * 128] = 1.0
    m["hot16_in"] = h16
    return m


_BUILD_CACHE = {}


def kernel(**inputs):
    debug = tuple(os.environ.get("MK_DEBUG", "").split(",")) if os.environ.get("MK_DEBUG") else ()
    key = debug
    if key not in _BUILD_CACHE:
        _BUILD_CACHE[key] = build_nc(debug)
    nc, dbg = _BUILD_CACHE[key]
    in_maps = [_shard_inputs(inputs, c) for c in range(8)]
    res = run_bass_kernel_spmd(nc, in_maps, core_ids=list(range(8)))
    if debug:
        kernel.dbg_results = res.results
    out = np.empty((4, S, D), np.float32)
    for c in range(8):
        b, r = c // 2, c % 2
        out[b, r * HS:(r + 1) * HS] = res.results[c]["y_out"]
    return out


# version 04
# speedup vs baseline: 1.0306x; 1.0306x over previous
import os
import numpy as np
from contextlib import ExitStack

import concourse.bass as bass
import concourse.mybir as mybir
from concourse.bass_utils import run_bass_kernel_spmd
from concourse.alu_op_type import AluOpType as A
from concourse.expressions import binary_op, make_scalar_value, RuntimeValue

F32 = mybir.dt.float32
BF16 = mybir.dt.bfloat16
I32 = mybir.dt.int32
AF = mybir.ActivationFunctionType
AX = mybir.AxisListType

S = 4096
HS = 2048
D = 1024
DH = 1024
NST = 16
DTR = 64
TB = 512
NBLK = 8
OBLK = 4
NG = 8
FFI = 2816
NFI = 22
EPS_LN = 1e-5
EPS_RMS = 1e-6

GROUPS_ALL = [list(range(8))]
GROUPS_PAIR = [[0, 1], [2, 3], [4, 5], [6, 7]]


def _rv(expr, mx=(1 << 30)):
    return RuntimeValue(expr, min_val=0, max_val=mx)


def build_nc(debug=()):
    nc = bass.Bass("TRN2", target_bir_lowering=False, debug=False)
    es = ExitStack()

    def din(name, shape, dt=F32):
        return nc.dram_tensor(name, list(shape), dt, kind="ExternalInput").ap()

    def dint(name, shape, dt=F32, shared=False):
        kw = {"kind": "Internal"}
        if shared:
            kw["addr_space"] = "Shared"
        return nc.dram_tensor(name, list(shape), dt, **kw).ap()

    x_half = din("x_half", (HS, D))
    w_in = din("w_in", (D, 2 * DH))
    conv_w = din("conv_w", (128, NG, 4))
    conv_b = din("conv_b", (128, NG))
    w_x = din("w_x", (D, 96))
    w_dt = din("w_dt", (DTR, DH))
    b_dt = din("b_dt", (128, NG))
    a_log = din("a_log", (128, NG, NST))
    d_vec = din("d_vec", (128, NG))
    w_out = din("w_out", (D, D))
    anorm_w = din("anorm_w", (128, NG))
    fnorm_w = din("fnorm_w", (128, NG))
    wqT = din("wqT", (D, 512))
    wkT = din("wkT", (D, 128))
    wvT = din("wvT", (D, 128))
    woT = din("woT", (D, D))
    gate_wT = din("gate_wT", (D, 8))
    expert_wT = din("expert_wT", (8, D, D))
    l1_wq = din("l1_wq", (D, 512))
    l1_wk = din("l1_wk", (D, 512))
    l1_wv = din("l1_wv", (D, 512))
    l1_wo = din("l1_wo", (D, D))
    l1_wg = din("l1_wg", (FFI, D))
    l1_wu = din("l1_wu", (FFI, D))
    l1_wd = din("l1_wd", (NG, 128, FFI))
    ident_in = din("ident_in", (128, 128))
    ones_in = din("ones_in", (128, 128))
    tri_in = din("tri_in", (128, 128))
    hot8_in = din("hot8_in", (8, 1024))
    sel_in = din("sel_in", (64, 256))
    hot16_in = din("hot16_in", (16, 2048))

    y_out = nc.dram_tensor("y_out", [HS, D], F32, kind="ExternalOutput").ap()

    dbg = {}
    def dbg_out(name, shape, dt=F32):
        if name in debug:
            dbg[name] = nc.dram_tensor("dbg_" + name, list(shape), dt,
                                       kind="ExternalOutput").ap()
        return dbg.get(name)

    cc_xn_in = dint("cc_xn_in", (D, HS), BF16)
    cc_xn_out = dint("cc_xn_out", (8 * D, HS), BF16, shared=True)
    cc_rs_in = dint("cc_rs_in", (1, HS))
    cc_rs_out = dint("cc_rs_out", (8, HS), shared=True)
    u_spill = dint("u_spill", (128, NG, S), BF16)
    res_spill = dint("res_spill", (128, NG, S), BF16)
    cc_dbc_in = dint("cc_dbc_in", (NBLK, 96, TB))
    cc_dbc_out = dint("cc_dbc_out", (NBLK, 96, TB))
    cc_s_in = dint("cc_s_in", (128, NG, S), BF16)
    cc_s_out = dint("cc_s_out", (8 * 128, NG, S), BF16, shared=True)
    cc_ao_in = dint("cc_ao_in", (128, 4, S), BF16)
    cc_ao_out = dint("cc_ao_out", (8 * 128, 4, S), BF16, shared=True)
    cc_xn1_in = dint("cc_xn1_in", (D, HS), BF16)
    cc_xn1_out = dint("cc_xn1_out", (8 * D, HS), BF16, shared=True)
    xn1f_spill = dint("xn1f_spill", (128, 16, D))
    x1_spill = dint("x1_spill", (128, 16, D))
    cc_ao1_in = dint("cc_ao1_in", (128, 4, S), BF16)
    cc_ao1_out = dint("cc_ao1_out", (8 * 128, 4, S), BF16, shared=True)
    ao_my = dint("ao_my", (128, 4, HS), BF16)
    ao_pt = dint("ao_pt", (128, 4, HS), BF16)
    s_my = dint("s_my", (128, NG, HS), BF16)
    s_pt = dint("s_pt", (128, NG, HS), BF16)
    xn1_loc = dint("xn1_loc", (D, S), BF16)
    ao1_my = dint("ao1_my", (128, 4, HS), BF16)
    ao1_pt = dint("ao1_pt", (128, 4, HS), BF16)
    wgq_spill = dint("wgq_spill", (128, NG, FFI), BF16)
    wuq_spill = dint("wuq_spill", (128, NG, FFI), BF16)
    wdq_spill = dint("wdq_spill", (128, NFI, D), BF16)

    def sb(name, shape, dt=F32):
        return es.enter_context(nc.sbuf_tensor(name, list(shape), dt))

    ident = sb("ident", (128, 128), BF16)
    identf = sb("identf", (128, 128), F32)
    onesb = sb("onesb", (128, 128), BF16)
    onesf = sb("onesf", (128, 128), F32)
    trib = sb("trib", (128, 128), BF16)
    hot8f = sb("hot8f", (8, 1024))
    hot8b = sb("hot8b", (8, 1024), BF16)
    hot16b = sb("hot16b", (16, 2048), BF16)
    selb = sb("selb", (64, 256), BF16)
    conv_w_sb = sb("conv_w_sb", (128, NG, 4))
    conv_b_sb = sb("conv_b_sb", (128, NG))
    b_dt_sb = sb("b_dt_sb", (128, NG))
    a_sb = sb("a_sb", (128, NG, NST))
    d_sb = sb("d_sb", (128, NG))
    anw_sb = sb("anw_sb", (128, NG))
    fnw_sb = sb("fnw_sb", (128, NG))
    w_x_sb = sb("w_x_sb", (128, NG, 96), BF16)
    w_dt_sb = sb("w_dt_sb", (DTR, DH), BF16)
    gw_sb = sb("gw_sb", (128, NG, 8), BF16)
    rn_rstd = sb("rn_rstd", (NBLK, TB))
    state = sb("state", (128, NG, NST))

    class Sem:
        def __init__(self, name):
            self.h = es.enter_context(nc.semaphore(name))
            self.n = 0

        def inc(self, instr, k):
            instr.then_inc(self.h, k)
            self.n += k

        def bump(self, eng, k=1):
            eng.sem_inc(self.h, k)
            self.n += k

        def tag(self, instr, k=1):
            instr.then_inc(self.h, k)
            self.n += k

        def edge(self, eng, instr, k=1):
            self.tag(instr, k)
            self.wait(eng)

        def wait(self, eng, n=None):
            eng.wait_ge(self.h, self.n if n is None else n)

    sWg = Sem("sWg")
    sWs = Sem("sWs")
    sA = Sem("sA")
    sV = Sem("sV")
    sS = Sem("sS")
    sT = Sem("sT")
    sG = Sem("sG")
    sD = Sem("sD")
    sO = Sem("sO")

    pid_expr = [None]

    def pid(eng):
        return eng.partition_id()

    def rv_role(eng):
        return _rv(binary_op(pid(eng), make_scalar_value(1), A.bitwise_and))

    def rv_mul(val, k, mx=(1 << 30)):
        return _rv(binary_op(val, make_scalar_value(k), A.mult), mx)

    def rv_xor1(eng):
        return _rv(binary_op(pid(eng), make_scalar_value(1), A.bitwise_xor))

    ds = bass.ds

    with es, nc.Block() as block:
        _eng_sem = {"vector": sV, "scalar": sS, "tensor": sT}

        def phase_barrier():
            sems = [sWg, sWs, sA, sV, sS, sT, sG, sO, sD]
            for engname in ["vector", "scalar", "tensor", "gpsimd", "sync"]:
                def bar(eng):
                    for sm in sems:
                        if sm.n:
                            eng.wait_ge(sm.h, sm.n)
                getattr(block, engname)(bar)

        def _emit(engname, fn):
            sem = _eng_sem.get(engname)
            n = sem.n if sem else 0

            def wrapped(eng):
                if sem is not None and n > 0:
                    eng.wait_ge(sem.h, n)
                fn(eng)
            getattr(block, engname)(wrapped)

        def g_loads(g):
            for dst, src in [(ident, ident_in), (onesb, ones_in), (trib, tri_in),
                             (hot16b, hot16_in), (selb, sel_in), (hot8b, hot8_in),
                             (w_x_sb, w_x.rearrange("(g p) c -> p g c", p=128)), (w_dt_sb, w_dt),
                             (gw_sb, gate_wT.rearrange("(g p) c -> p g c", p=128))]:
                sWg.inc(g.dma_start(out=dst[:], in_=src), 16)
        block.gpsimd(g_loads)

        def sy_loads(sy):
            for dst, src in [(identf, ident_in), (onesf, ones_in),
                             (conv_w_sb, conv_w), (conv_b_sb, conv_b),
                             (b_dt_sb, b_dt), (d_sb, d_vec),
                             (anw_sb, anorm_w), (fnw_sb, fnorm_w),
                             (a_sb, a_log), (hot8f, hot8_in)]:
                sWs.inc(sy.dma_start(out=dst[:], in_=src), 16)
        block.sync(sy_loads)

        def s_a(s):
            sWs.wait(s)
            sS.tag(s.activation(out=a_sb[:], in_=a_sb[:], func=AF.Exp))
        _emit("scalar", s_a)

        def v_a(v):
            sS.wait(v)
            v.tensor_scalar(out=a_sb[:], in0=a_sb[:], scalar1=-1.0, scalar2=None, op0=A.mult)
            sV.tag(v.memset(state[:], 0.0))
        _emit("vector", v_a)

        with nc.sbuf_tensor("xh", [128, 16, D], F32) as xh, \
             nc.sbuf_tensor("xnb", [128, 16, D], BF16) as xnb, \
             nc.sbuf_tensor("sq", [128, 16, 8], F32) as sq, \
             nc.sbuf_tensor("xnf", [128, NG, HS], BF16) as xnf, \
             nc.sbuf_tensor("rsh", [128, 16], F32) as rsh, \
             nc.sbuf_tensor("rs_t", [16, 128], F32) as rs_t, \
             nc.psum_tensor("prs", [16, 128], F32) as prs, \
             nc.psum_tensor("ptr", [128, 1024], BF16) as ptr, \
             nc.psum_tensor("ptr2", [128, 1024], BF16) as ptr2:

            def p0_load(sy):
                sA.inc(sy.dma_start(out=xh[:], in_=x_half.rearrange("(n p) d -> p n d", p=128)), 16)
            block.sync(p0_load)

            def p0_sq(s):
                sA.wait(s)
                for i in range(16):
                    r = s.activation(out=xnb[:, i, :], in_=xh[:, i, :], func=AF.Square,
                                     accum_out=sq[:, i, 1:2])
                sS.tag(r)
            _emit("scalar", p0_sq)

            def p0_v1(v):
                sA.wait(v)
                for i in range(16):
                    r = v.reduce_sum(out=sq[:, i, 0:1], in_=xh[:, i, :], axis=AX.X)
                sV.edge(v, r)
                sS.wait(v)
                for i in range(16):
                    v.tensor_scalar(out=sq[:, i, 2:3], in0=sq[:, i, 0:1], scalar1=1.0 / D, scalar2=None, op0=A.mult)
                    r = v.tensor_scalar(out=sq[:, i, 3:4], in0=sq[:, i, 1:2], scalar1=1.0 / D, scalar2=None, op0=A.mult)
                sV.edge(v, r)
                for i in range(16):
                    r = v.tensor_tensor(out=sq[:, i, 4:5], in0=sq[:, i, 2:3], in1=sq[:, i, 2:3], op=A.mult)
                sV.edge(v, r)
                for i in range(16):
                    r = v.tensor_tensor(out=sq[:, i, 4:5], in0=sq[:, i, 3:4], in1=sq[:, i, 4:5], op=A.subtract)
                sV.edge(v, r)
                for i in range(16):
                    r = v.tensor_scalar(out=sq[:, i, 5:6], in0=sq[:, i, 4:5], scalar1=EPS_LN, scalar2=None, op0=A.add)
                sV.tag(r)
            _emit("vector", p0_v1)

            def p0_s2(s):
                sV.wait(s)
                for i in range(16):
                    r = s.activation(out=sq[:, i, 6:7], in_=sq[:, i, 5:6], func=AF.Sqrt)
                sS.tag(r)
            _emit("scalar", p0_s2)

            def p0_v2(v):
                sS.wait(v)
                for i in range(16):
                    r = v.reciprocal(out=sq[:, i, 7:8], in_=sq[:, i, 6:7])
                sV.edge(v, r)
                for i in range(16):
                    v.tensor_scalar(out=xnb[:, i, :], in0=xh[:, i, :],
                                    scalar1=sq[:, i, 2:3], scalar2=sq[:, i, 7:8],
                                    op0=A.subtract, op1=A.mult)
                for i in range(16):
                    r = v.reciprocal(out=sq[:, i, 0:1], in_=sq[:, i, 5:6])
                sV.edge(v, r)
                for i in range(16):
                    r = v.tensor_tensor(out=sq[:, i, 3:4], in0=sq[:, i, 4:5], in1=sq[:, i, 0:1], op=A.mult)
                sV.edge(v, r)
                for i in range(16):
                    r = v.tensor_scalar(out=sq[:, i, 3:4], in0=sq[:, i, 3:4], scalar1=EPS_RMS, scalar2=None, op0=A.add)
                sV.tag(r)
            _emit("vector", p0_v2)

            def p0_s3(s):
                sV.wait(s)
                for i in range(16):
                    r = s.activation(out=sq[:, i, 5:6], in_=sq[:, i, 3:4], func=AF.Sqrt)
                sS.tag(r)
            _emit("scalar", p0_s3)

            def p0_v3(v):
                sS.wait(v)
                for i in range(16):
                    r = v.reciprocal(out=rsh[:, i:i + 1], in_=sq[:, i, 5:6])
                sV.tag(r)
            _emit("vector", p0_v3)

            def p0_trs(t):
                sV.wait(t)
                sWs.wait(t)
                sWg.wait(t)
                sT.tag(t.transpose(prs[:], rsh[:], identf[:]))
            _emit("tensor", p0_trs)

            def p0_evrs(v):
                sT.wait(v)
                sV.tag(v.tensor_copy(out=rs_t[:], in_=prs[:]))
            _emit("vector", p0_evrs)

            ev_cnt = [sV.n]
            for i in range(16):
                pt = ptr if i % 2 == 0 else ptr2

                def p0_t(t, i=i, pt=pt):
                    if i == 0:
                        sV.wait(t)
                        sWg.wait(t)
                    if i >= 2:
                        t.wait_ge(sV.h, ev_cnt[0] + i - 1)
                    for jj in range(NG):
                        r = t.transpose(pt[:, jj * 128:(jj + 1) * 128],
                                        xnb[:, i, jj * 128:(jj + 1) * 128], ident[:])
                    sT.tag(r)
                _emit("tensor", p0_t)
                tcur = sT.n

                def p0_ev(v, i=i, pt=pt, tcur=tcur):
                    v.wait_ge(sT.h, tcur)
                    for jj in range(NG):
                        r = v.tensor_copy(out=xnf[:, jj, i * 128:(i + 1) * 128],
                                          in_=pt[:, jj * 128:(jj + 1) * 128])
                    sV.tag(r)
                _emit("vector", p0_ev)

            def p0_st(sy):
                sV.wait(sy)
                sO.inc(sy.dma_start(out=cc_xn_in.rearrange("(g p) t -> p g t", p=128), in_=xnf[:]), 16)
                sO.inc(sy.dma_start(out=cc_rs_in[0:1, :].rearrange("o (n p) -> (o n) p", p=128), in_=rs_t[:]), 16)
            block.sync(p0_st)

            def p0_cc(g):
                sO.wait(g)
                sG.inc(g.collective_compute("AllGather", A.bypass, replica_groups=GROUPS_ALL,
                                            ins=[cc_xn_in[:]], outs=[cc_xn_out[:]]), 1)
                sG.inc(g.collective_compute("AllGather", A.bypass, replica_groups=GROUPS_ALL,
                                            ins=[cc_rs_in[:]], outs=[cc_rs_out[:]]), 1)
            block.gpsimd(p0_cc)

        xn_cm = nc.sbuf_tensor("xn", [128, NG, S], BF16)
        xn = xn_cm.__enter__()

        def p0_ld2(sy):
            sG.wait(sy)
            base = rv_mul(_rv(binary_op(pid(sy), make_scalar_value(~1 & 0xff), A.bitwise_and), 6), D, 6 * D)
            src0 = cc_xn_out[ds(base, D), :].rearrange("(g p) t -> p g t", p=128)
            sA.inc(sy.dma_start(out=xn[:, :, 0:HS], in_=src0), 16)
            base1 = rv_mul(_rv(binary_op(pid(sy), make_scalar_value(1), A.bitwise_or), 7), D, 7 * D)
            src1 = cc_xn_out[ds(base1, D), :].rearrange("(g p) t -> p g t", p=128)
            sA.inc(sy.dma_start(out=xn[:, :, HS:S], in_=src1), 16)
            b0 = _rv(binary_op(pid(sy), make_scalar_value(~1 & 0xff), A.bitwise_and), 6)
            sA.inc(sy.dma_start(out=rn_rstd[0:4, :], in_=cc_rs_out[ds(b0, 1), :].rearrange("o (n t) -> (o n) t", t=TB)), 16)
            b1 = _rv(binary_op(pid(sy), make_scalar_value(1), A.bitwise_or), 7)
            sA.inc(sy.dma_start(out=rn_rstd[4:8, :], in_=cc_rs_out[ds(b1, 1), :].rearrange("o (n t) -> (o n) t", t=TB)), 16)
        block.sync(p0_ld2)


        if dbg_out("xn", (128, NG, S), BF16):
            def dxn(sy):
                sA.wait(sy)
                sO.inc(sy.dma_start(out=dbg["xn"][:], in_=xn[:]), 16)
            block.sync(dxn)

        phase_barrier()
        with nc.sbuf_tensor("w_in_sb", [128, NG, 2 * DH], BF16) as w_in_sb, \
             nc.sbuf_tensor("xi_ext", [128, NG, TB + 3], F32) as xi_ext, \
             nc.sbuf_tensor("u_blk", [128, NG, TB], BF16) as u_blk, \
             nc.sbuf_tensor("res_blk", [128, NG, TB], BF16) as res_blk, \
             nc.sbuf_tensor("cacc", [128, NG, TB], F32) as cacc, \
             nc.sbuf_tensor("dbc_ev", [96, TB], F32) as dbc_ev, \
             nc.psum_tensor("ps_mm", [128, TB], F32) as ps_mm, \
             nc.psum_tensor("ps_mm2", [128, TB], F32) as ps_mm2, \
             nc.psum_tensor("ps_dbc", [96, TB], F32) as ps_dbc:

            def p1_wload(g):
                sWg.inc(g.dma_start(out=w_in_sb[:], in_=w_in.rearrange("(g p) c -> p g c", p=128)), 16)
            block.gpsimd(p1_wload)

            def p1_init(v):
                sV.tag(v.memset(xi_ext[:, :, 0:3], 0.0))
            _emit("vector", p1_init)

            p1_evs = []
            for blk in range(NBLK):
                t0 = blk * TB

                for grp in range(16):
                    is_res = grp >= NG
                    oc = grp % NG
                    gidx = blk * 16 + grp
                    pm = ps_mm if grp % 2 == 0 else ps_mm2

                    def p1_t(t, blk=blk, t0=t0, grp=grp, oc=oc, pm=pm, is_res=is_res, gidx=gidx):
                        if gidx == 0:
                            sWg.wait(t)
                            sWs.wait(t)
                            sA.wait(t)
                            sV.wait(t)
                        if gidx >= 2:
                            t.wait_ge(sV.h, p1_evs[gidx - 2])
                        col0 = (DH if is_res else 0) + oc * 128
                        for g in range(NG):
                            r = t.matmul(pm[:], lhsT=w_in_sb[:, g, col0:col0 + 128],
                                         rhs=xn[:, g, t0:t0 + TB],
                                         start=(g == 0), stop=(g == NG - 1))
                        sT.tag(r)
                    _emit("tensor", p1_t)
                    tcur = sT.n

                    def p1_ev(v, grp=grp, oc=oc, pm=pm, is_res=is_res, tcur=tcur,
                              so_n=sO.n, ss_n=sS.n):
                        v.wait_ge(sT.h, tcur)
                        if grp == 0:
                            v.wait_ge(sO.h, so_n)
                            v.wait_ge(sS.h, ss_n)
                        if is_res:
                            r = v.tensor_copy(out=res_blk[:, oc, :], in_=pm[:])
                        else:
                            r = v.tensor_copy(out=xi_ext[:, oc, 3:3 + TB], in_=pm[:])
                        sV.tag(r)
                    _emit("vector", p1_ev)
                    p1_evs.append(sV.n)

                def p1_conv(v, blk=blk, ss_n=sS.n):
                    v.wait_ge(sS.h, ss_n)
                    for oc in range(NG):
                        r = v.tensor_scalar(out=cacc[:, oc, :], in0=xi_ext[:, oc, 0:TB],
                                            scalar1=conv_w_sb[:, oc, 0:1], scalar2=None, op0=A.mult)
                    sV.edge(v, r)
                    for kk in range(1, 4):
                        for oc in range(NG):
                            r = v.scalar_tensor_tensor(out=cacc[:, oc, :], in0=xi_ext[:, oc, kk:kk + TB],
                                                       scalar=conv_w_sb[:, oc, kk:kk + 1],
                                                       in1=cacc[:, oc, :], op0=A.mult, op1=A.add)
                        sV.edge(v, r)
                    for oc in range(NG):
                        r = v.tensor_copy(out=xi_ext[:, oc, 0:3], in_=xi_ext[:, oc, TB:TB + 3])
                    sV.tag(r)
                _emit("vector", p1_conv)

                def p1_s(s, blk=blk, sv_n=sV.n, st_n=sT.n, so_n=sO.n):
                    s.wait_ge(sV.h, sv_n)
                    s.wait_ge(sT.h, st_n)
                    s.wait_ge(sO.h, so_n)
                    for oc in range(NG):
                        r = s.activation(out=u_blk[:, oc, :], in_=cacc[:, oc, :], func=AF.Silu,
                                         bias=conv_b_sb[:, oc:oc + 1], scale=1.0)
                    sS.tag(r)
                _emit("scalar", p1_s)

                def p1_t2(t, blk=blk, t0=t0):
                    sS.wait(t)
                    for g in range(NG):
                        r = t.matmul(ps_dbc[:], lhsT=w_x_sb[:, g, :], rhs=u_blk[:, g, :],
                                     start=(g == 0), stop=(g == NG - 1))
                    sT.tag(r)
                _emit("tensor", p1_t2)

                def p1_ev3(v, blk=blk, so_n=sO.n):
                    sT.wait(v)
                    v.wait_ge(sO.h, so_n)
                    sV.tag(v.tensor_copy(out=dbc_ev[:], in_=ps_dbc[:]))
                _emit("vector", p1_ev3)

                def p1_st(sy, blk=blk, t0=t0):
                    sV.wait(sy)
                    sS.wait(sy)
                    sO.inc(sy.dma_start(out=u_spill[:, :, t0:t0 + TB], in_=u_blk[:]), 16)
                    sO.inc(sy.dma_start(out=res_spill[:, :, t0:t0 + TB], in_=res_blk[:]), 16)
                    sO.inc(sy.dma_start(out=cc_dbc_in[blk], in_=dbc_ev[:]), 16)
                block.sync(p1_st)

            def p1_cc(g):
                sO.wait(g)
                sG.inc(g.collective_compute("AllReduce", A.add, replica_groups=GROUPS_PAIR,
                                            ins=[cc_dbc_in[:]], outs=[cc_dbc_out[:]]), 1)
            block.gpsimd(p1_cc)

            if dbg_out("u", (128, NG, S), BF16):
                def du_(sy):
                    sO.wait(sy)
                    sO.inc(sy.dma_start(out=dbg["u"][:], in_=u_spill[:]), 16)
                block.sync(du_)
            if dbg_out("res", (128, NG, S), BF16):
                def dr_(sy):
                    sO.wait(sy)
                    sO.inc(sy.dma_start(out=dbg["res"][:], in_=res_spill[:]), 16)
                block.sync(dr_)
            if dbg_out("dbc", (NBLK, 96, TB)):
                def dd_(sy):
                    sG.wait(sy)
                    sO.inc(sy.dma_start(out=dbg["dbc"][:], in_=cc_dbc_out[:]), 16)
                block.sync(dd_)

        phase_barrier()
        kscale_l0 = 0.125
        es2 = ExitStack()
        def sb2(name, shape, dt=F32):
            return es2.enter_context(nc.sbuf_tensor(name, list(shape), dt))
        wq_sb = sb2("wq_sb", (128, NG, 512), BF16)
        wk_sb = sb2("wk_sb", (128, NG, 128), BF16)
        wv_sb = sb2("wv_sb", (128, NG, 128), BF16)
        k_fm = sb2("k_fm", (128, S), BF16)
        v_tok = sb2("v_tok", (128, 32, 130), BF16)
        rnb = sb2("rnb", (128, NG, TB), BF16)
        qb = sb2("qb", (128, 4, TB), BF16)
        atb = sb2("atb", (128, 2, 2, TB), BF16)
        recd = sb2("recd", (128, 2, TB))
        ib2 = sb2("ib2", (64, 2, TB))
        nb = sb2("nb", (64, 2, TB), BF16)
        aob = sb2("aob", (128, 4, TB), BF16)
        psA = es2.enter_context(nc.psum_tensor("psA", [128, TB], F32))
        psB = es2.enter_context(nc.psum_tensor("psB", [128, TB], F32))
        psC = es2.enter_context(nc.psum_tensor("psC", [128, TB], F32))
        psD = es2.enter_context(nc.psum_tensor("psD", [128, TB], F32))
        av0 = es2.enter_context(nc.psum_tensor("av0", [128, TB], F32))
        av1 = es2.enter_context(nc.psum_tensor("av1", [128, TB], F32))
        psM = es2.enter_context(nc.psum_tensor("psM", [128, TB], F32))
        with es2:

            def p2_init(g):
                sWg.inc(g.dma_start(out=wq_sb[:], in_=wqT.rearrange("(g p) c -> p g c", p=128)), 16)
                sWg.inc(g.dma_start(out=wk_sb[:], in_=wkT.rearrange("(g p) c -> p g c", p=128)), 16)
                sWg.inc(g.dma_start(out=wv_sb[:], in_=wvT.rearrange("(g p) c -> p g c", p=128)), 16)
                g.memset(v_tok[:, :, 64:65], 1.0)
                r = g.memset(v_tok[:, :, 129:130], 1.0)
                sG.tag(r)
            block.gpsimd(p2_init)

            for i in range(NBLK):
                t0 = i * TB

                def p2_rn_t(t, i=i, sv_n=sV.n):
                    t.wait_ge(sV.h, sv_n)
                    r = t.matmul(psM[:], lhsT=hot8f[:, i * 128:(i + 1) * 128], rhs=rn_rstd[:],
                                 start=True, stop=True)
                    sT.tag(r)
                _emit("tensor", p2_rn_t)

                def p2_rn_v(v, i=i, st_n=sT.n):
                    v.wait_ge(sT.h, st_n)
                    for g in range(NG):
                        r = v.scalar_tensor_tensor(out=rnb[:, g, :], in0=xn[:, g, t0:t0 + TB],
                                                   scalar=anw_sb[:, g:g + 1], in1=psM[:],
                                                   op0=A.mult, op1=A.mult)
                    sV.tag(r)
                _emit("vector", p2_rn_v)
                rn_done = sV.n

                for hp in range(4):
                    pq = psA if hp % 2 == 0 else psB
                    def p2_q_t(t, hp=hp, pq=pq, rn_done=rn_done, ss_n=sS.n, swg=sWg.n):
                        t.wait_ge(sWg.h, swg)
                        t.wait_ge(sV.h, rn_done)
                        t.wait_ge(sS.h, ss_n)
                        for g in range(NG):
                            r = t.matmul(pq[:], lhsT=wq_sb[:, g, hp * 128:(hp + 1) * 128],
                                         rhs=rnb[:, g, :], start=(g == 0), stop=(g == NG - 1))
                        sT.tag(r)
                    _emit("tensor", p2_q_t)

                    def p2_q_s(s, hp=hp, pq=pq, st_n=sT.n):
                        s.wait_ge(sT.h, st_n)
                        sS.tag(s.activation(out=qb[:, hp, :], in_=pq[:], func=AF.Copy))
                    _emit("scalar", p2_q_s)

                def p2_kv_t(t, i=i, rn_done=rn_done, ss_n=sS.n):
                    t.wait_ge(sS.h, ss_n)
                    for g in range(NG):
                        r = t.matmul(psC[:], lhsT=wk_sb[:, g, :], rhs=rnb[:, g, :],
                                     start=(g == 0), stop=(g == NG - 1))
                    sT.tag(r)
                    for sub in range(4):
                        for g in range(NG):
                            r = t.matmul(psD[:, sub * 128:(sub + 1) * 128],
                                         lhsT=rnb[:, g, sub * 128:(sub + 1) * 128],
                                         rhs=wv_sb[:, g, :], start=(g == 0), stop=(g == NG - 1))
                    sT.tag(r)
                _emit("tensor", p2_kv_t)

                def p2_kv_s(s, i=i, t0=t0, st_n=sT.n):
                    s.wait_ge(sT.h, st_n)
                    s.activation(out=k_fm[:, t0:t0 + TB], in_=psC[:], func=AF.Copy, scale=kscale_l0)
                    for sub in range(4):
                        s.activation(out=v_tok[:, i * 4 + sub, 0:64],
                                     in_=psD[:, sub * 128 + 0:sub * 128 + 64], func=AF.Copy)
                        r = s.activation(out=v_tok[:, i * 4 + sub, 65:129],
                                         in_=psD[:, sub * 128 + 64:sub * 128 + 128], func=AF.Copy)
                    sS.tag(r)
                _emit("scalar", p2_kv_s)
                kv_done = sS.n

                nkt = 4 * i + 4
                for ht in range(4):
                    exp_done = {}
                    msk_done = {}
                    for kt in range(nkt):
                        par = kt % 2
                        scA = psA if par == 0 else psC
                        scB = psB if par == 0 else psD
                        diag = (kt >= 4 * i)

                        def p2_sc(t, i=i, ht=ht, kt=kt, par=par, scA=scA, scB=scB,
                                  kv_done=kv_done, ss_n=sS.n, exp_done=exp_done):
                            t.wait_ge(sS.h, kv_done)
                            if kt >= 2:
                                t.wait_ge(sS.h, exp_done[kt - 2])
                            elif kt < 2:
                                t.wait_ge(sS.h, ss_n)
                            r = t.matmul(scA[:], lhsT=k_fm[0:64, kt * 128:(kt + 1) * 128],
                                         rhs=qb[0:64, ht, :], start=True, stop=True)
                            r = t.matmul(scB[:], lhsT=k_fm[64:128, kt * 128:(kt + 1) * 128],
                                         rhs=qb[64:128, ht, :], start=True, stop=True)
                            sT.tag(r)
                        _emit("tensor", p2_sc)
                        sc_n = sT.n

                        def p2_exp(s, kt=kt, par=par, scA=scA, scB=scB, sc_n=sc_n,
                                   tav_need=None, exp_done=exp_done):
                            s.wait_ge(sT.h, sc_n)
                            s.activation(out=atb[:, 0, par, :], in_=scA[:], func=AF.Exp)
                            r = s.activation(out=atb[:, 1, par, :], in_=scB[:], func=AF.Exp)
                            sS.tag(r)
                            exp_done[kt] = sS.n
                        _emit("scalar", p2_exp)

                        if diag:
                            j = kt - 4 * i

                            def p2_msk(v, kt=kt, par=par, j=j, e_n=sS.n, msk_done=msk_done):
                                v.wait_ge(sS.h, e_n)
                                for hh in range(2):
                                    if j > 0:
                                        v.memset(atb[:, hh, par, 0:j * 128], 0.0)
                                    r = v.tensor_tensor(out=atb[:, hh, par, j * 128:(j + 1) * 128],
                                                        in0=atb[:, hh, par, j * 128:(j + 1) * 128],
                                                        in1=trib[:], op=A.mult)
                                sV.tag(r)
                                msk_done[kt] = sV.n
                            _emit("vector", p2_msk)

                        def p2_av(t, i=i, ht=ht, kt=kt, par=par, diag=diag,
                                  exp_done=exp_done, msk_done=msk_done, sg_n=sG.n):
                            if kt == 0:
                                t.wait_ge(sG.h, sg_n)
                            t.wait_ge(sS.h, exp_done[kt])
                            if diag:
                                t.wait_ge(sV.h, msk_done[kt])
                            r = t.matmul(av0[0:65, :], lhsT=v_tok[:, kt, 0:65],
                                         rhs=atb[:, 0, par, :], start=(kt == 0), stop=(kt == nkt - 1))
                            r = t.matmul(av1[0:65, :], lhsT=v_tok[:, kt, 65:130],
                                         rhs=atb[:, 1, par, :], start=(kt == 0), stop=(kt == nkt - 1))
                            sT.tag(r)
                        _emit("tensor", p2_av)

                    def p2_ep_v(v, ht=ht, st_n=sT.n, so_n=sO.n):
                        v.wait_ge(sT.h, st_n)
                        if ht == 0:
                            v.wait_ge(sO.h, so_n)
                        v.reciprocal(out=recd[64:65, 0, :], in_=av0[64:65, :])
                        r = v.reciprocal(out=recd[64:65, 1, :], in_=av1[64:65, :])
                        sV.tag(r)
                    _emit("vector", p2_ep_v)

                    def p2_ep_t(t, ht=ht, sv_n=sV.n):
                        t.wait_ge(sV.h, sv_n)
                        t.matmul(psM[0:64, :], lhsT=onesf[64:65, 0:64], rhs=recd[64:65, 0, :],
                                 start=True, stop=True)
                        r = t.matmul(psC[0:64, :], lhsT=onesf[64:65, 0:64], rhs=recd[64:65, 1, :],
                                     start=True, stop=True)
                        sT.tag(r)
                    _emit("tensor", p2_ep_t)

                    def p2_ep_sx(s, ht=ht, st_n=sT.n):
                        s.wait_ge(sT.h, st_n)
                        s.activation(out=ib2[:, 0, :], in_=psM[0:64, :], func=AF.Copy)
                        r = s.activation(out=ib2[:, 1, :], in_=psC[0:64, :], func=AF.Copy)
                        sS.tag(r)
                    _emit("scalar", p2_ep_sx)

                    def p2_ep_v2(v, ht=ht, ss_n=sS.n):
                        v.wait_ge(sS.h, ss_n)
                        v.tensor_tensor(out=nb[:, 0, :], in0=av0[0:64, :], in1=ib2[:, 0, :], op=A.mult)
                        r = v.tensor_tensor(out=nb[:, 1, :], in0=av1[0:64, :], in1=ib2[:, 1, :], op=A.mult)
                        sV.tag(r)
                    _emit("vector", p2_ep_v2)

                    def p2_ep_t2(t, ht=ht, sv_n=sV.n):
                        t.wait_ge(sV.h, sv_n)
                        t.matmul(psB[:], lhsT=selb[:, 0:128], rhs=nb[:, 0, :], start=True, stop=False)
                        r = t.matmul(psB[:], lhsT=selb[:, 128:256], rhs=nb[:, 1, :], start=False, stop=True)
                        sT.tag(r)
                    _emit("tensor", p2_ep_t2)

                    def p2_ep_s2(s, ht=ht, st_n=sT.n):
                        s.wait_ge(sT.h, st_n)
                        sS.tag(s.activation(out=aob[:, ht, :], in_=psB[:], func=AF.Copy))
                    _emit("scalar", p2_ep_s2)

                def p2_st(sy, i=i, t0=t0, ss_n=sS.n):
                    sy.wait_ge(sS.h, ss_n)
                    sO.inc(sy.dma_start(out=cc_ao_in[:, :, t0:t0 + TB], in_=aob[:]), 16)
                block.sync(p2_st)

            def p2_cc(g):
                sO.wait(g)
                sG.inc(g.collective_compute("AllGather", A.bypass, replica_groups=GROUPS_ALL,
                                            ins=[cc_ao_in[:]], outs=[cc_ao_out[:]]), 1)
            block.gpsimd(p2_cc)

            def p2_restage(sc):
                sG.wait(sc)
                p_ = pid(sc)
                role_ = _rv(binary_op(p_, make_scalar_value(1), A.bitwise_and), 1)
                hoff = _rv(binary_op(role_, make_scalar_value(HS), A.mult), HS)
                mybase = _rv(binary_op(p_, make_scalar_value(128), A.mult), 7 * 128)
                pbase = _rv(binary_op(_rv(binary_op(p_, make_scalar_value(1), A.bitwise_xor), 7),
                                      make_scalar_value(128), A.mult), 7 * 128)
                sD.inc(sc.dma_start(out=ao_my[:], in_=cc_ao_out[ds(mybase, 128), :, ds(hoff, HS)]), 16)
                sD.inc(sc.dma_start(out=ao_pt[:], in_=cc_ao_out[ds(pbase, 128), :, ds(hoff, HS)]), 16)
            block.scalar(p2_restage)

        xn_cm.__exit__(None, None, None)
        phase_barrier()

        es3 = ExitStack()
        def sb3(name, shape, dt=F32):
            return es3.enter_context(nc.sbuf_tensor(name, list(shape), dt))
        scr = sb3("scr", (128, 32, TB), BF16)
        scrd = sb3("scrd", (128, NG, TB))
        yb3 = sb3("yb3", (128, 2, TB))
        delta = sb3("delta", (128, NG, TB), BF16)
        du = sb3("du", (128, NG, TB), BF16)
        bcB = sb3("bcB", (128, NST, TB), BF16)
        bcC = sb3("bcC", (128, NST, TB), BF16)
        dbu = sb3("dbu", (128, NST, TB), BF16)
        hbuf = sb3("hbuf", (128, NST, TB), BF16)
        u3 = sb3("u3", (128, NG, TB), BF16)
        res3 = sb3("res3", (128, NG, TB), BF16)
        ygc = sb3("ygc", (128, NG, TB), BF16)
        spb = sb3("spb", (128, NG, TB), BF16)
        dbc3 = sb3("dbc3", (64, TB), BF16)
        w_out_sb = sb3("w_out_sb", (128, NG, D), BF16)
        dbc3B = sb3("dbc3B", (16, TB), BF16)
        dbc3C = sb3("dbc3C", (16, TB), BF16)
        pw0 = es3.enter_context(nc.psum_tensor("pw0", [128, TB], F32))
        pw1 = es3.enter_context(nc.psum_tensor("pw1", [128, TB], F32))
        pby = es3.enter_context(nc.psum_tensor("pby", [128, TB], F32))
        pbs = es3.enter_context(nc.psum_tensor("pbs", [128, TB], F32))
        pbB = es3.enter_context(nc.psum_tensor("pbB", [128, TB], F32))
        pbC = es3.enter_context(nc.psum_tensor("pbC", [128, TB], F32))
        with es3:

            for blk in range(NBLK):
                t0 = blk * TB

                def p3_ld(sy, blk=blk, t0=t0, sv_n=sV.n, st_n=sT.n, ss_n=sS.n):
                    sy.wait_ge(sV.h, sv_n)
                    sy.wait_ge(sS.h, ss_n)
                    sA.inc(sy.dma_start(out=u3[:], in_=u_spill[:, :, t0:t0 + TB]), 16)
                    sA.inc(sy.dma_start(out=res3[:], in_=res_spill[:, :, t0:t0 + TB]), 16)
                block.sync(p3_ld)

                def p3_ldg(g, blk=blk, sv_n=sV.n, st_n=sT.n):
                    if blk == 0:
                        sWg.inc(g.dma_start(out=w_out_sb[:], in_=w_out.rearrange("(g p) c -> p g c", p=128)), 16)
                    sG.wait(g)
                    g.wait_ge(sT.h, st_n)
                    sWg.inc(g.dma_start(out=dbc3[:], in_=cc_dbc_out[blk, 0:64, :]), 16)
                    sWg.inc(g.dma_start(out=dbc3B[:], in_=cc_dbc_out[blk, 64:80, :]), 16)
                    sWg.inc(g.dma_start(out=dbc3C[:], in_=cc_dbc_out[blk, 80:96, :]), 16)
                block.gpsimd(p3_ldg)

                for n in range(NST):
                    def p3_bc_t(t, n=n, swg=sWg.n, sv_n=sV.n):
                        t.wait_ge(sWg.h, swg)
                        if n >= 1:
                            t.wait_ge(sV.h, sv_n)
                        t.matmul(pbB[:], lhsT=hot16b[:, n * 128:(n + 1) * 128], rhs=dbc3B[:],
                                 start=True, stop=True)
                        r = t.matmul(pbC[:], lhsT=hot16b[:, n * 128:(n + 1) * 128], rhs=dbc3C[:],
                                     start=True, stop=True)
                        sT.tag(r)
                    _emit("tensor", p3_bc_t)

                    def p3_bc_v(v, n=n, st_n=sT.n):
                        v.wait_ge(sT.h, st_n)
                        v.tensor_copy(out=bcB[:, n, :], in_=pbB[:])
                        r = v.tensor_copy(out=bcC[:, n, :], in_=pbC[:])
                        sV.tag(r)
                    _emit("vector", p3_bc_v)

                for g in range(NG):
                    pw = pw0 if g % 2 == 0 else pw1
                    def p3_d_t(t, g=g, pw=pw, swg=sWg.n, sv_n=sV.n, ss_n=sS.n):
                        t.wait_ge(sWg.h, swg)
                        if g >= 2:
                            t.wait_ge(sV.h, sv_n)
                            t.wait_ge(sS.h, ss_n)
                        r = t.matmul(pw[:], lhsT=w_dt_sb[:, g * 128:(g + 1) * 128],
                                     rhs=dbc3[0:64, :], start=True, stop=True)
                        sT.tag(r)
                    _emit("tensor", p3_d_t)

                    def p3_d_v(v, g=g, pw=pw, st_n=sT.n):
                        v.wait_ge(sT.h, st_n)
                        r = v.tensor_scalar(out=delta[:, g, :], in0=pw[:],
                                            scalar1=b_dt_sb[:, g:g + 1], scalar2=0.0,
                                            op0=A.add, op1=A.max)
                        sV.tag(r)
                    _emit("vector", p3_d_v)

                    def p3_d_s(s, g=g, pw=pw, st_n=sT.n):
                        s.wait_ge(sT.h, st_n)
                        r = s.activation(out=scrd[:, g, :], in_=pw[:], func=AF.Abs,
                                         bias=b_dt_sb[:, g:g + 1])
                        sS.tag(r)
                    _emit("scalar", p3_d_s)

                def p3_sp_s(s):
                    for g in range(NG):
                        r = s.activation(out=scrd[:, g, :], in_=scrd[:, g, :], func=AF.Exp, scale=-1.0)
                    sS.edge(s, r)
                    for g in range(NG):
                        r = s.activation(out=scrd[:, g, :], in_=scrd[:, g, :], func=AF.Ln, bias=1.0)
                    sS.tag(r)
                _emit("scalar", p3_sp_s)

                def p3_dfin(v, ss_n=sS.n, sa_n=sA.n):
                    v.wait_ge(sS.h, ss_n)
                    v.wait_ge(sA.h, sa_n)
                    for g in range(NG):
                        r = v.tensor_tensor(out=delta[:, g, :], in0=delta[:, g, :],
                                            in1=scrd[:, g, :], op=A.add)
                    sV.edge(v, r)
                    for g in range(NG):
                        r = v.tensor_tensor(out=du[:, g, :], in0=delta[:, g, :],
                                            in1=u3[:, g, :], op=A.mult)
                    sV.tag(r)
                _emit("vector", p3_dfin)

                scan_done = {}
                for g in range(NG):
                    sl = (g % 2) * 16

                    def p3_dA(s, g=g, sl=sl, sv_n=sV.n, scan_done=scan_done):
                        if g >= 2:
                            s.wait_ge(sV.h, scan_done[g - 2])
                        else:
                            s.wait_ge(sV.h, sv_n)
                        for n in range(NST):
                            r = s.activation(out=scr[:, sl + n, :], in_=delta[:, g, :], func=AF.Exp,
                                             scale=a_sb[:, g, n:n + 1])
                        r = s.activation(out=yb3[:, 1, :], in_=res3[:, g, :], func=AF.Silu) if False else r
                        sS.tag(r)
                    _emit("scalar", p3_dA)

                    def p3_dbu(v, g=g, sl=sl, ss_n=sS.n):
                        v.wait_ge(sS.h, ss_n)
                        r = v.tensor_tensor(out=dbu[:],
                                            in0=du[:, g, :].unsqueeze(1).broadcast_to((128, NST, TB)),
                                            in1=bcB[:], op=A.mult)
                        sV.edge(v, r)
                        for n in range(NST):
                            r = v.tensor_tensor_scan(out=hbuf[:, n, :], data0=scr[:, sl + n, :],
                                                     data1=dbu[:, n, :],
                                                     initial=state[:, g, n:n + 1],
                                                     op0=A.mult, op1=A.add)
                        sV.edge(v, r)
                        scan_done[g] = sV.n
                        r = v.tensor_copy(out=state[:, g, :],
                                          in_=hbuf[:, :, 511:512].rearrange("p n o -> p (n o)"))
                        sV.edge(v, r)
                        r = v.tensor_tensor(out=hbuf[:], in0=hbuf[:], in1=bcC[:], op=A.mult)
                        sV.tag(r)
                    _emit("vector", p3_dbu)

                    def p3_y_t(t, g=g, sv_n=sV.n):
                        t.wait_ge(sV.h, sv_n)
                        for n in range(NST):
                            r = t.matmul(pby[:], lhsT=ident[:], rhs=hbuf[:, n, :],
                                         start=(n == 0), stop=(n == NST - 1))
                        sT.tag(r)
                    _emit("tensor", p3_y_t)

                    def p3_y_s(s, g=g, sa_n=sA.n, sv_n=sV.n):
                        s.wait_ge(sA.h, sa_n)
                        s.wait_ge(sV.h, sv_n)
                        sS.tag(s.activation(out=yb3[:, 1, :], in_=res3[:, g, :], func=AF.Silu))
                    _emit("scalar", p3_y_s)

                    def p3_y_v(v, g=g, st_n=sT.n, ss_n=sS.n):
                        v.wait_ge(sT.h, st_n)
                        v.wait_ge(sS.h, ss_n)
                        r = v.scalar_tensor_tensor(out=yb3[:, 0, :], in0=u3[:, g, :],
                                                   scalar=d_sb[:, g:g + 1], in1=pby[:],
                                                   op0=A.mult, op1=A.add)
                        sV.edge(v, r)
                        r = v.tensor_tensor(out=ygc[:, g, :], in0=yb3[:, 0, :],
                                            in1=yb3[:, 1, :], op=A.mult)
                        sV.tag(r)
                    _emit("vector", p3_y_v)

                for oc in range(NG):
                    def p3_s_t(t, oc=oc, sv_n=sV.n, ss_n=sS.n):
                        t.wait_ge(sV.h, sv_n)
                        if oc >= 1:
                            t.wait_ge(sS.h, ss_n)
                        for g in range(NG):
                            r = t.matmul(pbs[:], lhsT=w_out_sb[:, g, oc * 128:(oc + 1) * 128],
                                         rhs=ygc[:, g, :], start=(g == 0), stop=(g == NG - 1))
                        sT.tag(r)
                    _emit("tensor", p3_s_t)

                    def p3_s_s(s, oc=oc, st_n=sT.n, so_n=sO.n):
                        s.wait_ge(sT.h, st_n)
                        if oc == 0:
                            s.wait_ge(sO.h, so_n)
                        sS.tag(s.activation(out=spb[:, oc, :], in_=pbs[:], func=AF.Copy))
                    _emit("scalar", p3_s_s)

                def p3_st(sy, blk=blk, t0=t0, ss_n=sS.n):
                    sy.wait_ge(sS.h, ss_n)
                    sO.inc(sy.dma_start(out=cc_s_in[:, :, t0:t0 + TB], in_=spb[:]), 16)
                block.sync(p3_st)

            def p3_cc(g):
                sO.wait(g)
                sG.inc(g.collective_compute("AllGather", A.bypass, replica_groups=GROUPS_ALL,
                                            ins=[cc_s_in[:]], outs=[cc_s_out[:]]), 1)
            block.gpsimd(p3_cc)

            def p3_restage(sc):
                sG.wait(sc)
                p_ = pid(sc)
                role_ = _rv(binary_op(p_, make_scalar_value(1), A.bitwise_and), 1)
                hoff = _rv(binary_op(role_, make_scalar_value(HS), A.mult), HS)
                pbase = _rv(binary_op(_rv(binary_op(p_, make_scalar_value(1), A.bitwise_xor), 7),
                                      make_scalar_value(128), A.mult), 7 * 128)
                sD.inc(sc.dma_start(out=s_my[:], in_=cc_s_in[:, :, ds(hoff, HS)]), 16)
                sD.inc(sc.dma_start(out=s_pt[:], in_=cc_s_out[ds(pbase, 128), :, ds(hoff, HS)]), 16)
            block.scalar(p3_restage)

        phase_barrier()
        dbg_out("x1", (128, 16, D))
        dbg_out("xn1", (128, 16, D))
        es4 = ExitStack()
        def sb4(name, shape, dt=F32):
            return es4.enter_context(nc.sbuf_tensor(name, list(shape), dt))
        wo_sb = sb4("wo_sb", (128, NG, D), BF16)
        xnob = sb4("xnob", (128, NG, TB), BF16)
        ao8a = sb4("ao8a", (128, 4, TB), BF16)
        ao8b = sb4("ao8b", (128, 4, TB), BF16)
        smine = sb4("smine", (128, NG, TB), BF16)
        spart = sb4("spart", (128, NG, TB), BF16)
        hb4 = sb4("hb4", (128, NG, TB))
        rn2 = sb4("rn2", (128, NG, TB), BF16)
        sc4 = sb4("sc4", (128, 2, TB))
        strow = sb4("strow", (1, TB))
        lg4 = sb4("lg4", (128, 4, 8))
        cmb = sb4("cmb", (128, 4, 8))
        cmbw = sb4("cmbw", (128, 4, 4))
        combr = sb4("combr", (8, TB), BF16)
        cbsb = sb4("cbsb", (128, TB), BF16)
        ew = sb4("ew", (128, 2, NG, D), BF16)
        macc = sb4("macc", (128, NG, TB))
        gs4 = sb4("gs4", (128, NG, TB), BF16)
        sg4 = sb4("sg4", (128, NG, TB), BF16)
        x0t = sb4("x0t", (128, 4, D))
        x1t = sb4("x1t", (128, D))
        xn1t = sb4("xn1t", (128, D), BF16)
        st4 = sb4("st4", (128, 8))
        ph0 = es4.enter_context(nc.psum_tensor("ph0", [128, TB], F32))
        ph1 = es4.enter_context(nc.psum_tensor("ph1", [128, TB], F32))
        pst4 = es4.enter_context(nc.psum_tensor("pst4", [128, TB], F32))
        ppk4 = es4.enter_context(nc.psum_tensor("ppk4", [128, 1024], BF16))
        plg4 = es4.enter_context(nc.psum_tensor("plg4", [128, 64], F32))
        pct4 = es4.enter_context(nc.psum_tensor("pct4", [16, 128], F32))
        with es4:
            def p4_w(g):
                sWg.inc(g.dma_start(out=wo_sb[:], in_=woT.rearrange("(g p) c -> p g c", p=128)), 16)
            block.gpsimd(p4_w)

            for ob in range(OBLK):
                ot = ob * TB

                def p4_ld(sy, ob=ob, ot=ot, sv_n=sV.n, ss_n=sS.n, st_n=sT.n, sd_n=sD.n, so_n=sO.n):
                    sy.wait_ge(sV.h, sv_n)
                    sy.wait_ge(sS.h, ss_n)
                    sy.wait_ge(sT.h, st_n)
                    sy.wait_ge(sD.h, sd_n)
                    sy.wait_ge(sO.h, so_n)
                    sA.inc(sy.dma_start(out=xnob[:], in_=cc_xn_in.rearrange(
                        "(g p) t -> p g t", p=128)[:, :, ot:ot + TB]), 16)
                    sA.inc(sy.dma_start(out=ao8a[:], in_=ao_my[:, :, ot:ot + TB]), 16)
                    sA.inc(sy.dma_start(out=ao8b[:], in_=ao_pt[:, :, ot:ot + TB]), 16)
                    sA.inc(sy.dma_start(out=smine[:], in_=s_my[:, :, ot:ot + TB]), 16)
                    sA.inc(sy.dma_start(out=spart[:], in_=s_pt[:, :, ot:ot + TB]), 16)
                    sA.inc(sy.dma_start(out=x0t[:], in_=x_half.rearrange(
                        "(n p) d -> p n d", p=128)[:, ob * 4:(ob + 1) * 4, :]), 16)
                block.sync(p4_ld)

                for g in range(NG):
                    ph = ph0 if g % 2 == 0 else ph1
                    def p4_h_t(t, g=g, ph=ph, sa_n=sA.n, swg=sWg.n, sv_n=sV.n):
                        t.wait_ge(sA.h, sa_n)
                        t.wait_ge(sWg.h, swg)
                        if g >= 2:
                            t.wait_ge(sV.h, sv_n)
                        for j in range(NG):
                            rhs_ = ao8a[:, j, :] if j < 4 else ao8b[:, j - 4, :]
                            r = t.matmul(ph[:], lhsT=wo_sb[:, j, g * 128:(g + 1) * 128],
                                         rhs=rhs_, start=(j == 0), stop=(j == NG - 1))
                        sT.tag(r)
                    _emit("tensor", p4_h_t)

                    def p4_h_v(v, g=g, ph=ph, st_n=sT.n):
                        v.wait_ge(sT.h, st_n)
                        r = v.tensor_tensor(out=hb4[:, g, :], in0=ph[:], in1=xnob[:, g, :], op=A.add)
                        sV.tag(r)
                    _emit("vector", p4_h_v)

                for g in range(NG):
                    def p4_sq_s(s, g=g, sv_n=sV.n, st_n=sT.n):
                        s.wait_ge(sV.h, sv_n)
                        if g >= 2:
                            s.wait_ge(sT.h, st_n)
                        sS.tag(s.activation(out=sc4[:, g % 2, :], in_=hb4[:, g, :], func=AF.Square))
                    _emit("scalar", p4_sq_s)

                    def p4_st_tg(t, g=g, ss_n=sS.n):
                        t.wait_ge(sS.h, ss_n)
                        r = t.matmul(pst4[0:1, :], lhsT=onesf[:, 0:1], rhs=sc4[:, g % 2, :],
                                     start=(g == 0), stop=(g == NG - 1))
                        sT.tag(r)
                    _emit("tensor", p4_st_tg)

                def p4_st_v(v, st_n=sT.n):
                    v.wait_ge(sT.h, st_n)
                    r = v.tensor_scalar(out=strow[0:1, :], in0=pst4[0:1, :],
                                        scalar1=1.0 / D, scalar2=EPS_RMS, op0=A.mult, op1=A.add)
                    sV.tag(r)
                _emit("vector", p4_st_v)

                def p4_st_s(s, sv_n=sV.n):
                    s.wait_ge(sV.h, sv_n)
                    sS.tag(s.activation(out=strow[0:1, :], in_=strow[0:1, :], func=AF.Sqrt))
                _emit("scalar", p4_st_s)

                def p4_st_v2(v, ss_n=sS.n):
                    v.wait_ge(sS.h, ss_n)
                    sV.tag(v.reciprocal(out=strow[0:1, :], in_=strow[0:1, :]))
                _emit("vector", p4_st_v2)

                def p4_r2_t(t, sv_n=sV.n):
                    t.wait_ge(sV.h, sv_n)
                    r = t.matmul(pst4[:], lhsT=onesf[0:1, :], rhs=strow[0:1, :],
                                 start=True, stop=True)
                    sT.tag(r)
                _emit("tensor", p4_r2_t)

                def p4_rn2_v(v, st_n=sT.n):
                    v.wait_ge(sT.h, st_n)
                    for g in range(NG):
                        r = v.scalar_tensor_tensor(out=rn2[:, g, :], in0=hb4[:, g, :],
                                                   scalar=fnw_sb[:, g:g + 1], in1=pst4[:],
                                                   op0=A.mult, op1=A.mult)
                    sV.tag(r)
                _emit("vector", p4_rn2_v)

                for sub in range(4):
                    def p4_lg_t(t, sub=sub, sv_n=sV.n):
                        t.wait_ge(sV.h, sv_n)
                        for g in range(NG):
                            r = t.matmul(plg4[:, sub * 8:(sub + 1) * 8],
                                         lhsT=rn2[:, g, sub * 128:(sub + 1) * 128],
                                         rhs=gw_sb[:, g, :], start=(g == 0), stop=(g == NG - 1))
                        sT.tag(r)
                    _emit("tensor", p4_lg_t)

                def p4_lg_v(v, st_n=sT.n):
                    v.wait_ge(sT.h, st_n)
                    for sub in range(4):
                        r = v.tensor_copy(out=lg4[:, sub, :], in_=plg4[:, sub * 8:(sub + 1) * 8])
                    sV.edge(v, r)
                    r = v.tensor_reduce(out=cmbw[:, :, 0:1], in_=lg4[:], axis=AX.X, op=A.max)
                    sV.edge(v, r)
                    r = v.tensor_tensor(out=cmb[:], in0=lg4[:],
                                        in1=cmbw[:, :, 0:1].broadcast_to((128, 4, 8)), op=A.is_equal)
                    sV.edge(v, r)
                    r = v.scalar_tensor_tensor(out=sc4[:, 0, 0:32].rearrange("p (a b) -> p a b", b=8),
                                               in0=cmb[:], scalar=-1e30, in1=lg4[:],
                                               op0=A.mult, op1=A.add)
                    sV.edge(v, r)
                    r = v.tensor_reduce(out=cmbw[:, :, 1:2],
                                        in_=sc4[:, 0, 0:32].rearrange("p (a b) -> p a b", b=8),
                                        axis=AX.X, op=A.max)
                    sV.edge(v, r)
                    r = v.tensor_tensor(out=cmbw[:, :, 2:3], in0=cmbw[:, :, 0:1],
                                        in1=cmbw[:, :, 1:2], op=A.subtract)
                    sV.tag(r)
                _emit("vector", p4_lg_v)

                def p4_sig_s(s, sv_n=sV.n):
                    s.wait_ge(sV.h, sv_n)
                    sS.tag(s.activation(out=cmbw[:, :, 2:3], in_=cmbw[:, :, 2:3], func=AF.Sigmoid))
                _emit("scalar", p4_sig_s)

                def p4_cmb_v(v, ss_n=sS.n):
                    v.wait_ge(sS.h, ss_n)
                    r = v.tensor_scalar(out=cmbw[:, :, 3:4], in0=cmbw[:, :, 2:3],
                                        scalar1=-1.0, scalar2=1.0, op0=A.mult, op1=A.add)
                    sV.edge(v, r)
                    r = v.tensor_tensor(out=sc4[:, 1, 0:32].rearrange("p (a b) -> p a b", b=8),
                                        in0=lg4[:], in1=cmbw[:, :, 1:2].broadcast_to((128, 4, 8)),
                                        op=A.is_equal)
                    sV.edge(v, r)
                    r = v.tensor_tensor(out=cmb[:], in0=cmb[:],
                                        in1=cmbw[:, :, 2:3].broadcast_to((128, 4, 8)), op=A.mult)
                    sV.edge(v, r)
                    r = v.tensor_tensor(out=sc4[:, 1, 0:32].rearrange("p (a b) -> p a b", b=8),
                                        in0=sc4[:, 1, 0:32].rearrange("p (a b) -> p a b", b=8),
                                        in1=cmbw[:, :, 3:4].broadcast_to((128, 4, 8)), op=A.mult)
                    sV.edge(v, r)
                    r = v.tensor_tensor(out=cmb[:], in0=cmb[:],
                                        in1=sc4[:, 1, 0:32].rearrange("p (a b) -> p a b", b=8), op=A.add)
                    sV.tag(r)
                _emit("vector", p4_cmb_v)

                for sub in range(4):
                    def p4_ct_t(t, sub=sub, sv_n=sV.n):
                        t.wait_ge(sV.h, sv_n)
                        sT.tag(t.transpose(pct4[0:8, :], cmb[:, sub, :], identf[:]))
                    _emit("tensor", p4_ct_t)

                    def p4_ct_v(v, sub=sub, st_n=sT.n):
                        v.wait_ge(sT.h, st_n)
                        sV.tag(v.tensor_copy(out=combr[:, sub * 128:(sub + 1) * 128],
                                             in_=pct4[0:8, :]))
                    _emit("vector", p4_ct_v)

                for e in range(8):
                    def p4_ew(g2, e=e, ob=ob, st_n=sT.n):
                        g2.wait_ge(sT.h, st_n)
                        sWg.inc(g2.dma_start(out=ew[:, e % 2, :, :], in_=expert_wT.rearrange(
                            "e (g p) c -> e p g c", p=128)[e]), 16)
                    block.gpsimd(p4_ew)

                    def p4_cb_t(t, e=e, sv_n=sV.n, ss_n=sS.n):
                        t.wait_ge(sV.h, sv_n)
                        if e >= 1:
                            t.wait_ge(sS.h, ss_n)
                        r = t.matmul(pst4[:], lhsT=hot8b[:, e * 128:(e + 1) * 128], rhs=combr[:],
                                     start=True, stop=True)
                        sT.tag(r)
                    _emit("tensor", p4_cb_t)

                    def p4_cb_s(s, e=e, st_n=sT.n):
                        s.wait_ge(sT.h, st_n)
                        sS.tag(s.activation(out=cbsb[:], in_=pst4[:], func=AF.Copy))
                    _emit("scalar", p4_cb_s)

                    for oc in range(NG):
                        ph = ph0 if oc % 2 == 0 else ph1
                        def p4_e_t(t, e=e, oc=oc, ph=ph, swg=sWg.n, sv_n=sV.n):
                            t.wait_ge(sWg.h, swg)
                            if e + oc > 0:
                                t.wait_ge(sV.h, sv_n)
                            for g in range(NG):
                                r = t.matmul(ph[:], lhsT=ew[:, e % 2, g, oc * 128:(oc + 1) * 128],
                                             rhs=rn2[:, g, :], start=(g == 0), stop=(g == NG - 1))
                            sT.tag(r)
                        _emit("tensor", p4_e_t)

                        def p4_e_v(v, e=e, oc=oc, ph=ph, st_n=sT.n, ss_n=sS.n):
                            v.wait_ge(sT.h, st_n)
                            v.wait_ge(sS.h, ss_n)
                            if e == 0:
                                r = v.tensor_tensor(out=macc[:, oc, :], in0=ph[:], in1=cbsb[:], op=A.mult)
                            else:
                                r = v.tensor_tensor(out=sc4[:, 0, :], in0=ph[:], in1=cbsb[:], op=A.mult)
                                sV.edge(v, r)
                                r = v.tensor_tensor(out=macc[:, oc, :], in0=macc[:, oc, :],
                                                    in1=sc4[:, 0, :], op=A.add)
                            sV.tag(r)
                        _emit("vector", p4_e_v)

                def p4_out_v(v):
                    for g in range(NG):
                        r = v.tensor_tensor(out=macc[:, g, :], in0=macc[:, g, :], in1=hb4[:, g, :], op=A.add)
                    sV.tag(r)
                _emit("vector", p4_out_v)

                def p4_gs_s(s, sv_n=sV.n):
                    s.wait_ge(sV.h, sv_n)
                    for g in range(NG):
                        r = s.activation(out=gs4[:, g, :], in_=macc[:, g, :], func=AF.Silu)
                    sS.tag(r)
                _emit("scalar", p4_gs_s)

                def p4_sg_v(v, ss_n=sS.n, sa_n=sA.n):
                    v.wait_ge(sS.h, ss_n)
                    v.wait_ge(sA.h, sa_n)
                    for g in range(NG):
                        r = v.tensor_tensor(out=macc[:, g, :], in0=smine[:, g, :], in1=spart[:, g, :], op=A.add)
                    sV.edge(v, r)
                    for g in range(NG):
                        r = v.tensor_tensor(out=sg4[:, g, :], in0=macc[:, g, :], in1=gs4[:, g, :], op=A.mult)
                    sV.tag(r)
                _emit("vector", p4_sg_v)

                for sub in range(4):
                    def p4_tr_t(t, sub=sub, sv_n=sV.n, ss_n=sS.n):
                        t.wait_ge(sV.h, sv_n)
                        if sub > 0:
                            t.wait_ge(sS.h, ss_n)
                        for g in range(NG):
                            r = t.transpose(ppk4[:, g * 128:(g + 1) * 128],
                                            sg4[:, g, sub * 128:(sub + 1) * 128], ident[:])
                        sT.tag(r)
                    _emit("tensor", p4_tr_t)

                    def p4_x1_v(v, sub=sub, ob=ob, st_n=sT.n, so_n=sO.n):
                        v.wait_ge(sT.h, st_n)
                        v.wait_ge(sO.h, so_n)
                        r = v.tensor_tensor(out=x1t[:], in0=x0t[:, sub, :], in1=ppk4[:], op=A.add)
                        sV.edge(v, r)
                        r = v.reduce_sum(out=st4[:, 0:1], in_=x1t[:], axis=AX.X)
                        sV.tag(r)
                    _emit("vector", p4_x1_v)

                    def p4_ln_s(s, sub=sub, sv_n=sV.n):
                        s.wait_ge(sV.h, sv_n)
                        sS.tag(s.activation(out=xn1t[:], in_=x1t[:], func=AF.Square,
                                            accum_out=st4[:, 1:2]))
                    _emit("scalar", p4_ln_s)

                    def p4_ln_v(v, sub=sub, ss_n=sS.n):
                        v.wait_ge(sS.h, ss_n)
                        r = v.tensor_scalar(out=st4[:, 2:3], in0=st4[:, 0:1], scalar1=1.0 / D, scalar2=None, op0=A.mult)
                        sV.edge(v, r)
                        r = v.tensor_scalar(out=st4[:, 3:4], in0=st4[:, 1:2], scalar1=1.0 / D, scalar2=None, op0=A.mult)
                        sV.edge(v, r)
                        r = v.tensor_tensor(out=st4[:, 4:5], in0=st4[:, 2:3], in1=st4[:, 2:3], op=A.mult)
                        sV.edge(v, r)
                        r = v.tensor_tensor(out=st4[:, 4:5], in0=st4[:, 3:4], in1=st4[:, 4:5], op=A.subtract)
                        sV.edge(v, r)
                        r = v.tensor_scalar(out=st4[:, 4:5], in0=st4[:, 4:5], scalar1=EPS_LN, scalar2=None, op0=A.add)
                        sV.tag(r)
                    _emit("vector", p4_ln_v)

                    def p4_ln_s2(s, sv_n=sV.n):
                        s.wait_ge(sV.h, sv_n)
                        sS.tag(s.activation(out=st4[:, 5:6], in_=st4[:, 4:5], func=AF.Sqrt))
                    _emit("scalar", p4_ln_s2)

                    def p4_ln_v2(v, ss_n=sS.n):
                        v.wait_ge(sS.h, ss_n)
                        r = v.reciprocal(out=st4[:, 6:7], in_=st4[:, 5:6])
                        sV.edge(v, r)
                        r = v.tensor_scalar(out=xn1t[:], in0=x1t[:], scalar1=st4[:, 2:3],
                                            scalar2=st4[:, 6:7], op0=A.subtract, op1=A.mult)
                        sV.edge(v, r)
                        r = v.tensor_scalar(out=x0t[:, sub, :], in0=x1t[:], scalar1=st4[:, 2:3],
                                            scalar2=st4[:, 6:7], op0=A.subtract, op1=A.mult)
                        sV.tag(r)
                    _emit("vector", p4_ln_v2)

                    def p4_xn1_t(t, sub=sub, ob=ob, sv_n=sV.n, ss_n=sS.n):
                        t.wait_ge(sV.h, sv_n)
                        for g in range(NG):
                            r = t.transpose(ppk4[:, g * 128:(g + 1) * 128],
                                            xn1t[:, g * 128:(g + 1) * 128], ident[:])
                        sT.tag(r)
                    _emit("tensor", p4_xn1_t)

                    def p4_xn1_s(s, sub=sub, ob=ob, st_n=sT.n, so_n=sO.n):
                        s.wait_ge(sT.h, st_n)
                        s.wait_ge(sO.h, so_n)
                        r = s.activation(out=gs4[:, 0:2, :].rearrange("p g t -> p (g t)"),
                                         in_=ppk4[:], func=AF.Copy)
                        sS.tag(r)
                    _emit("scalar", p4_xn1_s)

                    def p4_st_dma(sy, sub=sub, ob=ob, ss_n=sS.n, sv_n=sV.n):
                        sy.wait_ge(sS.h, ss_n)
                        sy.wait_ge(sV.h, sv_n)
                        tt = ob * 4 + sub
                        if "x1" in debug:
                            sO.inc(sy.dma_start(out=dbg["x1"][:, tt, :], in_=x1t[:]), 16)
                        if "xn1" in debug:
                            sO.inc(sy.dma_start(out=dbg["xn1"][:, tt, :], in_=x0t[:, sub, :]), 16)
                        sO.inc(sy.dma_start(out=x1_spill[:, tt, :], in_=x1t[:]), 16)
                        sO.inc(sy.dma_start(out=xn1f_spill[:, tt, :], in_=x0t[:, sub, :]), 16)
                        sO.inc(sy.dma_start(out=cc_xn1_in.rearrange("(g p) t -> p g t", p=128)[
                            :, :, ob * TB + sub * 128:ob * TB + (sub + 1) * 128],
                            in_=gs4[:, 0:2, :].rearrange("p g t -> p (g t)").rearrange(
                                "p (g c) -> p g c", c=128)), 16)
                    block.sync(p4_st_dma)

            def p4_cc(g):
                sO.wait(g)
                sG.inc(g.collective_compute("AllGather", A.bypass, replica_groups=GROUPS_ALL,
                                            ins=[cc_xn1_in[:]], outs=[cc_xn1_out[:]]), 1)
            block.gpsimd(p4_cc)

        phase_barrier()
        def emit_quant(get_bufs, in_ap, out_ap, F, sv0=None):
            B = get_bufs
            nb = F // 16

            def qv1(v, sv0=sv0):
                if sv0 is not None:
                    v.wait_ge(sV.h, sv0)
                r = v.tensor_reduce(out=B["bm"], in_=in_ap.rearrange("p (b s) -> p b s", s=16),
                                    axis=AX.X, op=A.max, apply_absolute_value=True)
                sV.edge(v, r)
                r = v.reciprocal(out=B["bmr"], in_=B["bm"])
                sV.edge(v, r)
                r = v.tensor_scalar(out=B["bmr"], in0=B["bmr"], scalar1=6.0, scalar2=1e12,
                                    op0=A.mult, op1=A.min)
                r2 = v.tensor_scalar(out=B["bmd"], in0=B["bm"], scalar1=1.0 / 6.0, scalar2=1e-12,
                                     op0=A.mult, op1=A.max)
                sV.edge(v, r2)
                r = v.tensor_tensor(out=B["xs"].rearrange("p (b s) -> p b s", s=16),
                                    in0=in_ap.rearrange("p (b s) -> p b s", s=16),
                                    in1=B["bmr"].unsqueeze(2).broadcast_to((128, nb, 16)),
                                    op=A.mult)
                sV.tag(r)
            _emit("vector", qv1)

            def qs1(s, sv_n=sV.n):
                s.wait_ge(sV.h, sv_n)
                r = s.activation(out=B["a2"], in_=B["xs"], func=AF.Abs, scale=2.0)
                sS.edge(s, r)
                r = s.activation(out=B["sgn"], in_=B["xs"], func=AF.Sign)
                sS.tag(r)
            _emit("scalar", qs1)

            def qv2(v, ss_n=sS.n):
                v.wait_ge(sS.h, ss_n)
                r = v.tensor_scalar(out=B["q1"], in0=B["a2"], scalar1=0.5, scalar2=2.0, op0=A.mult, op1=A.add)
                r = v.tensor_scalar(out=B["q2"], in0=B["a2"], scalar1=0.25, scalar2=4.0, op0=A.mult, op1=A.add)
                sV.edge(v, r)
                r = v.tensor_tensor(out=B["q1"], in0=B["a2"], in1=B["q1"], op=A.min)
                sV.edge(v, r)
                r = v.tensor_tensor(out=B["q2"], in0=B["q1"], in1=B["q2"], op=A.min)
                sV.edge(v, r)
                r = v.tensor_scalar(out=B["li"], in0=B["q2"], scalar1=1.0, scalar2=None, op0=A.mult)
                sV.edge(v, r)
                r = v.tensor_copy(out=B["xs"], in_=B["li"])
                sV.edge(v, r)
                r = v.tensor_tensor(out=B["sgn2"], in0=B["q2"], in1=B["xs"], op=A.subtract)
                sV.edge(v, r)
                r = v.tensor_scalar(out=B["sgn2"], in0=B["sgn2"], scalar1=0.5, scalar2=None, op0=A.is_ge)
                sV.edge(v, r)
                r = v.tensor_tensor(out=B["xs"], in0=B["xs"], in1=B["sgn2"], op=A.add)
                sV.edge(v, r)
                r = v.tensor_scalar(out=B["q1"], in0=B["xs"], scalar1=0.5, scalar2=None, op0=A.mult)
                r = v.tensor_scalar(out=B["q2"], in0=B["xs"], scalar1=1.0, scalar2=-2.0, op0=A.mult, op1=A.add)
                r = v.tensor_scalar(out=B["a2"], in0=B["xs"], scalar1=2.0, scalar2=-8.0, op0=A.mult, op1=A.add)
                sV.edge(v, r)
                r = v.tensor_tensor(out=B["q1"], in0=B["q1"], in1=B["q2"], op=A.max)
                sV.edge(v, r)
                r = v.tensor_tensor(out=B["q1"], in0=B["q1"], in1=B["a2"], op=A.max)
                sV.edge(v, r)
                r = v.tensor_tensor(out=B["q1"], in0=B["q1"], in1=B["sgn"], op=A.mult)
                sV.edge(v, r)
                r = v.tensor_tensor(out=out_ap.rearrange("p (b s) -> p b s", s=16),
                                    in0=B["q1"].rearrange("p (b s) -> p b s", s=16),
                                    in1=B["bmd"].unsqueeze(2).broadcast_to((128, nb, 16)),
                                    op=A.mult)
                sV.tag(r)
            _emit("vector", qv2)

        es6 = ExitStack()
        def sb6(name, shape, dt=F32):
            return es6.enter_context(nc.sbuf_tensor(name, list(shape), dt))
        qw_in = sb6("qw_in", (128, FFI))
        qw_xs = sb6("qw_xs", (128, FFI))
        qw_a2 = sb6("qw_a2", (128, FFI))
        qw_q1 = sb6("qw_q1", (128, FFI))
        qw_q2 = sb6("qw_q2", (128, FFI))
        qw_sgn = sb6("qw_sgn", (128, FFI))
        qw_sg2 = sb6("qw_sg2", (128, FFI))
        qw_li = sb6("qw_li", (128, FFI), I32)
        qw_bm = sb6("qw_bm", (128, FFI // 16))
        qw_bmr = sb6("qw_bmr", (128, FFI // 16))
        qw_bmd = sb6("qw_bmd", (128, FFI // 16))
        qw_out = sb6("qw_out", (128, FFI), BF16)
        qw_tr = sb6("qw_tr", (128, FFI), BF16)
        pq6 = es6.enter_context(nc.psum_tensor("pq6", [128, 1024], BF16))
        pq6b = es6.enter_context(nc.psum_tensor("pq6b", [128, 1024], BF16))
        with es6:
            def qbufs(F):
                return {"xs": qw_xs[:, 0:F], "a2": qw_a2[:, 0:F], "q1": qw_q1[:, 0:F],
                        "q2": qw_q2[:, 0:F], "sgn": qw_sgn[:, 0:F], "sgn2": qw_sg2[:, 0:F],
                        "li": qw_li[:, 0:F],
                        "bm": qw_bm[:, 0:F // 16], "bmr": qw_bmr[:, 0:F // 16],
                        "bmd": qw_bmd[:, 0:F // 16]}

            for wsrc, wdst in ((l1_wg, wgq_spill), (l1_wu, wuq_spill)):
                for it in range(NFI):
                    def q6_ld(sy, wsrc=wsrc, it=it, sv_n=sV.n):
                        sy.wait_ge(sV.h, sv_n)
                        sA.inc(sy.dma_start(out=qw_in[:, 0:D], in_=wsrc[it * 128:(it + 1) * 128, :]), 16)
                    block.sync(q6_ld)

                    def q6_w(v, sa_n=sA.n):
                        v.wait_ge(sA.h, sa_n)
                        v.engine_nop() if hasattr(v, "engine_nop") else None
                        sV.bump(v)
                    _emit("vector", q6_w)
                    emit_quant(qbufs(D), qw_in[:, 0:D], qw_out[:, 0:D], D)

                    def q6_t(t, it=it, sv_n=sV.n, ss_n=sS.n):
                        t.wait_ge(sV.h, sv_n)
                        t.wait_ge(sS.h, ss_n)
                        for g in range(NG):
                            r = t.transpose(pq6[:, g * 128:(g + 1) * 128],
                                            qw_out[:, g * 128:(g + 1) * 128], ident[:])
                        sT.tag(r)
                    _emit("tensor", q6_t)

                    def q6_s(s, st_n=sT.n, so_n=sO.n):
                        s.wait_ge(sT.h, st_n)
                        s.wait_ge(sO.h, so_n)
                        r = s.activation(out=qw_tr[:, 0:1024], in_=pq6[:], func=AF.Copy)
                        sS.tag(r)
                    _emit("scalar", q6_s)

                    def q6_st(sy, wdst=wdst, it=it, ss_n=sS.n):
                        sy.wait_ge(sS.h, ss_n)
                        sO.inc(sy.dma_start(
                            out=wdst.rearrange("p g c -> p g c")[:, :, it * 128:(it + 1) * 128],
                            in_=qw_tr[:, 0:1024].rearrange("p (g c) -> p g c", c=128)), 16)
                    block.sync(q6_st)

            for it in range(NG):
                def q6_ld2(sy, it=it, sv_n=sV.n):
                    sy.wait_ge(sV.h, sv_n)
                    sA.inc(sy.dma_start(out=qw_in[:], in_=l1_wd[it]), 16)
                block.sync(q6_ld2)

                def q6_w2(v, sa_n=sA.n):
                    v.wait_ge(sA.h, sa_n)
                    sV.bump(v)
                _emit("vector", q6_w2)
                emit_quant(qbufs(FFI), qw_in[:], qw_out[:], FFI)

                for half in range(2):
                    def q6_t2(t, it=it, half=half, sv_n=sV.n, ss_n=sS.n):
                        t.wait_ge(sV.h, sv_n)
                        t.wait_ge(sS.h, ss_n)
                        nn = 11
                        for g in range(nn):
                            r = t.transpose(pq6[:, (g % 8) * 128:(g % 8 + 1) * 128] if False else
                                            pq6[:, g * 128:(g + 1) * 128] if g < 8 else pq6b[:, (g - 8) * 128:(g - 7) * 128],
                                            qw_out[:, (half * 11 + g) * 128:(half * 11 + g + 1) * 128], ident[:])
                        sT.tag(r)
                    _emit("tensor", q6_t2)

                    def q6_s2(s, it=it, half=half, st_n=sT.n, so_n=sO.n):
                        s.wait_ge(sT.h, st_n)
                        s.wait_ge(sO.h, so_n)
                        s.activation(out=qw_tr[:, 0:1024], in_=pq6[:], func=AF.Copy)
                        r = s.activation(out=qw_tr[:, 1024:1408], in_=pq6b[:, 0:384], func=AF.Copy)
                        sS.tag(r)
                    _emit("scalar", q6_s2)

                    def q6_st2(sy, it=it, half=half, ss_n=sS.n):
                        sy.wait_ge(sS.h, ss_n)
                        sO.inc(sy.dma_start(
                            out=wdq_spill[:, half * 11:(half + 1) * 11, it * 128:(it + 1) * 128],
                            in_=qw_tr[:, 0:1408].rearrange("p (g c) -> p g c", c=128)), 16)
                    block.sync(q6_st2)

        phase_barrier()
        es7 = ExitStack()
        def sb7(name, shape, dt=F32):
            return es7.enter_context(nc.sbuf_tensor(name, list(shape), dt))
        l1q_sb = sb7("l1q_sb", (128, NG, 512), BF16)
        l1k_sb = sb7("l1k_sb", (128, NG, 512), BF16)
        l1v_sb = sb7("l1v_sb", (128, NG, 512), BF16)
        xn1b = sb7("xn1b", (128, NG, TB), BF16)
        k1 = sb7("k1", (128, 4, S), BF16)
        v1 = sb7("v1", (128, 32, 520), BF16)
        qb7 = sb7("qb7", (128, 4, TB), BF16)
        atb7 = sb7("atb7", (128, 2, 2, TB), BF16)
        recd7 = sb7("recd7", (128, 2, TB))
        ib7 = sb7("ib7", (64, 2, TB))
        nb7 = sb7("nb7", (64, 2, TB), BF16)
        aob7 = sb7("aob7", (128, 4, TB), BF16)
        psA7 = es7.enter_context(nc.psum_tensor("psA7", [128, TB], F32))
        psB7 = es7.enter_context(nc.psum_tensor("psB7", [128, TB], F32))
        psC7 = es7.enter_context(nc.psum_tensor("psC7", [128, TB], F32))
        psD7 = es7.enter_context(nc.psum_tensor("psD7", [128, TB], F32))
        av07 = es7.enter_context(nc.psum_tensor("av07", [128, TB], F32))
        av17 = es7.enter_context(nc.psum_tensor("av17", [128, TB], F32))
        psM7 = es7.enter_context(nc.psum_tensor("psM7", [128, TB], F32))
        with es7:
            def p7_init(g):
                sWg.inc(g.dma_start(out=l1q_sb[:], in_=l1_wq.rearrange("(g p) c -> p g c", p=128)), 16)
                sWg.inc(g.dma_start(out=l1k_sb[:], in_=l1_wk.rearrange("(g p) c -> p g c", p=128)), 16)
                sWg.inc(g.dma_start(out=l1v_sb[:], in_=l1_wv.rearrange("(g p) c -> p g c", p=128)), 16)
                for ht in range(4):
                    g.memset(v1[:, :, ht * 130 + 64:ht * 130 + 65], 1.0)
                    r = g.memset(v1[:, :, ht * 130 + 129:ht * 130 + 130], 1.0)
                p_ = pid(g)
                b0 = _rv(binary_op(_rv(binary_op(p_, make_scalar_value(~1 & 0xff), A.bitwise_and), 6),
                                   make_scalar_value(D), A.mult), 6 * D)
                b1 = _rv(binary_op(_rv(binary_op(p_, make_scalar_value(1), A.bitwise_or), 7),
                                   make_scalar_value(D), A.mult), 7 * D)
                sWg.inc(g.dma_start(out=xn1_loc[:, 0:HS], in_=cc_xn1_out[ds(b0, D), :]), 16)
                sWg.inc(g.dma_start(out=xn1_loc[:, HS:S], in_=cc_xn1_out[ds(b1, D), :]), 16)
                sG.tag(r)
            block.gpsimd(p7_init)

            for i in range(NBLK):
                t0 = i * TB

                def p7_ld(sy, i=i, t0=t0, sv_n=sV.n, ss_n=sS.n, swg=sWg.n):
                    sy.wait_ge(sV.h, sv_n)
                    sy.wait_ge(sS.h, ss_n)
                    sy.wait_ge(sWg.h, swg)
                    sA.inc(sy.dma_start(out=xn1b[:], in_=xn1_loc.rearrange(
                        "(g p) t -> p g t", p=128)[:, :, t0:t0 + TB]), 16)
                block.sync(p7_ld)

                for hp in range(4):
                    pq = psA7 if hp % 2 == 0 else psB7
                    def p7_q_t(t, hp=hp, pq=pq, sa_n=sA.n, ss_n=sS.n, swg=sWg.n):
                        t.wait_ge(sA.h, sa_n)
                        t.wait_ge(sWg.h, swg)
                        t.wait_ge(sS.h, ss_n)
                        for g in range(NG):
                            r = t.matmul(pq[:], lhsT=l1q_sb[:, g, hp * 128:(hp + 1) * 128],
                                         rhs=xn1b[:, g, :], start=(g == 0), stop=(g == NG - 1))
                        sT.tag(r)
                    _emit("tensor", p7_q_t)

                    def p7_q_s(s, hp=hp, pq=pq, st_n=sT.n):
                        s.wait_ge(sT.h, st_n)
                        sS.tag(s.activation(out=qb7[:, hp, :], in_=pq[:], func=AF.Copy))
                    _emit("scalar", p7_q_s)

                for hp in range(4):
                    pq = psA7 if hp % 2 == 0 else psB7
                    def p7_k_t(t, hp=hp, pq=pq, ss_n=sS.n):
                        t.wait_ge(sS.h, ss_n)
                        for g in range(NG):
                            r = t.matmul(pq[:], lhsT=l1k_sb[:, g, hp * 128:(hp + 1) * 128],
                                         rhs=xn1b[:, g, :], start=(g == 0), stop=(g == NG - 1))
                        sT.tag(r)
                    _emit("tensor", p7_k_t)

                    def p7_k_s(s, hp=hp, pq=pq, t0=t0, st_n=sT.n):
                        s.wait_ge(sT.h, st_n)
                        sS.tag(s.activation(out=k1[:, hp, t0:t0 + TB], in_=pq[:], func=AF.Copy, scale=0.125))
                    _emit("scalar", p7_k_s)

                for sub in range(4):
                    pv = psC7 if sub % 2 == 0 else psD7
                    def p7_v_t(t, i=i, sub=sub, pv=pv, ss_n=sS.n):
                        t.wait_ge(sS.h, ss_n)
                        for hp in range(4):
                            for g in range(NG):
                                r = t.matmul(pv[:, hp * 128:(hp + 1) * 128],
                                             lhsT=xn1b[:, g, sub * 128:(sub + 1) * 128],
                                             rhs=l1v_sb[:, g, hp * 128:(hp + 1) * 128],
                                             start=(g == 0), stop=(g == NG - 1))
                        sT.tag(r)
                    _emit("tensor", p7_v_t)

                    def p7_v_s(s, i=i, sub=sub, pv=pv, st_n=sT.n):
                        s.wait_ge(sT.h, st_n)
                        kt = i * 4 + sub
                        for hp in range(4):
                            s.activation(out=v1[:, kt, hp * 130:hp * 130 + 64],
                                         in_=pv[:, hp * 128:hp * 128 + 64], func=AF.Copy)
                            r = s.activation(out=v1[:, kt, hp * 130 + 65:hp * 130 + 129],
                                             in_=pv[:, hp * 128 + 64:hp * 128 + 128], func=AF.Copy)
                        sS.tag(r)
                    _emit("scalar", p7_v_s)
                kv_done7 = sS.n

                nkt = 4 * i + 4
                for ht in range(4):
                    exp_done = {}
                    msk_done = {}
                    for kt in range(nkt):
                        par = kt % 2
                        scA = psA7 if par == 0 else psC7
                        scB = psB7 if par == 0 else psD7
                        diag = (kt >= 4 * i)

                        def p7_sc(t, i=i, ht=ht, kt=kt, scA=scA, scB=scB,
                                  kv_done7=kv_done7, ss_n=sS.n, exp_done=exp_done):
                            t.wait_ge(sS.h, kv_done7)
                            if kt >= 2:
                                t.wait_ge(sS.h, exp_done[kt - 2])
                            else:
                                t.wait_ge(sS.h, ss_n)
                            r = t.matmul(scA[:], lhsT=k1[0:64, ht, kt * 128:(kt + 1) * 128],
                                         rhs=qb7[0:64, ht, :], start=True, stop=True)
                            r = t.matmul(scB[:], lhsT=k1[64:128, ht, kt * 128:(kt + 1) * 128],
                                         rhs=qb7[64:128, ht, :], start=True, stop=True)
                            sT.tag(r)
                        _emit("tensor", p7_sc)
                        sc_n = sT.n

                        def p7_exp(s, kt=kt, par=par, scA=scA, scB=scB, sc_n=sc_n, exp_done=exp_done):
                            s.wait_ge(sT.h, sc_n)
                            s.activation(out=atb7[:, 0, par, :], in_=scA[:], func=AF.Exp)
                            r = s.activation(out=atb7[:, 1, par, :], in_=scB[:], func=AF.Exp)
                            sS.tag(r)
                            exp_done[kt] = sS.n
                        _emit("scalar", p7_exp)

                        if diag:
                            j = kt - 4 * i

                            def p7_msk(v, kt=kt, par=par, j=j, e_n=sS.n, msk_done=msk_done):
                                v.wait_ge(sS.h, e_n)
                                for hh in range(2):
                                    if j > 0:
                                        v.memset(atb7[:, hh, par, 0:j * 128], 0.0)
                                    r = v.tensor_tensor(out=atb7[:, hh, par, j * 128:(j + 1) * 128],
                                                        in0=atb7[:, hh, par, j * 128:(j + 1) * 128],
                                                        in1=trib[:], op=A.mult)
                                sV.tag(r)
                                msk_done[kt] = sV.n
                            _emit("vector", p7_msk)

                        def p7_av(t, ht=ht, kt=kt, par=par, diag=diag, nkt=nkt,
                                  exp_done=exp_done, msk_done=msk_done, sg_n=sG.n):
                            if kt == 0:
                                t.wait_ge(sG.h, sg_n)
                            t.wait_ge(sS.h, exp_done[kt])
                            if diag:
                                t.wait_ge(sV.h, msk_done[kt])
                            r = t.matmul(av07[0:65, :], lhsT=v1[:, kt, ht * 130:ht * 130 + 65],
                                         rhs=atb7[:, 0, par, :], start=(kt == 0), stop=(kt == nkt - 1))
                            r = t.matmul(av17[0:65, :], lhsT=v1[:, kt, ht * 130 + 65:ht * 130 + 130],
                                         rhs=atb7[:, 1, par, :], start=(kt == 0), stop=(kt == nkt - 1))
                            sT.tag(r)
                        _emit("tensor", p7_av)

                    def p7_ep_v(v, ht=ht, st_n=sT.n, so_n=sO.n):
                        v.wait_ge(sT.h, st_n)
                        if ht == 0:
                            v.wait_ge(sO.h, so_n)
                        v.reciprocal(out=recd7[64:65, 0, :], in_=av07[64:65, :])
                        r = v.reciprocal(out=recd7[64:65, 1, :], in_=av17[64:65, :])
                        sV.tag(r)
                    _emit("vector", p7_ep_v)

                    def p7_ep_t(t, ht=ht, sv_n=sV.n):
                        t.wait_ge(sV.h, sv_n)
                        t.matmul(psM7[0:64, :], lhsT=onesf[64:65, 0:64], rhs=recd7[64:65, 0, :],
                                 start=True, stop=True)
                        r = t.matmul(psC7[0:64, :], lhsT=onesf[64:65, 0:64], rhs=recd7[64:65, 1, :],
                                     start=True, stop=True)
                        sT.tag(r)
                    _emit("tensor", p7_ep_t)

                    def p7_ep_sx(s, ht=ht, st_n=sT.n):
                        s.wait_ge(sT.h, st_n)
                        s.activation(out=ib7[:, 0, :], in_=psM7[0:64, :], func=AF.Copy)
                        r = s.activation(out=ib7[:, 1, :], in_=psC7[0:64, :], func=AF.Copy)
                        sS.tag(r)
                    _emit("scalar", p7_ep_sx)

                    def p7_ep_v2(v, ht=ht, ss_n=sS.n):
                        v.wait_ge(sS.h, ss_n)
                        v.tensor_tensor(out=nb7[:, 0, :], in0=av07[0:64, :], in1=ib7[:, 0, :], op=A.mult)
                        r = v.tensor_tensor(out=nb7[:, 1, :], in0=av17[0:64, :], in1=ib7[:, 1, :], op=A.mult)
                        sV.tag(r)
                    _emit("vector", p7_ep_v2)

                    def p7_ep_t2(t, ht=ht, sv_n=sV.n):
                        t.wait_ge(sV.h, sv_n)
                        t.matmul(psB7[:], lhsT=selb[:, 0:128], rhs=nb7[:, 0, :], start=True, stop=False)
                        r = t.matmul(psB7[:], lhsT=selb[:, 128:256], rhs=nb7[:, 1, :], start=False, stop=True)
                        sT.tag(r)
                    _emit("tensor", p7_ep_t2)

                    def p7_ep_s2(s, ht=ht, st_n=sT.n):
                        s.wait_ge(sT.h, st_n)
                        sS.tag(s.activation(out=aob7[:, ht, :], in_=psB7[:], func=AF.Copy))
                    _emit("scalar", p7_ep_s2)

                def p7_st(sy, i=i, t0=t0, ss_n=sS.n):
                    sy.wait_ge(sS.h, ss_n)
                    sO.inc(sy.dma_start(out=cc_ao1_in[:, :, t0:t0 + TB], in_=aob7[:]), 16)
                block.sync(p7_st)

            def p7_cc(g):
                sO.wait(g)
                sG.inc(g.collective_compute("AllGather", A.bypass, replica_groups=GROUPS_ALL,
                                            ins=[cc_ao1_in[:]], outs=[cc_ao1_out[:]]), 1)
            block.gpsimd(p7_cc)

            def p7_restage(sc):
                sG.wait(sc)
                p_ = pid(sc)
                role_ = _rv(binary_op(p_, make_scalar_value(1), A.bitwise_and), 1)
                hoff = _rv(binary_op(role_, make_scalar_value(HS), A.mult), HS)
                mybase = _rv(binary_op(p_, make_scalar_value(128), A.mult), 7 * 128)
                pbase = _rv(binary_op(_rv(binary_op(p_, make_scalar_value(1), A.bitwise_xor), 7),
                                      make_scalar_value(128), A.mult), 7 * 128)
                sD.inc(sc.dma_start(out=ao1_my[:], in_=cc_ao1_out[ds(mybase, 128), :, ds(hoff, HS)]), 16)
                sD.inc(sc.dma_start(out=ao1_pt[:], in_=cc_ao1_out[ds(pbase, 128), :, ds(hoff, HS)]), 16)
            block.scalar(p7_restage)

        phase_barrier()
        dbg_out("g1", (128, NG, HS), BF16)
        dbg_out("xq", (128, 16, D), BF16)
        es8 = ExitStack()
        def sb8(name, shape, dt=F32):
            return es8.enter_context(nc.sbuf_tensor(name, list(shape), dt))
        l1wo_sb = sb8("l1wo_sb", (128, NG, D), BF16)
        wdq_sb = sb8("wdq_sb", (128, 2, NFI, 128), BF16)
        xin8 = sb8("xin8", (128, D))
        q8_xs = sb8("q8_xs", (128, D))
        q8_a2 = sb8("q8_a2", (128, D))
        q8_q1 = sb8("q8_q1", (128, D))
        q8_q2 = sb8("q8_q2", (128, D))
        q8_sgn = sb8("q8_sgn", (128, D))
        q8_sg2 = sb8("q8_sg2", (128, D))
        q8_li = sb8("q8_li", (128, D), I32)
        q8_bm = sb8("q8_bm", (128, D // 16))
        q8_bmr = sb8("q8_bmr", (128, D // 16))
        q8_bmd = sb8("q8_bmd", (128, D // 16))
        xq8 = sb8("xq8", (128, 4, D), BF16)
        xqf = sb8("xqf", (128, NG, TB), BF16)
        wg8 = sb8("wg8", (128, 2, NG, TB), BF16)
        wu8 = sb8("wu8", (128, 2, NG, TB), BF16)
        h8s = sb8("h8s", (128, TB))
        hq8 = sb8("hq8", (128, TB), BF16)
        hqf = sb8("hqf", (128, NFI, TB), BF16)
        g18 = sb8("g18", (128, NG, TB), BF16)
        ao18 = sb8("ao18", (128, NG, TB), BF16)
        xn1fm8 = sb8("xn1fm8", (128, NG, TB), BF16)
        s1s = sb8("s1s", (128, TB))
        sg18 = sb8("sg18", (128, NG, TB), BF16)
        x1t8 = sb8("x1t8", (128, D))
        x2t8 = sb8("x2t8", (128, D))
        pg8 = es8.enter_context(nc.psum_tensor("pg8", [128, TB], F32))
        pu8 = es8.enter_context(nc.psum_tensor("pu8", [128, TB], F32))
        pt8 = es8.enter_context(nc.psum_tensor("pt8", [128, TB], BF16))
        pd8 = es8.enter_context(nc.psum_tensor("pd8", [128, TB], F32))
        pd8b = es8.enter_context(nc.psum_tensor("pd8b", [128, TB], F32))
        ppk8 = es8.enter_context(nc.psum_tensor("ppk8", [128, 1024], BF16))
        with es8:
            def q8bufs(F):
                return {"xs": q8_xs[:, 0:F], "a2": q8_a2[:, 0:F], "q1": q8_q1[:, 0:F],
                        "q2": q8_q2[:, 0:F], "sgn": q8_sgn[:, 0:F], "sgn2": q8_sg2[:, 0:F],
                        "li": q8_li[:, 0:F],
                        "bm": q8_bm[:, 0:F // 16], "bmr": q8_bmr[:, 0:F // 16],
                        "bmd": q8_bmd[:, 0:F // 16]}

            def p8_init(g):
                sWg.inc(g.dma_start(out=l1wo_sb[:], in_=l1_wo.rearrange("(g p) c -> p g c", p=128)), 16)
            block.gpsimd(p8_init)

            FBLK = [(0, 512), (512, 512), (1024, 512), (1536, 512), (2048, 512), (2560, 256)]
            for ob in range(OBLK):
                ot = ob * TB

                def p8_ld(sy, ob=ob, ot=ot, sv_n=sV.n, ss_n=sS.n, st_n=sT.n, so_n=sO.n):
                    sy.wait_ge(sV.h, sv_n)
                    sy.wait_ge(sS.h, ss_n)
                    sy.wait_ge(sT.h, st_n)
                    sy.wait_ge(sO.h, so_n)
                    sA.inc(sy.dma_start(out=ao18[:, 0:4, :], in_=ao1_my[:, :, ot:ot + TB]), 16)
                    sA.inc(sy.dma_start(out=ao18[:, 4:8, :], in_=ao1_pt[:, :, ot:ot + TB]), 16)
                    sA.inc(sy.dma_start(out=xn1fm8[:], in_=cc_xn1_in.rearrange(
                        "(g p) t -> p g t", p=128)[:, :, ot:ot + TB]), 16)
                block.sync(p8_ld)

                def p8_ldw(g, ob=ob, st_n=sT.n):
                    g.wait_ge(sT.h, st_n)
                    pass
                block.gpsimd(p8_ldw)

                for sub in range(4):
                    def p8_qld(sy, ob=ob, sub=sub, sv_n=sV.n):
                        sy.wait_ge(sV.h, sv_n)
                        sA.inc(sy.dma_start(out=xin8[:], in_=xn1f_spill[:, ob * 4 + sub, :]), 16)
                    block.sync(p8_qld)

                    def p8_qw(v, sub=sub, sa_n=sA.n):
                        v.wait_ge(sA.h, sa_n)
                        sV.bump(v)
                    _emit("vector", p8_qw)
                    emit_quant(q8bufs(D), xin8[:], xq8[:, sub, :], D)

                    if "xq" in debug:
                        def dbg_xq(sy, ob=ob, sub=sub, sv_n=sV.n):
                            sy.wait_ge(sV.h, sv_n)
                            sO.inc(sy.dma_start(out=dbg["xq"][:, ob * 4 + sub, :], in_=xq8[:, sub, :]), 16)
                        block.sync(dbg_xq)

                    def p8_xt(t, sub=sub, sv_n=sV.n, ss_n=sS.n):
                        t.wait_ge(sV.h, sv_n)
                        t.wait_ge(sS.h, ss_n)
                        for g in range(NG):
                            r = t.transpose(ppk8[:, g * 128:(g + 1) * 128],
                                            xq8[:, sub, g * 128:(g + 1) * 128], ident[:])
                        sT.tag(r)
                    _emit("tensor", p8_xt)

                    def p8_xs(s, sub=sub, st_n=sT.n):
                        s.wait_ge(sT.h, st_n)
                        for g in range(NG):
                            r = s.activation(out=xqf[:, g, sub * 128:(sub + 1) * 128],
                                             in_=ppk8[:, g * 128:(g + 1) * 128], func=AF.Copy)
                        sS.tag(r)
                    _emit("scalar", p8_xs)

                for sub in range(4):
                    for fb, (f0, fw) in enumerate(FBLK):
                        def p8_gw(g2, ob=ob, sub=sub, fb=fb, f0=f0, fw=fw, st_n=sT.n):
                            g2.wait_ge(sT.h, st_n)
                            sWg.inc(g2.dma_start(out=wg8[:, fb % 2, :, 0:fw],
                                                 in_=wgq_spill[:, :, f0:f0 + fw]), 16)
                            sWg.inc(g2.dma_start(out=wu8[:, fb % 2, :, 0:fw],
                                                 in_=wuq_spill[:, :, f0:f0 + fw]), 16)
                        block.gpsimd(p8_gw)

                        def p8_g_t(t, sub=sub, fb=fb, f0=f0, fw=fw, swg=sWg.n, ss_n=sS.n, sv_n=sV.n):
                            t.wait_ge(sWg.h, swg)
                            t.wait_ge(sS.h, ss_n)
                            t.wait_ge(sV.h, sv_n)
                            for g in range(NG):
                                t.matmul(pg8[:, 0:fw], lhsT=xqf[:, g, sub * 128:(sub + 1) * 128],
                                         rhs=wg8[:, fb % 2, g, 0:fw], start=(g == 0), stop=(g == NG - 1))
                            for g in range(NG):
                                r = t.matmul(pu8[:, 0:fw], lhsT=xqf[:, g, sub * 128:(sub + 1) * 128],
                                             rhs=wu8[:, fb % 2, g, 0:fw], start=(g == 0), stop=(g == NG - 1))
                            sT.tag(r)
                        _emit("tensor", p8_g_t)

                        def p8_sil(s, fb=fb, fw=fw, st_n=sT.n):
                            s.wait_ge(sT.h, st_n)
                            sS.tag(s.activation(out=h8s[:, 0:fw], in_=pg8[:, 0:fw], func=AF.Silu))
                        _emit("scalar", p8_sil)

                        def p8_hm(v, fb=fb, fw=fw, ss_n=sS.n):
                            v.wait_ge(sS.h, ss_n)
                            r = v.tensor_tensor(out=h8s[:, 0:fw], in0=h8s[:, 0:fw], in1=pu8[:, 0:fw], op=A.mult)
                            sV.tag(r)
                        _emit("vector", p8_hm)
                        emit_quant(q8bufs(fw), h8s[:, 0:fw], hq8[:, 0:fw], fw)

                        nsubt = fw // 128
                        def p8_ht(t, sub=sub, fb=fb, f0=f0, nsubt=nsubt, sv_n=sV.n, ss_n=sS.n):
                            t.wait_ge(sV.h, sv_n)
                            t.wait_ge(sS.h, ss_n)
                            for cch in range(nsubt):
                                r = t.transpose(pt8[:, cch * 128:(cch + 1) * 128],
                                                hq8[:, cch * 128:(cch + 1) * 128], ident[:])
                            sT.tag(r)
                        _emit("tensor", p8_ht)

                        def p8_hs(s, sub=sub, fb=fb, f0=f0, nsubt=nsubt, st_n=sT.n):
                            s.wait_ge(sT.h, st_n)
                            for cch in range(nsubt):
                                r = s.activation(out=hqf[:, f0 // 128 + cch, sub * 128:(sub + 1) * 128],
                                                 in_=pt8[:, cch * 128:(cch + 1) * 128], func=AF.Copy)
                            sS.tag(r)
                        _emit("scalar", p8_hs)

                for oc in range(NG):
                    pdd = pd8 if oc % 2 == 0 else pd8b
                    def p8_dw(g2, oc=oc, st_n=sT.n):
                        g2.wait_ge(sT.h, st_n)
                        sWg.inc(g2.dma_start(out=wdq_sb[:, oc % 2, :, :],
                                             in_=wdq_spill[:, :, oc * 128:(oc + 1) * 128]), 16)
                    block.gpsimd(p8_dw)

                    def p8_d_t(t, oc=oc, pdd=pdd, ss_n=sS.n, swg=sWg.n):
                        t.wait_ge(sS.h, ss_n)
                        t.wait_ge(sWg.h, swg)
                        for g in range(NFI):
                            r = t.matmul(pdd[:], lhsT=wdq_sb[:, oc % 2, g, :],
                                         rhs=hqf[:, g, :], start=(g == 0), stop=(g == NFI - 1))
                        sT.tag(r)
                    _emit("tensor", p8_d_t)

                    def p8_d_s(s, oc=oc, pdd=pdd, st_n=sT.n):
                        s.wait_ge(sT.h, st_n)
                        sS.tag(s.activation(out=g18[:, oc, :], in_=pdd[:], func=AF.Silu))
                    _emit("scalar", p8_d_s)

                if "g1" in debug:
                    def dbg_g1(sy, ob=ob, ot=ot, ss_n=sS.n):
                        sy.wait_ge(sS.h, ss_n)
                        sO.inc(sy.dma_start(out=dbg["g1"][:, :, ot:ot + TB], in_=g18[:]), 16)
                    block.sync(dbg_g1)

                for g in range(NG):
                    pdd = pd8 if g % 2 == 0 else pd8b
                    def p8_s1_t(t, g=g, pdd=pdd, sa_n=sA.n, swg=sWg.n, sv_n=sV.n, ss_n=sS.n):
                        t.wait_ge(sA.h, sa_n)
                        t.wait_ge(sWg.h, swg)
                        t.wait_ge(sS.h, ss_n)
                        if g >= 2:
                            t.wait_ge(sV.h, sv_n)
                        for j in range(NG):
                            r = t.matmul(pdd[:], lhsT=l1wo_sb[:, j, g * 128:(g + 1) * 128],
                                         rhs=ao18[:, j, :], start=(j == 0), stop=(j == NG - 1))
                        sT.tag(r)
                    _emit("tensor", p8_s1_t)

                    def p8_s1_v(v, g=g, pdd=pdd, st_n=sT.n, ss_n=sS.n):
                        v.wait_ge(sT.h, st_n)
                        v.wait_ge(sS.h, ss_n)
                        r = v.tensor_tensor(out=s1s[:], in0=pdd[:], in1=xn1fm8[:, g, :], op=A.add)
                        sV.edge(v, r)
                        r = v.tensor_tensor(out=sg18[:, g, :], in0=s1s[:], in1=g18[:, g, :], op=A.mult)
                        sV.tag(r)
                    _emit("vector", p8_s1_v)

                for sub in range(4):
                    def p8_tr_t(t, sub=sub, sv_n=sV.n, so_n=sO.n):
                        t.wait_ge(sV.h, sv_n)
                        for g in range(NG):
                            r = t.transpose(ppk8[:, g * 128:(g + 1) * 128],
                                            sg18[:, g, sub * 128:(sub + 1) * 128], ident[:])
                        sT.tag(r)
                    _emit("tensor", p8_tr_t)

                    def p8_x1ld(sy, ob=ob, sub=sub, sv_n=sV.n):
                        sy.wait_ge(sV.h, sv_n)
                        sA.inc(sy.dma_start(out=x1t8[:], in_=x1_spill[:, ob * 4 + sub, :]), 16)
                    block.sync(p8_x1ld)

                    def p8_x2_v(v, sub=sub, st_n=sT.n, so_n=sO.n, sa_n=sA.n):
                        v.wait_ge(sT.h, st_n)
                        v.wait_ge(sO.h, so_n)
                        v.wait_ge(sA.h, sa_n)
                        r = v.tensor_tensor(out=x2t8[:], in0=x1t8[:], in1=ppk8[:], op=A.add)
                        sV.tag(r)
                    _emit("vector", p8_x2_v)

                    def p8_y(sy, ob=ob, sub=sub, sv_n=sV.n):
                        sy.wait_ge(sV.h, sv_n)
                        tt = ob * 4 + sub
                        sO.inc(sy.dma_start(out=y_out.rearrange("(n p) d -> p n d", p=128)[:, tt, :],
                                            in_=x2t8[:]), 16)
                    block.sync(p8_y)

        phase_barrier()
        def fin(sy):
            sO.wait(sy)
            sG.wait(sy)
        block.sync(fin)

    return nc, dbg



def _shard_inputs(inputs, c):
    b, r = c // 2, c % 2
    I = inputs
    t = lambda a: np.ascontiguousarray(a)
    x = I["L0_attn_x"][b]
    m = {}
    m["x_half"] = t(x[r * HS:(r + 1) * HS])
    ocols = list(range(r * DH, (r + 1) * DH)) + list(range(2048 + r * DH, 2048 + (r + 1) * DH))
    m["w_in"] = t(I["L0_attn_W_in"][:, ocols])
    m["conv_w"] = t(I["L0_attn_conv_w"][r * DH:(r + 1) * DH, 0, :].reshape(NG, 128, 4).transpose(1, 0, 2))
    m["conv_b"] = t(I["L0_attn_conv_b"][r * DH:(r + 1) * DH].reshape(NG, 128).T)
    m["w_x"] = t(I["L0_attn_W_x"][r * DH:(r + 1) * DH])
    m["w_dt"] = t(I["L0_attn_W_dt"][:, r * DH:(r + 1) * DH])
    m["b_dt"] = t(I["L0_attn_b_dt"][r * DH:(r + 1) * DH].reshape(NG, 128).T)
    m["a_log"] = t(I["L0_attn_A_log"][r * DH:(r + 1) * DH].reshape(NG, 128, NST).transpose(1, 0, 2))
    m["d_vec"] = t(I["L0_attn_D"][r * DH:(r + 1) * DH].reshape(NG, 128).T)
    m["w_out"] = t(I["L0_attn_W_out"][r * DH:(r + 1) * DH])
    m["anorm_w"] = t(I["L0_ffn_attn_norm_w"].reshape(NG, 128).T)
    m["fnorm_w"] = t(I["L0_ffn_ffn_norm_w"].reshape(NG, 128).T)
    pair_cols = []
    for ht in range(4):
        for hl in (ht, ht + 4):
            gh = 8 * r + hl
            pair_cols.extend(range(gh * 64, (gh + 1) * 64))
    m["wqT"] = t(I["L0_ffn_wq"].T[:, pair_cols])
    m["wkT"] = t(I["L0_ffn_wk"].T[:, r * 128:(r + 1) * 128])
    m["wvT"] = t(I["L0_ffn_wv"].T[:, r * 128:(r + 1) * 128])
    woTfull = I["L0_ffn_wo"].T
    pair_rows = []
    for rr in (r, 1 - r):
        for ht in range(4):
            for hl in (ht, ht + 4):
                gh = 8 * rr + hl
                pair_rows.extend(range(gh * 64, (gh + 1) * 64))
    m["woT"] = t(woTfull[pair_rows])
    m["gate_wT"] = t(I["L0_ffn_gate_w"].T)
    m["expert_wT"] = t(np.stack([I["L0_ffn_expert_w"][e].T for e in range(8)]))
    m["l1_wq"] = t(I["L1_attn_Wq"][:, pair_cols])
    m["l1_wk"] = t(I["L1_attn_Wk"][:, pair_cols])
    m["l1_wv"] = t(I["L1_attn_Wv"][:, pair_cols])
    m["l1_wo"] = t(I["L1_attn_Wo"][pair_rows])
    m["l1_wg"] = t(I["L1_ffn_w_gate"])
    m["l1_wu"] = t(I["L1_ffn_w_up"])
    m["l1_wd"] = t(I["L1_ffn_w_down"].reshape(NG, 128, FFI))
    m["ident_in"] = np.eye(128, dtype=np.float32)
    m["ones_in"] = np.ones((128, 128), np.float32)
    kk, qq = np.meshgrid(np.arange(128), np.arange(128), indexing="ij")
    m["tri_in"] = (qq >= kk).astype(np.float32)
    h8 = np.zeros((8, 1024), np.float32)
    for j in range(8):
        h8[j, j * 128:(j + 1) * 128] = 1.0
    m["hot8_in"] = h8
    sel = np.zeros((64, 256), np.float32)
    sel[:, 0:64] = np.eye(64)
    sel[:, 128 + 64:256] = np.eye(64)
    m["sel_in"] = sel
    h16 = np.zeros((16, 2048), np.float32)
    for j in range(16):
        h16[j, j * 128:(j + 1) * 128] = 1.0
    m["hot16_in"] = h16
    return m


_BUILD_CACHE = {}


def kernel(**inputs):
    debug = tuple(os.environ.get("MK_DEBUG", "").split(",")) if os.environ.get("MK_DEBUG") else ()
    key = debug
    if key not in _BUILD_CACHE:
        _BUILD_CACHE[key] = build_nc(debug)
    nc, dbg = _BUILD_CACHE[key]
    in_maps = [_shard_inputs(inputs, c) for c in range(8)]
    res = run_bass_kernel_spmd(nc, in_maps, core_ids=list(range(8)))
    kernel.dbg_results = res.results
    out = np.empty((4, S, D), np.float32)
    for c in range(8):
        b, r = c // 2, c % 2
        out[b, r * HS:(r + 1) * HS] = res.results[c]["y_out"]
    return out


# version 07 (final)
# speedup vs baseline: 1.2647x; 1.2271x over previous
import os
import numpy as np
from contextlib import ExitStack

import concourse.bass as bass
import concourse.mybir as mybir
from concourse.bass_utils import run_bass_kernel_spmd
from concourse.alu_op_type import AluOpType as A
from concourse.expressions import binary_op, make_scalar_value, RuntimeValue

F32 = mybir.dt.float32
BF16 = mybir.dt.bfloat16
I32 = mybir.dt.int32
AF = mybir.ActivationFunctionType
AX = mybir.AxisListType

S = 4096
HS = 2048
D = 1024
DH = 1024
NST = 16
DTR = 64
TB = 512
NBLK = 8
OBLK = 4
NG = 8
FFI = 2816
NFI = 22
EPS_LN = 1e-5
EPS_RMS = 1e-6

GROUPS_ALL = [list(range(8))]
GROUPS_PAIR = [[0, 1], [2, 3], [4, 5], [6, 7]]


def _rv(expr, mx=(1 << 30)):
    return RuntimeValue(expr, min_val=0, max_val=mx)


def build_nc(debug=()):
    nc = bass.Bass("TRN2", target_bir_lowering=False, debug=False)
    es = ExitStack()

    def din(name, shape, dt=F32):
        return nc.dram_tensor(name, list(shape), dt, kind="ExternalInput").ap()

    def dint(name, shape, dt=F32, shared=False):
        kw = {"kind": "Internal"}
        if shared:
            kw["addr_space"] = "Shared"
        return nc.dram_tensor(name, list(shape), dt, **kw).ap()

    x_half = din("x_half", (HS, D))
    w_in = din("w_in", (D, 2 * DH))
    conv_w = din("conv_w", (128, NG, 4))
    conv_b = din("conv_b", (128, NG))
    w_x = din("w_x", (D, 96))
    w_dt = din("w_dt", (DTR, DH))
    b_dt = din("b_dt", (128, NG))
    a_log = din("a_log", (128, NG, NST))
    d_vec = din("d_vec", (128, NG))
    w_out = din("w_out", (D, D))
    anorm_w = din("anorm_w", (128, NG))
    fnorm_w = din("fnorm_w", (128, NG))
    wqT = din("wqT", (D, 512))
    wkT = din("wkT", (D, 128))
    wvT = din("wvT", (D, 128))
    woT = din("woT", (D, D))
    gate_wT = din("gate_wT", (D, 8))
    expert_wT = din("expert_wT", (8, D, D))
    l1_wq = din("l1_wq", (D, 512))
    l1_wk = din("l1_wk", (D, 512))
    l1_wv = din("l1_wv", (D, 512))
    l1_wo = din("l1_wo", (D, D))
    l1_wg = din("l1_wg", (FFI, D))
    l1_wu = din("l1_wu", (FFI, D))
    l1_wd = din("l1_wd", (NG, 128, FFI))
    ident_in = din("ident_in", (128, 128))
    ones_in = din("ones_in", (128, 128))
    tri_in = din("tri_in", (128, 128))
    hot8_in = din("hot8_in", (8, 1024))
    sel_in = din("sel_in", (64, 256))
    hot16_in = din("hot16_in", (16, 2048))

    y_out = nc.dram_tensor("y_out", [HS, D], F32, kind="ExternalOutput").ap()

    dbg = {}
    def dbg_out(name, shape, dt=F32):
        if name in debug:
            dbg[name] = nc.dram_tensor("dbg_" + name, list(shape), dt,
                                       kind="ExternalOutput").ap()
        return dbg.get(name)

    cc_xn_in = dint("cc_xn_in", (D, HS), BF16)
    cc_xn_out = dint("cc_xn_out", (8 * D, HS), BF16, shared=True)
    cc_rs_in = dint("cc_rs_in", (1, HS))
    cc_rs_out = dint("cc_rs_out", (8, HS), shared=True)
    u_spill = dint("u_spill", (128, NG, S), BF16)
    res_spill = dint("res_spill", (128, NG, S), BF16)
    cc_dbc_in = dint("cc_dbc_in", (NBLK, 96, TB))
    cc_dbc_out = dint("cc_dbc_out", (NBLK, 96, TB))
    cc_s_in = dint("cc_s_in", (128, NG, S), BF16)
    cc_s_out = dint("cc_s_out", (8 * 128, NG, S), BF16, shared=True)
    cc_ao_in = dint("cc_ao_in", (128, 4, S), BF16)
    cc_ao_out = dint("cc_ao_out", (8 * 128, 4, S), BF16, shared=True)
    cc_xn1_in = dint("cc_xn1_in", (D, HS), BF16)
    cc_xn1_out = dint("cc_xn1_out", (8 * D, HS), BF16, shared=True)
    xn1f_spill = dint("xn1f_spill", (128, 16, D))
    x1_spill = dint("x1_spill", (128, 16, D))
    cc_ao1_in = dint("cc_ao1_in", (128, 4, S), BF16)
    cc_ao1_out = dint("cc_ao1_out", (8 * 128, 4, S), BF16, shared=True)
    ao_my = dint("ao_my", (128, 4, HS), BF16)
    ao_pt = dint("ao_pt", (128, 4, HS), BF16)
    s_my = dint("s_my", (128, NG, HS), BF16)
    s_pt = dint("s_pt", (128, NG, HS), BF16)
    xn1_loc = dint("xn1_loc", (D, S), BF16)
    ao1_my = dint("ao1_my", (128, 4, HS), BF16)
    ao1_pt = dint("ao1_pt", (128, 4, HS), BF16)
    wgq_spill = dint("wgq_spill", (128, NG, FFI), BF16)
    wuq_spill = dint("wuq_spill", (128, NG, FFI), BF16)
    wdq_spill = dint("wdq_spill", (128, NFI, D), BF16)

    def sb(name, shape, dt=F32):
        return es.enter_context(nc.sbuf_tensor(name, list(shape), dt))

    ident = sb("ident", (128, 128), BF16)
    identf = sb("identf", (128, 128), F32)
    onesb = sb("onesb", (128, 128), BF16)
    onesf = sb("onesf", (128, 128), F32)
    trib = sb("trib", (128, 128), BF16)
    hot8f = sb("hot8f", (8, 1024))
    hot8b = sb("hot8b", (8, 1024), BF16)
    hot16b = sb("hot16b", (16, 2048), BF16)
    selb = sb("selb", (64, 256), BF16)
    conv_w_sb = sb("conv_w_sb", (128, NG, 4))
    conv_b_sb = sb("conv_b_sb", (128, NG))
    b_dt_sb = sb("b_dt_sb", (128, NG))
    a_sb = sb("a_sb", (128, NG, NST))
    d_sb = sb("d_sb", (128, NG))
    anw_sb = sb("anw_sb", (128, NG))
    fnw_sb = sb("fnw_sb", (128, NG))
    w_x_sb = sb("w_x_sb", (128, NG, 96), BF16)
    w_dt_sb = sb("w_dt_sb", (DTR, DH), BF16)
    gw_sb = sb("gw_sb", (128, NG, 8), BF16)
    rn_rstd = sb("rn_rstd", (NBLK, TB))
    state = sb("state", (128, NG, NST))

    class Sem:
        def __init__(self, name):
            self.h = es.enter_context(nc.semaphore(name))
            self.n = 0

        def inc(self, instr, k):
            instr.then_inc(self.h, k)
            self.n += k

        def bump(self, eng, k=1):
            eng.sem_inc(self.h, k)
            self.n += k

        def tag(self, instr, k=1):
            instr.then_inc(self.h, k)
            self.n += k

        def edge(self, eng, instr, k=1):
            self.tag(instr, k)
            self.wait(eng)

        def wait(self, eng, n=None):
            eng.wait_ge(self.h, self.n if n is None else n)

    sWg = Sem("sWg")
    sWs = Sem("sWs")
    sA = Sem("sA")
    sV = Sem("sV")
    sS = Sem("sS")
    sT = Sem("sT")
    sG = Sem("sG")
    sD = Sem("sD")
    sO = Sem("sO")
    sQ = Sem("sQ")
    sX = Sem("sX")
    sY = Sem("sY")

    pid_expr = [None]

    def pid(eng):
        return eng.partition_id()

    def rv_role(eng):
        return _rv(binary_op(pid(eng), make_scalar_value(1), A.bitwise_and))

    def rv_mul(val, k, mx=(1 << 30)):
        return _rv(binary_op(val, make_scalar_value(k), A.mult), mx)

    def rv_xor1(eng):
        return _rv(binary_op(pid(eng), make_scalar_value(1), A.bitwise_xor))

    ds = bass.ds

    with es, nc.Block() as block:
        _eng_sem = {"vector": sV, "scalar": sS, "tensor": sT}

        def phase_barrier():
            sems = [sWg, sWs, sA, sV, sS, sT, sG, sO, sD, sQ, sX, sY]
            for engname in ["vector", "scalar", "tensor", "gpsimd", "sync"]:
                def bar(eng):
                    for sm in sems:
                        if sm.n:
                            eng.wait_ge(sm.h, sm.n)
                getattr(block, engname)(bar)

        tick_hook = [None]

        def _emit(engname, fn):
            sem = _eng_sem.get(engname)
            n = sem.n if sem else 0

            def wrapped(eng):
                if sem is not None and n > 0:
                    eng.wait_ge(sem.h, n)
                fn(eng)
            getattr(block, engname)(wrapped)
            if tick_hook[0] is not None:
                h = tick_hook[0]
                tick_hook[0] = None
                h()
                tick_hook[0] = h

        def g_loads(g):
            for dst, src in [(ident, ident_in), (onesb, ones_in), (trib, tri_in),
                             (hot16b, hot16_in), (selb, sel_in), (hot8b, hot8_in),
                             (w_x_sb, w_x.rearrange("(g p) c -> p g c", p=128)), (w_dt_sb, w_dt),
                             (gw_sb, gate_wT.rearrange("(g p) c -> p g c", p=128))]:
                sWg.inc(g.dma_start(out=dst[:], in_=src), 16)
        block.gpsimd(g_loads)

        def sy_loads(sy):
            for dst, src in [(identf, ident_in), (onesf, ones_in),
                             (conv_w_sb, conv_w), (conv_b_sb, conv_b),
                             (b_dt_sb, b_dt), (d_sb, d_vec),
                             (anw_sb, anorm_w), (fnw_sb, fnorm_w),
                             (a_sb, a_log), (hot8f, hot8_in)]:
                sWs.inc(sy.dma_start(out=dst[:], in_=src), 16)
        block.sync(sy_loads)

        def s_a(s):
            sWs.wait(s)
            sS.tag(s.activation(out=a_sb[:], in_=a_sb[:], func=AF.Exp))
        _emit("scalar", s_a)

        def v_a(v):
            sS.wait(v)
            v.tensor_scalar(out=a_sb[:], in0=a_sb[:], scalar1=-1.0, scalar2=None, op0=A.mult)
            sV.tag(v.memset(state[:], 0.0))
        _emit("vector", v_a)

        with nc.sbuf_tensor("xh", [128, 16, D], F32) as xh, \
             nc.sbuf_tensor("xnb", [128, 16, D], BF16) as xnb, \
             nc.sbuf_tensor("sq", [128, 16, 8], F32) as sq, \
             nc.sbuf_tensor("xnf", [128, NG, HS], BF16) as xnf, \
             nc.sbuf_tensor("rsh", [128, 16], F32) as rsh, \
             nc.sbuf_tensor("rs_t", [16, 128], F32) as rs_t, \
             nc.psum_tensor("prs", [16, 128], F32) as prs, \
             nc.psum_tensor("ptr", [128, 1024], BF16) as ptr, \
             nc.psum_tensor("ptr2", [128, 1024], BF16) as ptr2:

            def p0_load(sy):
                sA.inc(sy.dma_start(out=xh[:], in_=x_half.rearrange("(n p) d -> p n d", p=128)), 16)
            block.sync(p0_load)

            def p0_sq(s):
                sA.wait(s)
                for i in range(16):
                    r = s.activation(out=xnb[:, i, :], in_=xh[:, i, :], func=AF.Square,
                                     accum_out=sq[:, i, 1:2])
                sS.tag(r)
            _emit("scalar", p0_sq)

            def p0_v1(v):
                sA.wait(v)
                for i in range(16):
                    r = v.reduce_sum(out=sq[:, i, 0:1], in_=xh[:, i, :], axis=AX.X)
                sV.edge(v, r)
                sS.wait(v)
                for i in range(16):
                    v.tensor_scalar(out=sq[:, i, 2:3], in0=sq[:, i, 0:1], scalar1=1.0 / D, scalar2=None, op0=A.mult)
                    r = v.tensor_scalar(out=sq[:, i, 3:4], in0=sq[:, i, 1:2], scalar1=1.0 / D, scalar2=None, op0=A.mult)
                sV.edge(v, r)
                for i in range(16):
                    r = v.tensor_tensor(out=sq[:, i, 4:5], in0=sq[:, i, 2:3], in1=sq[:, i, 2:3], op=A.mult)
                sV.edge(v, r)
                for i in range(16):
                    r = v.tensor_tensor(out=sq[:, i, 4:5], in0=sq[:, i, 3:4], in1=sq[:, i, 4:5], op=A.subtract)
                sV.edge(v, r)
                for i in range(16):
                    r = v.tensor_scalar(out=sq[:, i, 5:6], in0=sq[:, i, 4:5], scalar1=EPS_LN, scalar2=None, op0=A.add)
                sV.tag(r)
            _emit("vector", p0_v1)

            def p0_s2(s):
                sV.wait(s)
                for i in range(16):
                    r = s.activation(out=sq[:, i, 6:7], in_=sq[:, i, 5:6], func=AF.Sqrt)
                sS.tag(r)
            _emit("scalar", p0_s2)

            def p0_v2(v):
                sS.wait(v)
                for i in range(16):
                    r = v.reciprocal(out=sq[:, i, 7:8], in_=sq[:, i, 6:7])
                sV.edge(v, r)
                for i in range(16):
                    v.tensor_scalar(out=xnb[:, i, :], in0=xh[:, i, :],
                                    scalar1=sq[:, i, 2:3], scalar2=sq[:, i, 7:8],
                                    op0=A.subtract, op1=A.mult)
                for i in range(16):
                    r = v.reciprocal(out=sq[:, i, 0:1], in_=sq[:, i, 5:6])
                sV.edge(v, r)
                for i in range(16):
                    r = v.tensor_tensor(out=sq[:, i, 3:4], in0=sq[:, i, 4:5], in1=sq[:, i, 0:1], op=A.mult)
                sV.edge(v, r)
                for i in range(16):
                    r = v.tensor_scalar(out=sq[:, i, 3:4], in0=sq[:, i, 3:4], scalar1=EPS_RMS, scalar2=None, op0=A.add)
                sV.tag(r)
            _emit("vector", p0_v2)

            def p0_s3(s):
                sV.wait(s)
                for i in range(16):
                    r = s.activation(out=sq[:, i, 5:6], in_=sq[:, i, 3:4], func=AF.Sqrt)
                sS.tag(r)
            _emit("scalar", p0_s3)

            def p0_v3(v):
                sS.wait(v)
                for i in range(16):
                    r = v.reciprocal(out=rsh[:, i:i + 1], in_=sq[:, i, 5:6])
                sV.tag(r)
            _emit("vector", p0_v3)

            def p0_trs(t):
                sV.wait(t)
                sWs.wait(t)
                sWg.wait(t)
                sT.tag(t.transpose(prs[:], rsh[:], identf[:]))
            _emit("tensor", p0_trs)

            def p0_evrs(v):
                sT.wait(v)
                sV.tag(v.tensor_copy(out=rs_t[:], in_=prs[:]))
            _emit("vector", p0_evrs)

            ev_cnt = [sV.n]
            for i in range(16):
                pt = ptr if i % 2 == 0 else ptr2

                def p0_t(t, i=i, pt=pt):
                    if i == 0:
                        sV.wait(t)
                        sWg.wait(t)
                    if i >= 2:
                        t.wait_ge(sV.h, ev_cnt[0] + i - 1)
                    for jj in range(NG):
                        r = t.transpose(pt[:, jj * 128:(jj + 1) * 128],
                                        xnb[:, i, jj * 128:(jj + 1) * 128], ident[:])
                    sT.tag(r)
                _emit("tensor", p0_t)
                tcur = sT.n

                def p0_ev(v, i=i, pt=pt, tcur=tcur):
                    v.wait_ge(sT.h, tcur)
                    for jj in range(NG):
                        r = v.tensor_copy(out=xnf[:, jj, i * 128:(i + 1) * 128],
                                          in_=pt[:, jj * 128:(jj + 1) * 128])
                    sV.tag(r)
                _emit("vector", p0_ev)

            def p0_st(sy):
                sV.wait(sy)
                sO.inc(sy.dma_start(out=cc_xn_in.rearrange("(g p) t -> p g t", p=128), in_=xnf[:]), 16)
                sO.inc(sy.dma_start(out=cc_rs_in[0:1, :].rearrange("o (n p) -> (o n) p", p=128), in_=rs_t[:]), 16)
            block.sync(p0_st)

            def p0_cc(g):
                sO.wait(g)
                sG.inc(g.collective_compute("AllGather", A.bypass, replica_groups=GROUPS_ALL,
                                            ins=[cc_xn_in[:]], outs=[cc_xn_out[:]]), 1)
                sG.inc(g.collective_compute("AllGather", A.bypass, replica_groups=GROUPS_ALL,
                                            ins=[cc_rs_in[:]], outs=[cc_rs_out[:]]), 1)
            block.gpsimd(p0_cc)

        xn_cm = nc.sbuf_tensor("xn", [128, NG, S], BF16)
        xn = xn_cm.__enter__()

        def p0_ld2(sy):
            sG.wait(sy)
            base = rv_mul(_rv(binary_op(pid(sy), make_scalar_value(~1 & 0xff), A.bitwise_and), 6), D, 6 * D)
            src0 = cc_xn_out[ds(base, D), :].rearrange("(g p) t -> p g t", p=128)
            sA.inc(sy.dma_start(out=xn[:, :, 0:HS], in_=src0), 16)
            base1 = rv_mul(_rv(binary_op(pid(sy), make_scalar_value(1), A.bitwise_or), 7), D, 7 * D)
            src1 = cc_xn_out[ds(base1, D), :].rearrange("(g p) t -> p g t", p=128)
            sA.inc(sy.dma_start(out=xn[:, :, HS:S], in_=src1), 16)
            b0 = _rv(binary_op(pid(sy), make_scalar_value(~1 & 0xff), A.bitwise_and), 6)
            sA.inc(sy.dma_start(out=rn_rstd[0:4, :], in_=cc_rs_out[ds(b0, 1), :].rearrange("o (n t) -> (o n) t", t=TB)), 16)
            b1 = _rv(binary_op(pid(sy), make_scalar_value(1), A.bitwise_or), 7)
            sA.inc(sy.dma_start(out=rn_rstd[4:8, :], in_=cc_rs_out[ds(b1, 1), :].rearrange("o (n t) -> (o n) t", t=TB)), 16)
        block.sync(p0_ld2)


        if dbg_out("xn", (128, NG, S), BF16):
            def dxn(sy):
                sA.wait(sy)
                sO.inc(sy.dma_start(out=dbg["xn"][:], in_=xn[:]), 16)
            block.sync(dxn)

        phase_barrier()
        with nc.sbuf_tensor("w_in_sb", [128, NG, 2 * DH], BF16) as w_in_sb, \
             nc.sbuf_tensor("xi_ext", [128, NG, TB + 3], F32) as xi_ext, \
             nc.sbuf_tensor("u_blk", [128, NG, TB], BF16) as u_blk, \
             nc.sbuf_tensor("res_blk", [128, NG, TB], BF16) as res_blk, \
             nc.sbuf_tensor("cacc", [128, NG, TB], F32) as cacc, \
             nc.sbuf_tensor("dbc_ev", [96, TB], F32) as dbc_ev, \
             nc.psum_tensor("ps_mm", [128, TB], F32) as ps_mm, \
             nc.psum_tensor("ps_mm2", [128, TB], F32) as ps_mm2, \
             nc.psum_tensor("ps_dbc", [96, TB], F32) as ps_dbc:

            def p1_wload(g):
                sWg.inc(g.dma_start(out=w_in_sb[:], in_=w_in.rearrange("(g p) c -> p g c", p=128)), 16)
            block.gpsimd(p1_wload)

            def p1_init(v):
                sV.tag(v.memset(xi_ext[:, :, 0:3], 0.0))
            _emit("vector", p1_init)

            p1_evs = []
            for blk in range(NBLK):
                t0 = blk * TB

                for grp in range(16):
                    is_res = grp >= NG
                    oc = grp % NG
                    gidx = blk * 16 + grp
                    pm = ps_mm if grp % 2 == 0 else ps_mm2

                    def p1_t(t, blk=blk, t0=t0, grp=grp, oc=oc, pm=pm, is_res=is_res, gidx=gidx):
                        if gidx == 0:
                            sWg.wait(t)
                            sWs.wait(t)
                            sA.wait(t)
                            sV.wait(t)
                        if gidx >= 2:
                            t.wait_ge(sV.h, p1_evs[gidx - 2])
                        col0 = (DH if is_res else 0) + oc * 128
                        for g in range(NG):
                            r = t.matmul(pm[:], lhsT=w_in_sb[:, g, col0:col0 + 128],
                                         rhs=xn[:, g, t0:t0 + TB],
                                         start=(g == 0), stop=(g == NG - 1))
                        sT.tag(r)
                    _emit("tensor", p1_t)
                    tcur = sT.n

                    def p1_ev(v, grp=grp, oc=oc, pm=pm, is_res=is_res, tcur=tcur,
                              so_n=sO.n, ss_n=sS.n):
                        v.wait_ge(sT.h, tcur)
                        if grp == 0:
                            v.wait_ge(sO.h, so_n)
                            v.wait_ge(sS.h, ss_n)
                        if is_res:
                            r = v.tensor_copy(out=res_blk[:, oc, :], in_=pm[:])
                        else:
                            r = v.tensor_copy(out=xi_ext[:, oc, 3:3 + TB], in_=pm[:])
                        sV.tag(r)
                    _emit("vector", p1_ev)
                    p1_evs.append(sV.n)

                def p1_conv(v, blk=blk, ss_n=sS.n):
                    v.wait_ge(sS.h, ss_n)
                    for oc in range(NG):
                        r = v.tensor_scalar(out=cacc[:, oc, :], in0=xi_ext[:, oc, 0:TB],
                                            scalar1=conv_w_sb[:, oc, 0:1], scalar2=None, op0=A.mult)
                    sV.edge(v, r)
                    for kk in range(1, 4):
                        for oc in range(NG):
                            r = v.scalar_tensor_tensor(out=cacc[:, oc, :], in0=xi_ext[:, oc, kk:kk + TB],
                                                       scalar=conv_w_sb[:, oc, kk:kk + 1],
                                                       in1=cacc[:, oc, :], op0=A.mult, op1=A.add)
                        sV.edge(v, r)
                    for oc in range(NG):
                        r = v.tensor_copy(out=xi_ext[:, oc, 0:3], in_=xi_ext[:, oc, TB:TB + 3])
                    sV.tag(r)
                _emit("vector", p1_conv)

                def p1_s(s, blk=blk, sv_n=sV.n, st_n=sT.n, so_n=sO.n):
                    s.wait_ge(sV.h, sv_n)
                    s.wait_ge(sT.h, st_n)
                    s.wait_ge(sO.h, so_n)
                    for oc in range(NG):
                        r = s.activation(out=u_blk[:, oc, :], in_=cacc[:, oc, :], func=AF.Silu,
                                         bias=conv_b_sb[:, oc:oc + 1], scale=1.0)
                    sS.tag(r)
                _emit("scalar", p1_s)

                def p1_t2(t, blk=blk, t0=t0):
                    sS.wait(t)
                    for g in range(NG):
                        r = t.matmul(ps_dbc[:], lhsT=w_x_sb[:, g, :], rhs=u_blk[:, g, :],
                                     start=(g == 0), stop=(g == NG - 1))
                    sT.tag(r)
                _emit("tensor", p1_t2)

                def p1_ev3(v, blk=blk, so_n=sO.n):
                    sT.wait(v)
                    v.wait_ge(sO.h, so_n)
                    sV.tag(v.tensor_copy(out=dbc_ev[:], in_=ps_dbc[:]))
                _emit("vector", p1_ev3)

                def p1_st(sy, blk=blk, t0=t0):
                    sV.wait(sy)
                    sS.wait(sy)
                    sO.inc(sy.dma_start(out=u_spill[:, :, t0:t0 + TB], in_=u_blk[:]), 16)
                    sO.inc(sy.dma_start(out=res_spill[:, :, t0:t0 + TB], in_=res_blk[:]), 16)
                    sO.inc(sy.dma_start(out=cc_dbc_in[blk], in_=dbc_ev[:]), 16)
                block.sync(p1_st)

            def p1_cc(g):
                sO.wait(g)
                sG.inc(g.collective_compute("AllReduce", A.add, replica_groups=GROUPS_PAIR,
                                            ins=[cc_dbc_in[:]], outs=[cc_dbc_out[:]]), 1)
            block.gpsimd(p1_cc)

            if dbg_out("u", (128, NG, S), BF16):
                def du_(sy):
                    sO.wait(sy)
                    sO.inc(sy.dma_start(out=dbg["u"][:], in_=u_spill[:]), 16)
                block.sync(du_)
            if dbg_out("res", (128, NG, S), BF16):
                def dr_(sy):
                    sO.wait(sy)
                    sO.inc(sy.dma_start(out=dbg["res"][:], in_=res_spill[:]), 16)
                block.sync(dr_)
            if dbg_out("dbc", (NBLK, 96, TB)):
                def dd_(sy):
                    sG.wait(sy)
                    sO.inc(sy.dma_start(out=dbg["dbc"][:], in_=cc_dbc_out[:]), 16)
                block.sync(dd_)

        phase_barrier()
        kscale_l0 = 0.125
        es2 = ExitStack()
        def sb2(name, shape, dt=F32):
            return es2.enter_context(nc.sbuf_tensor(name, list(shape), dt))
        wq_sb = sb2("wq_sb", (128, NG, 512), BF16)
        wk_sb = sb2("wk_sb", (128, NG, 128), BF16)
        wv_sb = sb2("wv_sb", (128, NG, 128), BF16)
        k_fm = sb2("k_fm", (128, S), BF16)
        v_tok = sb2("v_tok", (128, 32, 130), BF16)
        rnb = sb2("rnb", (128, NG, TB), BF16)
        qb = sb2("qb", (128, 4, TB), BF16)
        atb = sb2("atb", (128, 2, 2, TB), BF16)
        recd = sb2("recd", (128, 2, TB))
        ib2 = sb2("ib2", (64, 2, TB))
        nb = sb2("nb", (64, 2, TB), BF16)
        aob = sb2("aob", (128, 4, TB), BF16)
        psA = es2.enter_context(nc.psum_tensor("psA", [128, TB], F32))
        psB = es2.enter_context(nc.psum_tensor("psB", [128, TB], F32))
        psC = es2.enter_context(nc.psum_tensor("psC", [128, TB], F32))
        psD = es2.enter_context(nc.psum_tensor("psD", [128, TB], F32))
        av0 = es2.enter_context(nc.psum_tensor("av0", [128, TB], F32))
        av1 = es2.enter_context(nc.psum_tensor("av1", [128, TB], F32))
        psM = es2.enter_context(nc.psum_tensor("psM", [128, TB], F32))
        with es2:

            def p2_init(g):
                sWg.inc(g.dma_start(out=wq_sb[:], in_=wqT.rearrange("(g p) c -> p g c", p=128)), 16)
                sWg.inc(g.dma_start(out=wk_sb[:], in_=wkT.rearrange("(g p) c -> p g c", p=128)), 16)
                sWg.inc(g.dma_start(out=wv_sb[:], in_=wvT.rearrange("(g p) c -> p g c", p=128)), 16)
                g.memset(v_tok[:, :, 64:65], 1.0)
                r = g.memset(v_tok[:, :, 129:130], 1.0)
                sG.tag(r)
            block.gpsimd(p2_init)

            for i in range(NBLK):
                t0 = i * TB

                def p2_rn_t(t, i=i, sv_n=sV.n):
                    t.wait_ge(sV.h, sv_n)
                    r = t.matmul(psM[:], lhsT=hot8f[:, i * 128:(i + 1) * 128], rhs=rn_rstd[:],
                                 start=True, stop=True)
                    sT.tag(r)
                _emit("tensor", p2_rn_t)

                def p2_rn_v(v, i=i, st_n=sT.n):
                    v.wait_ge(sT.h, st_n)
                    for g in range(NG):
                        r = v.scalar_tensor_tensor(out=rnb[:, g, :], in0=xn[:, g, t0:t0 + TB],
                                                   scalar=anw_sb[:, g:g + 1], in1=psM[:],
                                                   op0=A.mult, op1=A.mult)
                    sV.tag(r)
                _emit("vector", p2_rn_v)
                rn_done = sV.n

                for hp in range(4):
                    pq = psA if hp % 2 == 0 else psB
                    def p2_q_t(t, hp=hp, pq=pq, rn_done=rn_done, ss_n=sS.n, swg=sWg.n):
                        t.wait_ge(sWg.h, swg)
                        t.wait_ge(sV.h, rn_done)
                        t.wait_ge(sS.h, ss_n)
                        for g in range(NG):
                            r = t.matmul(pq[:], lhsT=wq_sb[:, g, hp * 128:(hp + 1) * 128],
                                         rhs=rnb[:, g, :], start=(g == 0), stop=(g == NG - 1))
                        sT.tag(r)
                    _emit("tensor", p2_q_t)

                    def p2_q_s(s, hp=hp, pq=pq, st_n=sT.n):
                        s.wait_ge(sT.h, st_n)
                        sS.tag(s.activation(out=qb[:, hp, :], in_=pq[:], func=AF.Copy))
                    _emit("scalar", p2_q_s)

                def p2_kv_t(t, i=i, rn_done=rn_done, ss_n=sS.n):
                    t.wait_ge(sS.h, ss_n)
                    for g in range(NG):
                        r = t.matmul(psC[:], lhsT=wk_sb[:, g, :], rhs=rnb[:, g, :],
                                     start=(g == 0), stop=(g == NG - 1))
                    sT.tag(r)
                    for sub in range(4):
                        for g in range(NG):
                            r = t.matmul(psD[:, sub * 128:(sub + 1) * 128],
                                         lhsT=rnb[:, g, sub * 128:(sub + 1) * 128],
                                         rhs=wv_sb[:, g, :], start=(g == 0), stop=(g == NG - 1))
                    sT.tag(r)
                _emit("tensor", p2_kv_t)

                def p2_kv_s(s, i=i, t0=t0, st_n=sT.n):
                    s.wait_ge(sT.h, st_n)
                    s.activation(out=k_fm[:, t0:t0 + TB], in_=psC[:], func=AF.Copy, scale=kscale_l0)
                    for sub in range(4):
                        s.activation(out=v_tok[:, i * 4 + sub, 0:64],
                                     in_=psD[:, sub * 128 + 0:sub * 128 + 64], func=AF.Copy)
                        r = s.activation(out=v_tok[:, i * 4 + sub, 65:129],
                                         in_=psD[:, sub * 128 + 64:sub * 128 + 128], func=AF.Copy)
                    sS.tag(r)
                _emit("scalar", p2_kv_s)
                kv_done = sS.n

                nkt = 4 * i + 4
                for ht in range(4):
                    exp_done = {}
                    msk_done = {}
                    sc_done = {}

                    def mk_sc2(kt, i=i, ht=ht, exp_done=exp_done, sc_done=sc_done,
                               kv_done=kv_done):
                        par = kt % 2
                        scA = psA if par == 0 else psC
                        scB = psB if par == 0 else psD

                        def p2_sc(t, kt=kt, scA=scA, scB=scB, ss_n=sS.n):
                            t.wait_ge(sS.h, kv_done)
                            if kt >= 2:
                                t.wait_ge(sS.h, exp_done[kt - 2])
                            else:
                                t.wait_ge(sS.h, ss_n)
                            r = t.matmul(scA[:], lhsT=k_fm[0:64, kt * 128:(kt + 1) * 128],
                                         rhs=qb[0:64, ht, :], start=True, stop=True)
                            r = t.matmul(scB[:], lhsT=k_fm[64:128, kt * 128:(kt + 1) * 128],
                                         rhs=qb[64:128, ht, :], start=True, stop=True)
                            sT.tag(r)
                        _emit("tensor", p2_sc)
                        sc_done[kt] = sT.n

                    def mk_exp2(kt, i=i, exp_done=exp_done, msk_done=msk_done, sc_done=sc_done):
                        par = kt % 2
                        scA = psA if par == 0 else psC
                        scB = psB if par == 0 else psD
                        diag = (kt >= 4 * i)

                        def p2_exp(s, kt=kt, par=par, scA=scA, scB=scB):
                            s.wait_ge(sT.h, sc_done[kt])
                            s.activation(out=atb[:, 0, par, :], in_=scA[:], func=AF.Exp)
                            r = s.activation(out=atb[:, 1, par, :], in_=scB[:], func=AF.Exp)
                            sS.tag(r)
                            exp_done[kt] = sS.n
                        _emit("scalar", p2_exp)

                        if diag:
                            j = kt - 4 * i

                            def p2_msk(v, kt=kt, par=par, j=j, e_n=sS.n):
                                v.wait_ge(sS.h, e_n)
                                for hh in range(2):
                                    if j > 0:
                                        v.memset(atb[:, hh, par, 0:j * 128], 0.0)
                                    r = v.tensor_tensor(out=atb[:, hh, par, j * 128:(j + 1) * 128],
                                                        in0=atb[:, hh, par, j * 128:(j + 1) * 128],
                                                        in1=trib[:], op=A.mult)
                                sV.tag(r)
                                msk_done[kt] = sV.n
                            _emit("vector", p2_msk)

                    def mk_av2(kt, i=i, ht=ht, nkt=nkt, exp_done=exp_done, msk_done=msk_done):
                        par = kt % 2
                        diag = (kt >= 4 * i)

                        def p2_av(t, kt=kt, par=par, diag=diag, sg_n=sG.n):
                            if kt == 0:
                                t.wait_ge(sG.h, sg_n)
                            t.wait_ge(sS.h, exp_done[kt])
                            if diag:
                                t.wait_ge(sV.h, msk_done[kt])
                            r = t.matmul(av0[0:65, :], lhsT=v_tok[:, kt, 0:65],
                                         rhs=atb[:, 0, par, :], start=(kt == 0), stop=(kt == nkt - 1))
                            r = t.matmul(av1[0:65, :], lhsT=v_tok[:, kt, 65:130],
                                         rhs=atb[:, 1, par, :], start=(kt == 0), stop=(kt == nkt - 1))
                            sT.tag(r)
                        _emit("tensor", p2_av)

                    for kt in range(nkt + 1):
                        if kt < nkt:
                            mk_sc2(kt)
                        if kt >= 1:
                            mk_exp2(kt - 1)
                            mk_av2(kt - 1)

                    def p2_ep_v(v, ht=ht, st_n=sT.n, so_n=sO.n):
                        v.wait_ge(sT.h, st_n)
                        if ht == 0:
                            v.wait_ge(sO.h, so_n)
                        v.reciprocal(out=recd[64:65, 0, :], in_=av0[64:65, :])
                        r = v.reciprocal(out=recd[64:65, 1, :], in_=av1[64:65, :])
                        sV.tag(r)
                    _emit("vector", p2_ep_v)

                    def p2_ep_t(t, ht=ht, sv_n=sV.n):
                        t.wait_ge(sV.h, sv_n)
                        t.matmul(psM[0:64, :], lhsT=onesf[64:65, 0:64], rhs=recd[64:65, 0, :],
                                 start=True, stop=True)
                        r = t.matmul(psC[0:64, :], lhsT=onesf[64:65, 0:64], rhs=recd[64:65, 1, :],
                                     start=True, stop=True)
                        sT.tag(r)
                    _emit("tensor", p2_ep_t)

                    def p2_ep_sx(s, ht=ht, st_n=sT.n):
                        s.wait_ge(sT.h, st_n)
                        s.activation(out=ib2[:, 0, :], in_=psM[0:64, :], func=AF.Copy)
                        r = s.activation(out=ib2[:, 1, :], in_=psC[0:64, :], func=AF.Copy)
                        sS.tag(r)
                    _emit("scalar", p2_ep_sx)

                    def p2_ep_v2(v, ht=ht, ss_n=sS.n):
                        v.wait_ge(sS.h, ss_n)
                        v.tensor_tensor(out=nb[:, 0, :], in0=av0[0:64, :], in1=ib2[:, 0, :], op=A.mult)
                        r = v.tensor_tensor(out=nb[:, 1, :], in0=av1[0:64, :], in1=ib2[:, 1, :], op=A.mult)
                        sV.tag(r)
                    _emit("vector", p2_ep_v2)

                    def p2_ep_t2(t, ht=ht, sv_n=sV.n):
                        t.wait_ge(sV.h, sv_n)
                        t.matmul(psB[:], lhsT=selb[:, 0:128], rhs=nb[:, 0, :], start=True, stop=False)
                        r = t.matmul(psB[:], lhsT=selb[:, 128:256], rhs=nb[:, 1, :], start=False, stop=True)
                        sT.tag(r)
                    _emit("tensor", p2_ep_t2)

                    def p2_ep_s2(s, ht=ht, st_n=sT.n):
                        s.wait_ge(sT.h, st_n)
                        sS.tag(s.activation(out=aob[:, ht, :], in_=psB[:], func=AF.Copy))
                    _emit("scalar", p2_ep_s2)

                def p2_st(sy, i=i, t0=t0, ss_n=sS.n):
                    sy.wait_ge(sS.h, ss_n)
                    sO.inc(sy.dma_start(out=cc_ao_in[:, :, t0:t0 + TB], in_=aob[:]), 16)
                block.sync(p2_st)

            def p2_cc(g):
                sO.wait(g)
                sG.inc(g.collective_compute("AllGather", A.bypass, replica_groups=GROUPS_ALL,
                                            ins=[cc_ao_in[:]], outs=[cc_ao_out[:]]), 1)
            block.gpsimd(p2_cc)

            def p2_restage(sc):
                sG.wait(sc)
                p_ = pid(sc)
                role_ = _rv(binary_op(p_, make_scalar_value(1), A.bitwise_and), 1)
                hoff = _rv(binary_op(role_, make_scalar_value(HS), A.mult), HS)
                mybase = _rv(binary_op(p_, make_scalar_value(128), A.mult), 7 * 128)
                pbase = _rv(binary_op(_rv(binary_op(p_, make_scalar_value(1), A.bitwise_xor), 7),
                                      make_scalar_value(128), A.mult), 7 * 128)
                sD.inc(sc.dma_start(out=ao_my[:], in_=cc_ao_out[ds(mybase, 128), :, ds(hoff, HS)]), 16)
                sD.inc(sc.dma_start(out=ao_pt[:], in_=cc_ao_out[ds(pbase, 128), :, ds(hoff, HS)]), 16)
            block.scalar(p2_restage)

        xn_cm.__exit__(None, None, None)
        phase_barrier()

        es3 = ExitStack()
        def sb3(name, shape, dt=F32):
            return es3.enter_context(nc.sbuf_tensor(name, list(shape), dt))
        scr = sb3("scr", (128, 32, TB), BF16)
        scrd = sb3("scrd", (128, NG, TB))
        yb3 = sb3("yb3", (128, 2, TB))
        delta = sb3("delta", (128, NG, TB), BF16)
        du = sb3("du", (128, NG, TB), BF16)
        bcB = sb3("bcB", (128, NST, TB), BF16)
        bcC = sb3("bcC", (128, NST, TB), BF16)
        dbu = sb3("dbu", (128, NST, TB), BF16)
        hbuf = sb3("hbuf", (128, NST, TB), BF16)
        u3 = sb3("u3", (128, NG, TB), BF16)
        res3 = sb3("res3", (128, NG, TB), BF16)
        ygc = sb3("ygc", (128, NG, TB), BF16)
        spb = sb3("spb", (128, NG, TB), BF16)
        dbc3 = sb3("dbc3", (64, TB), BF16)
        w_out_sb = sb3("w_out_sb", (128, NG, D), BF16)
        dbc3B = sb3("dbc3B", (16, TB), BF16)
        dbc3C = sb3("dbc3C", (16, TB), BF16)
        pw0 = es3.enter_context(nc.psum_tensor("pw0", [128, TB], F32))
        pw1 = es3.enter_context(nc.psum_tensor("pw1", [128, TB], F32))
        pby = es3.enter_context(nc.psum_tensor("pby", [128, TB], F32))
        pbs = es3.enter_context(nc.psum_tensor("pbs", [128, TB], F32))
        pbB = es3.enter_context(nc.psum_tensor("pbB", [128, TB], F32))
        pbC = es3.enter_context(nc.psum_tensor("pbC", [128, TB], F32))
        with es3:

            for blk in range(NBLK):
                t0 = blk * TB

                def p3_ld(sy, blk=blk, t0=t0, sv_n=sV.n, st_n=sT.n, ss_n=sS.n):
                    sy.wait_ge(sV.h, sv_n)
                    sy.wait_ge(sS.h, ss_n)
                    sA.inc(sy.dma_start(out=u3[:], in_=u_spill[:, :, t0:t0 + TB]), 16)
                    sA.inc(sy.dma_start(out=res3[:], in_=res_spill[:, :, t0:t0 + TB]), 16)
                block.sync(p3_ld)

                def p3_ldg(g, blk=blk, sv_n=sV.n, st_n=sT.n):
                    if blk == 0:
                        sWg.inc(g.dma_start(out=w_out_sb[:], in_=w_out.rearrange("(g p) c -> p g c", p=128)), 16)
                    sG.wait(g)
                    g.wait_ge(sT.h, st_n)
                    sWg.inc(g.dma_start(out=dbc3[:], in_=cc_dbc_out[blk, 0:64, :]), 16)
                    sWg.inc(g.dma_start(out=dbc3B[:], in_=cc_dbc_out[blk, 64:80, :]), 16)
                    sWg.inc(g.dma_start(out=dbc3C[:], in_=cc_dbc_out[blk, 80:96, :]), 16)
                block.gpsimd(p3_ldg)

                for n in range(NST):
                    def p3_bc_t(t, n=n, swg=sWg.n, sv_n=sV.n):
                        t.wait_ge(sWg.h, swg)
                        if n >= 1:
                            t.wait_ge(sV.h, sv_n)
                        t.matmul(pbB[:], lhsT=hot16b[:, n * 128:(n + 1) * 128], rhs=dbc3B[:],
                                 start=True, stop=True)
                        r = t.matmul(pbC[:], lhsT=hot16b[:, n * 128:(n + 1) * 128], rhs=dbc3C[:],
                                     start=True, stop=True)
                        sT.tag(r)
                    _emit("tensor", p3_bc_t)

                    def p3_bc_v(v, n=n, st_n=sT.n):
                        v.wait_ge(sT.h, st_n)
                        v.tensor_copy(out=bcB[:, n, :], in_=pbB[:])
                        r = v.tensor_copy(out=bcC[:, n, :], in_=pbC[:])
                        sV.tag(r)
                    _emit("vector", p3_bc_v)

                for g in range(NG):
                    pw = pw0 if g % 2 == 0 else pw1
                    def p3_d_t(t, g=g, pw=pw, swg=sWg.n, sv_n=sV.n, ss_n=sS.n):
                        t.wait_ge(sWg.h, swg)
                        if g >= 2:
                            t.wait_ge(sV.h, sv_n)
                            t.wait_ge(sS.h, ss_n)
                        r = t.matmul(pw[:], lhsT=w_dt_sb[:, g * 128:(g + 1) * 128],
                                     rhs=dbc3[0:64, :], start=True, stop=True)
                        sT.tag(r)
                    _emit("tensor", p3_d_t)

                    def p3_d_v(v, g=g, pw=pw, st_n=sT.n):
                        v.wait_ge(sT.h, st_n)
                        r = v.tensor_scalar(out=delta[:, g, :], in0=pw[:],
                                            scalar1=b_dt_sb[:, g:g + 1], scalar2=0.0,
                                            op0=A.add, op1=A.max)
                        sV.tag(r)
                    _emit("vector", p3_d_v)

                    def p3_d_s(s, g=g, pw=pw, st_n=sT.n):
                        s.wait_ge(sT.h, st_n)
                        r = s.activation(out=scrd[:, g, :], in_=pw[:], func=AF.Abs,
                                         bias=b_dt_sb[:, g:g + 1])
                        sS.tag(r)
                    _emit("scalar", p3_d_s)

                def p3_sp_s(s):
                    for g in range(NG):
                        r = s.activation(out=scrd[:, g, :], in_=scrd[:, g, :], func=AF.Exp, scale=-1.0)
                    sS.edge(s, r)
                    for g in range(NG):
                        r = s.activation(out=scrd[:, g, :], in_=scrd[:, g, :], func=AF.Ln, bias=1.0)
                    sS.tag(r)
                _emit("scalar", p3_sp_s)

                def p3_dfin(v, ss_n=sS.n, sa_n=sA.n):
                    v.wait_ge(sS.h, ss_n)
                    v.wait_ge(sA.h, sa_n)
                    for g in range(NG):
                        r = v.tensor_tensor(out=delta[:, g, :], in0=delta[:, g, :],
                                            in1=scrd[:, g, :], op=A.add)
                    sV.edge(v, r)
                    for g in range(NG):
                        r = v.tensor_tensor(out=du[:, g, :], in0=delta[:, g, :],
                                            in1=u3[:, g, :], op=A.mult)
                    sV.tag(r)
                _emit("vector", p3_dfin)

                scan_done = {}
                for g in range(NG):
                    sl = (g % 2) * 16

                    def p3_dA(s, g=g, sl=sl, sv_n=sV.n, scan_done=scan_done):
                        if g >= 2:
                            s.wait_ge(sV.h, scan_done[g - 2])
                        else:
                            s.wait_ge(sV.h, sv_n)
                        for n in range(NST):
                            r = s.activation(out=scr[:, sl + n, :], in_=delta[:, g, :], func=AF.Exp,
                                             scale=a_sb[:, g, n:n + 1])
                        r = s.activation(out=yb3[:, 1, :], in_=res3[:, g, :], func=AF.Silu) if False else r
                        sS.tag(r)
                    _emit("scalar", p3_dA)

                    def p3_dbu(v, g=g, sl=sl, ss_n=sS.n):
                        v.wait_ge(sS.h, ss_n)
                        r = v.tensor_tensor(out=dbu[:],
                                            in0=du[:, g, :].unsqueeze(1).broadcast_to((128, NST, TB)),
                                            in1=bcB[:], op=A.mult)
                        sV.edge(v, r)
                        for n in range(NST):
                            r = v.tensor_tensor_scan(out=hbuf[:, n, :], data0=scr[:, sl + n, :],
                                                     data1=dbu[:, n, :],
                                                     initial=state[:, g, n:n + 1],
                                                     op0=A.mult, op1=A.add)
                        sV.edge(v, r)
                        scan_done[g] = sV.n
                        r = v.tensor_copy(out=state[:, g, :],
                                          in_=hbuf[:, :, 511:512].rearrange("p n o -> p (n o)"))
                        sV.edge(v, r)
                        r = v.tensor_tensor(out=hbuf[:], in0=hbuf[:], in1=bcC[:], op=A.mult)
                        sV.tag(r)
                    _emit("vector", p3_dbu)

                    def p3_y_t(t, g=g, sv_n=sV.n):
                        t.wait_ge(sV.h, sv_n)
                        for n in range(NST):
                            r = t.matmul(pby[:], lhsT=ident[:], rhs=hbuf[:, n, :],
                                         start=(n == 0), stop=(n == NST - 1))
                        sT.tag(r)
                    _emit("tensor", p3_y_t)

                    def p3_y_s(s, g=g, sa_n=sA.n, sv_n=sV.n):
                        s.wait_ge(sA.h, sa_n)
                        s.wait_ge(sV.h, sv_n)
                        sS.tag(s.activation(out=yb3[:, 1, :], in_=res3[:, g, :], func=AF.Silu))
                    _emit("scalar", p3_y_s)

                    def p3_y_v(v, g=g, st_n=sT.n, ss_n=sS.n):
                        v.wait_ge(sT.h, st_n)
                        v.wait_ge(sS.h, ss_n)
                        r = v.scalar_tensor_tensor(out=yb3[:, 0, :], in0=u3[:, g, :],
                                                   scalar=d_sb[:, g:g + 1], in1=pby[:],
                                                   op0=A.mult, op1=A.add)
                        sV.edge(v, r)
                        r = v.tensor_tensor(out=ygc[:, g, :], in0=yb3[:, 0, :],
                                            in1=yb3[:, 1, :], op=A.mult)
                        sV.tag(r)
                    _emit("vector", p3_y_v)

                for oc in range(NG):
                    def p3_s_t(t, oc=oc, sv_n=sV.n, ss_n=sS.n):
                        t.wait_ge(sV.h, sv_n)
                        if oc >= 1:
                            t.wait_ge(sS.h, ss_n)
                        for g in range(NG):
                            r = t.matmul(pbs[:], lhsT=w_out_sb[:, g, oc * 128:(oc + 1) * 128],
                                         rhs=ygc[:, g, :], start=(g == 0), stop=(g == NG - 1))
                        sT.tag(r)
                    _emit("tensor", p3_s_t)

                    def p3_s_s(s, oc=oc, st_n=sT.n, so_n=sO.n):
                        s.wait_ge(sT.h, st_n)
                        if oc == 0:
                            s.wait_ge(sO.h, so_n)
                        sS.tag(s.activation(out=spb[:, oc, :], in_=pbs[:], func=AF.Copy))
                    _emit("scalar", p3_s_s)

                def p3_st(sy, blk=blk, t0=t0, ss_n=sS.n):
                    sy.wait_ge(sS.h, ss_n)
                    sO.inc(sy.dma_start(out=cc_s_in[:, :, t0:t0 + TB], in_=spb[:]), 16)
                block.sync(p3_st)

            def p3_cc(g):
                sO.wait(g)
                sG.inc(g.collective_compute("AllGather", A.bypass, replica_groups=GROUPS_ALL,
                                            ins=[cc_s_in[:]], outs=[cc_s_out[:]]), 1)
            block.gpsimd(p3_cc)

            def p3_restage(sc):
                sG.wait(sc)
                p_ = pid(sc)
                role_ = _rv(binary_op(p_, make_scalar_value(1), A.bitwise_and), 1)
                hoff = _rv(binary_op(role_, make_scalar_value(HS), A.mult), HS)
                pbase = _rv(binary_op(_rv(binary_op(p_, make_scalar_value(1), A.bitwise_xor), 7),
                                      make_scalar_value(128), A.mult), 7 * 128)
                sD.inc(sc.dma_start(out=s_my[:], in_=cc_s_in[:, :, ds(hoff, HS)]), 16)
                sD.inc(sc.dma_start(out=s_pt[:], in_=cc_s_out[ds(pbase, 128), :, ds(hoff, HS)]), 16)
            block.scalar(p3_restage)

        phase_barrier()
        es6a = ExitStack()
        def sb6a(name, shape, dt=F32):
            return es6a.enter_context(nc.sbuf_tensor(name, list(shape), dt))
        qg_in = sb6a("qg_in", (128, D))
        qg_xs = sb6a("qg_xs", (128, D))
        qg_a2 = sb6a("qg_a2", (128, D))
        qg_q1 = sb6a("qg_q1", (128, D))
        qg_q2 = sb6a("qg_q2", (128, D))
        qg_sgn = sb6a("qg_sgn", (128, D))
        qg_sg2 = qg_a2
        qg_li = sb6a("qg_li", (128, D), I32)
        qg_bmr = sb6a("qg_bmr", (128, D // 16))
        qg_bmd = sb6a("qg_bmd", (128, D // 16))
        qg_out = sb6a("qg_out", (128, D), BF16)
        pq6a = es6a.enter_context(nc.psum_tensor("pq6a", [128, 1024], BF16))
        q6tag = {"ld": 0, "s1": 0, "s2": 0, "s3": 0, "s4": 0, "s5a": 0, "s5": 0,
                 "s6": 0, "s7": 0, "st": 0}
        p6_stages = []

        def p6_add_tile(wsrc, wdst, it):
            def st_ld():
                s1_prev = q6tag["s1"]; sv_n = sV.n
                def f(sy):
                    if s1_prev:
                        sy.wait_ge(sQ.h, s1_prev)
                    else:
                        sy.wait_ge(sV.h, sv_n)
                    sWs.inc(sy.dma_start(out=qg_in[:], in_=wsrc[it * 128:(it + 1) * 128, :]), 16)
                block.sync(f)
                q6tag["ld"] = sWs.n

            def st_1():
                ld_n = q6tag["ld"]; s5_prev = q6tag["s5"]
                def f(v):
                    v.wait_ge(sWs.h, ld_n)
                    if s5_prev:
                        v.wait_ge(sQ.h, s5_prev)
                    r = v.tensor_reduce(out=qg_bmd[:], in_=qg_in[:].rearrange("p (b s) -> p b s", s=16),
                                        axis=AX.X, op=A.max, apply_absolute_value=True)
                    sQ.edge(v, r)
                    r = v.reciprocal(out=qg_bmr[:], in_=qg_bmd[:])
                    sQ.edge(v, r)
                    r = v.tensor_scalar(out=qg_bmr[:], in0=qg_bmr[:], scalar1=6.0, scalar2=1e12,
                                        op0=A.mult, op1=A.min)
                    r2 = v.tensor_scalar(out=qg_bmd[:], in0=qg_bmd[:], scalar1=1.0 / 6.0, scalar2=1e-12,
                                         op0=A.mult, op1=A.max)
                    sQ.edge(v, r2)
                    r = v.tensor_tensor(out=qg_xs[:].rearrange("p (b s) -> p b s", s=16),
                                        in0=qg_in[:].rearrange("p (b s) -> p b s", s=16),
                                        in1=qg_bmr[:].unsqueeze(2).broadcast_to((128, D // 16, 16)),
                                        op=A.mult)
                    sQ.tag(r)
                _emit("vector", f)
                q6tag["s1"] = sQ.n

            def st_2():
                s1_n = q6tag["s1"]
                def f(s):
                    s.wait_ge(sQ.h, s1_n)
                    r = s.activation(out=qg_a2[:], in_=qg_xs[:], func=AF.Abs, scale=2.0)
                    r = s.activation(out=qg_sgn[:], in_=qg_xs[:], func=AF.Sign)
                    sQ.tag(r)
                _emit("scalar", f)
                q6tag["s2"] = sQ.n

            def st_3():
                s2_n = q6tag["s2"]
                def f(v):
                    v.wait_ge(sQ.h, s2_n)
                    r = v.tensor_scalar(out=qg_q1[:], in0=qg_a2[:], scalar1=0.5, scalar2=2.0,
                                        op0=A.mult, op1=A.add)
                    r = v.tensor_scalar(out=qg_q2[:], in0=qg_a2[:], scalar1=0.25, scalar2=4.0,
                                        op0=A.mult, op1=A.add)
                    sQ.edge(v, r)
                    r = v.tensor_tensor(out=qg_q1[:], in0=qg_a2[:], in1=qg_q1[:], op=A.min)
                    sQ.edge(v, r)
                    r = v.tensor_tensor(out=qg_q2[:], in0=qg_q1[:], in1=qg_q2[:], op=A.min)
                    sQ.tag(r)
                _emit("vector", f)
                q6tag["s3"] = sQ.n

            def st_4():
                s3_n = q6tag["s3"]
                def f(v):
                    v.wait_ge(sQ.h, s3_n)
                    r = v.tensor_scalar(out=qg_li[:], in0=qg_q2[:], scalar1=1.0, scalar2=None, op0=A.mult)
                    sQ.edge(v, r)
                    r = v.tensor_copy(out=qg_xs[:], in_=qg_li[:])
                    sQ.edge(v, r)
                    r = v.tensor_tensor(out=qg_sg2[:], in0=qg_q2[:], in1=qg_xs[:], op=A.subtract)
                    sQ.edge(v, r)
                    r = v.tensor_scalar(out=qg_sg2[:], in0=qg_sg2[:], scalar1=0.5, scalar2=None, op0=A.is_ge)
                    sQ.edge(v, r)
                    r = v.tensor_tensor(out=qg_xs[:], in0=qg_xs[:], in1=qg_sg2[:], op=A.add)
                    sQ.tag(r)
                _emit("vector", f)
                q6tag["s4"] = sQ.n

            def st_5a():
                s4_n = q6tag["s4"]
                def f(v):
                    v.wait_ge(sQ.h, s4_n)
                    r = v.tensor_scalar(out=qg_q1[:], in0=qg_xs[:], scalar1=0.5, scalar2=None, op0=A.mult)
                    r = v.tensor_scalar(out=qg_q2[:], in0=qg_xs[:], scalar1=1.0, scalar2=-2.0,
                                        op0=A.mult, op1=A.add)
                    r = v.tensor_scalar(out=qg_a2[:], in0=qg_xs[:], scalar1=2.0, scalar2=-8.0,
                                        op0=A.mult, op1=A.add)
                    sQ.tag(r)
                _emit("vector", f)
                q6tag["s5a"] = sQ.n

            def st_5b():
                s5a_n = q6tag["s5a"]; st_prev = q6tag["st"]
                def f(v):
                    v.wait_ge(sQ.h, s5a_n)
                    if st_prev:
                        v.wait_ge(sY.h, st_prev)
                    r = v.tensor_tensor(out=qg_q1[:], in0=qg_q1[:], in1=qg_q2[:], op=A.max)
                    sQ.edge(v, r)
                    r = v.tensor_tensor(out=qg_q1[:], in0=qg_q1[:], in1=qg_a2[:], op=A.max)
                    sQ.edge(v, r)
                    r = v.tensor_tensor(out=qg_q1[:], in0=qg_q1[:], in1=qg_sgn[:], op=A.mult)
                    sQ.edge(v, r)
                    r = v.tensor_tensor(out=qg_out[:].rearrange("p (b s) -> p b s", s=16),
                                        in0=qg_q1[:].rearrange("p (b s) -> p b s", s=16),
                                        in1=qg_bmd[:].unsqueeze(2).broadcast_to((128, D // 16, 16)),
                                        op=A.mult)
                    sQ.tag(r)
                _emit("vector", f)
                q6tag["s5"] = sQ.n

            def st_6():
                s5_n = q6tag["s5"]
                def f(t):
                    t.wait_ge(sQ.h, s5_n)
                    for g in range(NG):
                        r = t.transpose(pq6a[:, g * 128:(g + 1) * 128],
                                        qg_out[:, g * 128:(g + 1) * 128], ident[:])
                    sQ.tag(r)
                _emit("tensor", f)
                q6tag["s6"] = sQ.n

            def st_7():
                s6_n = q6tag["s6"]
                def f(s):
                    s.wait_ge(sQ.h, s6_n)
                    r = s.activation(out=qg_out[:], in_=pq6a[:], func=AF.Copy)
                    sQ.tag(r)
                _emit("scalar", f)
                q6tag["s7"] = sQ.n

            def st_st():
                s7_n = q6tag["s7"]
                def f(sy, wdst=wdst, it=it):
                    sy.wait_ge(sQ.h, s7_n)
                    sY.inc(sy.dma_start(
                        out=wdst[:, :, it * 128:(it + 1) * 128],
                        in_=qg_out[:].rearrange("p (g c) -> p g c", c=128)), 16)
                block.sync(f)
                q6tag["st"] = sY.n

            p6_stages.extend([st_ld, st_1, st_2, st_3, st_4, st_5a, st_5b,
                              st_6, st_7, st_st])

        for wsrc6, wdst6 in ((l1_wg, wgq_spill), (l1_wu, wuq_spill)):
            for it6 in range(NFI):
                p6_add_tile(wsrc6, wdst6, it6)

        def p6_tick():
            if p6_stages:
                p6_stages.pop(0)()
        tick_hook[0] = p6_tick

        dbg_out("x1", (128, 16, D))
        dbg_out("xn1", (128, 16, D))
        es4 = ExitStack()
        def sb4(name, shape, dt=F32):
            return es4.enter_context(nc.sbuf_tensor(name, list(shape), dt))
        wo_sb = sb4("wo_sb", (128, NG, D), BF16)
        xnob = sb4("xnob", (128, NG, TB), BF16)
        ao8a = sb4("ao8a", (128, 4, TB), BF16)
        ao8b = sb4("ao8b", (128, 4, TB), BF16)
        smine = sb4("smine", (128, NG, TB), BF16)
        spart = sb4("spart", (128, NG, TB), BF16)
        hb4 = sb4("hb4", (128, NG, TB))
        rn2 = sb4("rn2", (128, NG, TB), BF16)
        sc4 = sb4("sc4", (128, 2, TB))
        lg4 = sb4("lg4", (128, 4, 8))
        cmb = sb4("cmb", (128, 4, 8))
        cmbw = sb4("cmbw", (128, 4, 4))
        combr = sb4("combr", (8, TB), BF16)
        cbsb = sb4("cbsb", (128, TB), BF16)
        ew = sb4("ew", (128, 2, NG, D), BF16)
        macc = sb4("macc", (128, NG, TB))
        strow = macc[0:1, 0, :]
        gs4 = sb4("gs4", (128, NG, TB), BF16)
        sg4 = sb4("sg4", (128, NG, TB), BF16)
        x0t = sb4("x0t", (128, 2, D))
        x1t = sb4("x1t", (128, D))
        xn1t = sb4("xn1t", (128, D), BF16)
        st4 = sb4("st4", (128, 8))
        ph0 = es4.enter_context(nc.psum_tensor("ph0", [128, TB], F32))
        ph1 = es4.enter_context(nc.psum_tensor("ph1", [128, TB], F32))
        pst4 = es4.enter_context(nc.psum_tensor("pst4", [128, TB], F32))
        ppk4 = es4.enter_context(nc.psum_tensor("ppk4", [128, 1024], BF16))
        plg4 = es4.enter_context(nc.psum_tensor("plg4", [128, 64], F32))
        pct4 = es4.enter_context(nc.psum_tensor("pct4", [16, 128], F32))
        with es4:
            def p4_w(g):
                sWg.inc(g.dma_start(out=wo_sb[:], in_=woT.rearrange("(g p) c -> p g c", p=128)), 16)
            block.gpsimd(p4_w)

            for ob in range(OBLK):
                ot = ob * TB

                def p4_ld(sy, ob=ob, ot=ot, sv_n=sV.n, ss_n=sS.n, st_n=sT.n, sd_n=sD.n, so_n=sO.n):
                    sy.wait_ge(sV.h, sv_n)
                    sy.wait_ge(sS.h, ss_n)
                    sy.wait_ge(sT.h, st_n)
                    sy.wait_ge(sD.h, sd_n)
                    sy.wait_ge(sO.h, so_n)
                    sA.inc(sy.dma_start(out=xnob[:], in_=cc_xn_in.rearrange(
                        "(g p) t -> p g t", p=128)[:, :, ot:ot + TB]), 16)
                    sA.inc(sy.dma_start(out=ao8a[:], in_=ao_my[:, :, ot:ot + TB]), 16)
                    sA.inc(sy.dma_start(out=ao8b[:], in_=ao_pt[:, :, ot:ot + TB]), 16)
                    sA.inc(sy.dma_start(out=smine[:], in_=s_my[:, :, ot:ot + TB]), 16)
                    sA.inc(sy.dma_start(out=spart[:], in_=s_pt[:, :, ot:ot + TB]), 16)
                block.sync(p4_ld)

                for g in range(NG):
                    ph = ph0 if g % 2 == 0 else ph1
                    def p4_h_t(t, g=g, ph=ph, sa_n=sA.n, swg=sWg.n, sv_n=sV.n):
                        t.wait_ge(sA.h, sa_n)
                        t.wait_ge(sWg.h, swg)
                        if g >= 2:
                            t.wait_ge(sV.h, sv_n)
                        for j in range(NG):
                            rhs_ = ao8a[:, j, :] if j < 4 else ao8b[:, j - 4, :]
                            r = t.matmul(ph[:], lhsT=wo_sb[:, j, g * 128:(g + 1) * 128],
                                         rhs=rhs_, start=(j == 0), stop=(j == NG - 1))
                        sT.tag(r)
                    _emit("tensor", p4_h_t)

                    def p4_h_v(v, g=g, ph=ph, st_n=sT.n):
                        v.wait_ge(sT.h, st_n)
                        r = v.tensor_tensor(out=hb4[:, g, :], in0=ph[:], in1=xnob[:, g, :], op=A.add)
                        sV.tag(r)
                    _emit("vector", p4_h_v)

                for g in range(NG):
                    def p4_sq_s(s, g=g, sv_n=sV.n, st_n=sT.n):
                        s.wait_ge(sV.h, sv_n)
                        if g >= 2:
                            s.wait_ge(sT.h, st_n)
                        sS.tag(s.activation(out=sc4[:, g % 2, :], in_=hb4[:, g, :], func=AF.Square))
                    _emit("scalar", p4_sq_s)

                    def p4_st_tg(t, g=g, ss_n=sS.n):
                        t.wait_ge(sS.h, ss_n)
                        r = t.matmul(pst4[0:1, :], lhsT=onesf[:, 0:1], rhs=sc4[:, g % 2, :],
                                     start=(g == 0), stop=(g == NG - 1))
                        sT.tag(r)
                    _emit("tensor", p4_st_tg)

                def p4_st_v(v, st_n=sT.n):
                    v.wait_ge(sT.h, st_n)
                    r = v.tensor_scalar(out=strow[0:1, :], in0=pst4[0:1, :],
                                        scalar1=1.0 / D, scalar2=EPS_RMS, op0=A.mult, op1=A.add)
                    sV.tag(r)
                _emit("vector", p4_st_v)

                def p4_st_s(s, sv_n=sV.n):
                    s.wait_ge(sV.h, sv_n)
                    sS.tag(s.activation(out=strow[0:1, :], in_=strow[0:1, :], func=AF.Sqrt))
                _emit("scalar", p4_st_s)

                def p4_st_v2(v, ss_n=sS.n):
                    v.wait_ge(sS.h, ss_n)
                    sV.tag(v.reciprocal(out=strow[0:1, :], in_=strow[0:1, :]))
                _emit("vector", p4_st_v2)

                def p4_r2_t(t, sv_n=sV.n):
                    t.wait_ge(sV.h, sv_n)
                    r = t.matmul(pst4[:], lhsT=onesf[0:1, :], rhs=strow[0:1, :],
                                 start=True, stop=True)
                    sT.tag(r)
                _emit("tensor", p4_r2_t)

                def p4_rn2_v(v, st_n=sT.n):
                    v.wait_ge(sT.h, st_n)
                    for g in range(NG):
                        r = v.scalar_tensor_tensor(out=rn2[:, g, :], in0=hb4[:, g, :],
                                                   scalar=fnw_sb[:, g:g + 1], in1=pst4[:],
                                                   op0=A.mult, op1=A.mult)
                    sV.tag(r)
                _emit("vector", p4_rn2_v)

                for sub in range(4):
                    def p4_lg_t(t, sub=sub, sv_n=sV.n):
                        t.wait_ge(sV.h, sv_n)
                        for g in range(NG):
                            r = t.matmul(plg4[:, sub * 8:(sub + 1) * 8],
                                         lhsT=rn2[:, g, sub * 128:(sub + 1) * 128],
                                         rhs=gw_sb[:, g, :], start=(g == 0), stop=(g == NG - 1))
                        sT.tag(r)
                    _emit("tensor", p4_lg_t)

                def p4_lg_v(v, st_n=sT.n):
                    v.wait_ge(sT.h, st_n)
                    for sub in range(4):
                        r = v.tensor_copy(out=lg4[:, sub, :], in_=plg4[:, sub * 8:(sub + 1) * 8])
                    sV.edge(v, r)
                    r = v.tensor_reduce(out=cmbw[:, :, 0:1], in_=lg4[:], axis=AX.X, op=A.max)
                    sV.edge(v, r)
                    r = v.tensor_tensor(out=cmb[:], in0=lg4[:],
                                        in1=cmbw[:, :, 0:1].broadcast_to((128, 4, 8)), op=A.is_equal)
                    sV.edge(v, r)
                    r = v.scalar_tensor_tensor(out=sc4[:, 0, 0:32].rearrange("p (a b) -> p a b", b=8),
                                               in0=cmb[:], scalar=-1e30, in1=lg4[:],
                                               op0=A.mult, op1=A.add)
                    sV.edge(v, r)
                    r = v.tensor_reduce(out=cmbw[:, :, 1:2],
                                        in_=sc4[:, 0, 0:32].rearrange("p (a b) -> p a b", b=8),
                                        axis=AX.X, op=A.max)
                    sV.edge(v, r)
                    r = v.tensor_tensor(out=cmbw[:, :, 2:3], in0=cmbw[:, :, 0:1],
                                        in1=cmbw[:, :, 1:2], op=A.subtract)
                    sV.tag(r)
                _emit("vector", p4_lg_v)

                def p4_sig_s(s, sv_n=sV.n):
                    s.wait_ge(sV.h, sv_n)
                    sS.tag(s.activation(out=cmbw[:, :, 2:3], in_=cmbw[:, :, 2:3], func=AF.Sigmoid))
                _emit("scalar", p4_sig_s)

                def p4_cmb_v(v, ss_n=sS.n):
                    v.wait_ge(sS.h, ss_n)
                    r = v.tensor_scalar(out=cmbw[:, :, 3:4], in0=cmbw[:, :, 2:3],
                                        scalar1=-1.0, scalar2=1.0, op0=A.mult, op1=A.add)
                    sV.edge(v, r)
                    r = v.tensor_tensor(out=sc4[:, 1, 0:32].rearrange("p (a b) -> p a b", b=8),
                                        in0=lg4[:], in1=cmbw[:, :, 1:2].broadcast_to((128, 4, 8)),
                                        op=A.is_equal)
                    sV.edge(v, r)
                    r = v.tensor_tensor(out=cmb[:], in0=cmb[:],
                                        in1=cmbw[:, :, 2:3].broadcast_to((128, 4, 8)), op=A.mult)
                    sV.edge(v, r)
                    r = v.tensor_tensor(out=sc4[:, 1, 0:32].rearrange("p (a b) -> p a b", b=8),
                                        in0=sc4[:, 1, 0:32].rearrange("p (a b) -> p a b", b=8),
                                        in1=cmbw[:, :, 3:4].broadcast_to((128, 4, 8)), op=A.mult)
                    sV.edge(v, r)
                    r = v.tensor_tensor(out=cmb[:], in0=cmb[:],
                                        in1=sc4[:, 1, 0:32].rearrange("p (a b) -> p a b", b=8), op=A.add)
                    sV.tag(r)
                _emit("vector", p4_cmb_v)

                for sub in range(4):
                    def p4_ct_t(t, sub=sub, sv_n=sV.n):
                        t.wait_ge(sV.h, sv_n)
                        sT.tag(t.transpose(pct4[0:8, :], cmb[:, sub, :], identf[:]))
                    _emit("tensor", p4_ct_t)

                    def p4_ct_v(v, sub=sub, st_n=sT.n):
                        v.wait_ge(sT.h, st_n)
                        sV.tag(v.tensor_copy(out=combr[:, sub * 128:(sub + 1) * 128],
                                             in_=pct4[0:8, :]))
                    _emit("vector", p4_ct_v)

                for e in range(8):
                    def p4_ew(g2, e=e, ob=ob, st_n=sT.n):
                        g2.wait_ge(sT.h, st_n)
                        sWg.inc(g2.dma_start(out=ew[:, e % 2, :, :], in_=expert_wT.rearrange(
                            "e (g p) c -> e p g c", p=128)[e]), 16)
                    block.gpsimd(p4_ew)

                    def p4_cb_t(t, e=e, sv_n=sV.n, ss_n=sS.n):
                        t.wait_ge(sV.h, sv_n)
                        if e >= 1:
                            t.wait_ge(sS.h, ss_n)
                        r = t.matmul(pst4[:], lhsT=hot8b[:, e * 128:(e + 1) * 128], rhs=combr[:],
                                     start=True, stop=True)
                        sT.tag(r)
                    _emit("tensor", p4_cb_t)

                    def p4_cb_s(s, e=e, st_n=sT.n):
                        s.wait_ge(sT.h, st_n)
                        sS.tag(s.activation(out=cbsb[:], in_=pst4[:], func=AF.Copy))
                    _emit("scalar", p4_cb_s)

                    for oc in range(NG):
                        ph = ph0 if oc % 2 == 0 else ph1
                        def p4_e_t(t, e=e, oc=oc, ph=ph, swg=sWg.n, sv_n=sV.n):
                            t.wait_ge(sWg.h, swg)
                            if e + oc > 0:
                                t.wait_ge(sV.h, sv_n)
                            for g in range(NG):
                                r = t.matmul(ph[:], lhsT=ew[:, e % 2, g, oc * 128:(oc + 1) * 128],
                                             rhs=rn2[:, g, :], start=(g == 0), stop=(g == NG - 1))
                            sT.tag(r)
                        _emit("tensor", p4_e_t)

                        def p4_e_v(v, e=e, oc=oc, ph=ph, st_n=sT.n, ss_n=sS.n):
                            v.wait_ge(sT.h, st_n)
                            v.wait_ge(sS.h, ss_n)
                            if e == 0:
                                r = v.tensor_tensor(out=macc[:, oc, :], in0=ph[:], in1=cbsb[:], op=A.mult)
                            else:
                                r = v.tensor_tensor(out=sc4[:, 0, :], in0=ph[:], in1=cbsb[:], op=A.mult)
                                sV.edge(v, r)
                                r = v.tensor_tensor(out=macc[:, oc, :], in0=macc[:, oc, :],
                                                    in1=sc4[:, 0, :], op=A.add)
                            sV.tag(r)
                        _emit("vector", p4_e_v)

                def p4_out_v(v):
                    for g in range(NG):
                        r = v.tensor_tensor(out=macc[:, g, :], in0=macc[:, g, :], in1=hb4[:, g, :], op=A.add)
                    sV.tag(r)
                _emit("vector", p4_out_v)

                def p4_gs_s(s, sv_n=sV.n):
                    s.wait_ge(sV.h, sv_n)
                    for g in range(NG):
                        r = s.activation(out=gs4[:, g, :], in_=macc[:, g, :], func=AF.Silu)
                    sS.tag(r)
                _emit("scalar", p4_gs_s)

                def p4_sg_v(v, ss_n=sS.n, sa_n=sA.n):
                    v.wait_ge(sS.h, ss_n)
                    v.wait_ge(sA.h, sa_n)
                    for g in range(NG):
                        r = v.tensor_tensor(out=macc[:, g, :], in0=smine[:, g, :], in1=spart[:, g, :], op=A.add)
                    sV.edge(v, r)
                    for g in range(NG):
                        r = v.tensor_tensor(out=sg4[:, g, :], in0=macc[:, g, :], in1=gs4[:, g, :], op=A.mult)
                    sV.tag(r)
                _emit("vector", p4_sg_v)

                for sub in range(4):
                    def p4_x0_ld(sy, ob=ob, sub=sub, so_n=sO.n, sv_n=sV.n):
                        sy.wait_ge(sO.h, so_n)
                        sy.wait_ge(sV.h, sv_n)
                        sX.inc(sy.dma_start(out=x0t[:, sub % 2, :], in_=x_half.rearrange(
                            "(n p) d -> p n d", p=128)[:, ob * 4 + sub, :]), 16)
                    block.sync(p4_x0_ld)

                    def p4_tr_t(t, sub=sub, sv_n=sV.n, ss_n=sS.n):
                        t.wait_ge(sV.h, sv_n)
                        if sub > 0:
                            t.wait_ge(sS.h, ss_n)
                        for g in range(NG):
                            r = t.transpose(ppk4[:, g * 128:(g + 1) * 128],
                                            sg4[:, g, sub * 128:(sub + 1) * 128], ident[:])
                        sT.tag(r)
                    _emit("tensor", p4_tr_t)

                    def p4_x1_v(v, sub=sub, ob=ob, st_n=sT.n, so_n=sO.n, sx_n=sX.n):
                        v.wait_ge(sT.h, st_n)
                        v.wait_ge(sO.h, so_n)
                        v.wait_ge(sX.h, sx_n)
                        r = v.tensor_tensor(out=x1t[:], in0=x0t[:, sub % 2, :], in1=ppk4[:], op=A.add)
                        sV.edge(v, r)
                        r = v.reduce_sum(out=st4[:, 0:1], in_=x1t[:], axis=AX.X)
                        sV.tag(r)
                    _emit("vector", p4_x1_v)

                    def p4_ln_s(s, sub=sub, sv_n=sV.n):
                        s.wait_ge(sV.h, sv_n)
                        sS.tag(s.activation(out=xn1t[:], in_=x1t[:], func=AF.Square,
                                            accum_out=st4[:, 1:2]))
                    _emit("scalar", p4_ln_s)

                    def p4_ln_v(v, sub=sub, ss_n=sS.n):
                        v.wait_ge(sS.h, ss_n)
                        r = v.tensor_scalar(out=st4[:, 2:3], in0=st4[:, 0:1], scalar1=1.0 / D, scalar2=None, op0=A.mult)
                        sV.edge(v, r)
                        r = v.tensor_scalar(out=st4[:, 3:4], in0=st4[:, 1:2], scalar1=1.0 / D, scalar2=None, op0=A.mult)
                        sV.edge(v, r)
                        r = v.tensor_tensor(out=st4[:, 4:5], in0=st4[:, 2:3], in1=st4[:, 2:3], op=A.mult)
                        sV.edge(v, r)
                        r = v.tensor_tensor(out=st4[:, 4:5], in0=st4[:, 3:4], in1=st4[:, 4:5], op=A.subtract)
                        sV.edge(v, r)
                        r = v.tensor_scalar(out=st4[:, 4:5], in0=st4[:, 4:5], scalar1=EPS_LN, scalar2=None, op0=A.add)
                        sV.tag(r)
                    _emit("vector", p4_ln_v)

                    def p4_ln_s2(s, sv_n=sV.n):
                        s.wait_ge(sV.h, sv_n)
                        sS.tag(s.activation(out=st4[:, 5:6], in_=st4[:, 4:5], func=AF.Sqrt))
                    _emit("scalar", p4_ln_s2)

                    def p4_ln_v2(v, ss_n=sS.n):
                        v.wait_ge(sS.h, ss_n)
                        r = v.reciprocal(out=st4[:, 6:7], in_=st4[:, 5:6])
                        sV.edge(v, r)
                        r = v.tensor_scalar(out=xn1t[:], in0=x1t[:], scalar1=st4[:, 2:3],
                                            scalar2=st4[:, 6:7], op0=A.subtract, op1=A.mult)
                        sV.edge(v, r)
                        r = v.tensor_scalar(out=x0t[:, sub % 2, :], in0=x1t[:], scalar1=st4[:, 2:3],
                                            scalar2=st4[:, 6:7], op0=A.subtract, op1=A.mult)
                        sV.tag(r)
                    _emit("vector", p4_ln_v2)

                    def p4_xn1_t(t, sub=sub, ob=ob, sv_n=sV.n, ss_n=sS.n):
                        t.wait_ge(sV.h, sv_n)
                        for g in range(NG):
                            r = t.transpose(ppk4[:, g * 128:(g + 1) * 128],
                                            xn1t[:, g * 128:(g + 1) * 128], ident[:])
                        sT.tag(r)
                    _emit("tensor", p4_xn1_t)

                    def p4_xn1_s(s, sub=sub, ob=ob, st_n=sT.n, so_n=sO.n):
                        s.wait_ge(sT.h, st_n)
                        s.wait_ge(sO.h, so_n)
                        r = s.activation(out=gs4[:, 0:2, :].rearrange("p g t -> p (g t)"),
                                         in_=ppk4[:], func=AF.Copy)
                        sS.tag(r)
                    _emit("scalar", p4_xn1_s)

                    def p4_st_dma(sy, sub=sub, ob=ob, ss_n=sS.n, sv_n=sV.n):
                        sy.wait_ge(sS.h, ss_n)
                        sy.wait_ge(sV.h, sv_n)
                        tt = ob * 4 + sub
                        if "x1" in debug:
                            sO.inc(sy.dma_start(out=dbg["x1"][:, tt, :], in_=x1t[:]), 16)
                        if "xn1" in debug:
                            sO.inc(sy.dma_start(out=dbg["xn1"][:, tt, :], in_=x0t[:, sub % 2, :]), 16)
                        sO.inc(sy.dma_start(out=x1_spill[:, tt, :], in_=x1t[:]), 16)
                        sO.inc(sy.dma_start(out=xn1f_spill[:, tt, :], in_=x0t[:, sub % 2, :]), 16)
                        sO.inc(sy.dma_start(out=cc_xn1_in.rearrange("(g p) t -> p g t", p=128)[
                            :, :, ob * TB + sub * 128:ob * TB + (sub + 1) * 128],
                            in_=gs4[:, 0:2, :].rearrange("p g t -> p (g t)").rearrange(
                                "p (g c) -> p g c", c=128)), 16)
                    block.sync(p4_st_dma)

            def p4_cc(g):
                sO.wait(g)
                sG.inc(g.collective_compute("AllGather", A.bypass, replica_groups=GROUPS_ALL,
                                            ins=[cc_xn1_in[:]], outs=[cc_xn1_out[:]]), 1)
            block.gpsimd(p4_cc)

        tick_hook[0] = None
        while p6_stages:
            p6_stages.pop(0)()
        es6a.close()

        phase_barrier()
        def emit_quant(get_bufs, in_ap, out_ap, F, sv0=None):
            B = get_bufs
            nb = F // 16

            def qv1(v, sv0=sv0):
                if sv0 is not None:
                    v.wait_ge(sV.h, sv0)
                r = v.tensor_reduce(out=B["bm"], in_=in_ap.rearrange("p (b s) -> p b s", s=16),
                                    axis=AX.X, op=A.max, apply_absolute_value=True)
                sV.edge(v, r)
                r = v.reciprocal(out=B["bmr"], in_=B["bm"])
                sV.edge(v, r)
                r = v.tensor_scalar(out=B["bmr"], in0=B["bmr"], scalar1=6.0, scalar2=1e12,
                                    op0=A.mult, op1=A.min)
                r2 = v.tensor_scalar(out=B["bmd"], in0=B["bm"], scalar1=1.0 / 6.0, scalar2=1e-12,
                                     op0=A.mult, op1=A.max)
                sV.edge(v, r2)
                r = v.tensor_tensor(out=B["xs"].rearrange("p (b s) -> p b s", s=16),
                                    in0=in_ap.rearrange("p (b s) -> p b s", s=16),
                                    in1=B["bmr"].unsqueeze(2).broadcast_to((128, nb, 16)),
                                    op=A.mult)
                sV.tag(r)
            _emit("vector", qv1)

            def qs1(s, sv_n=sV.n):
                s.wait_ge(sV.h, sv_n)
                r = s.activation(out=B["a2"], in_=B["xs"], func=AF.Abs, scale=2.0)
                sS.edge(s, r)
                r = s.activation(out=B["sgn"], in_=B["xs"], func=AF.Sign)
                sS.tag(r)
            _emit("scalar", qs1)

            def qv2(v, ss_n=sS.n):
                v.wait_ge(sS.h, ss_n)
                r = v.tensor_scalar(out=B["q1"], in0=B["a2"], scalar1=0.5, scalar2=2.0, op0=A.mult, op1=A.add)
                r = v.tensor_scalar(out=B["q2"], in0=B["a2"], scalar1=0.25, scalar2=4.0, op0=A.mult, op1=A.add)
                sV.edge(v, r)
                r = v.tensor_tensor(out=B["q1"], in0=B["a2"], in1=B["q1"], op=A.min)
                sV.edge(v, r)
                r = v.tensor_tensor(out=B["q2"], in0=B["q1"], in1=B["q2"], op=A.min)
                sV.edge(v, r)
                r = v.tensor_scalar(out=B["li"], in0=B["q2"], scalar1=1.0, scalar2=None, op0=A.mult)
                sV.edge(v, r)
                r = v.tensor_copy(out=B["xs"], in_=B["li"])
                sV.edge(v, r)
                r = v.tensor_tensor(out=B["sgn2"], in0=B["q2"], in1=B["xs"], op=A.subtract)
                sV.edge(v, r)
                r = v.tensor_scalar(out=B["sgn2"], in0=B["sgn2"], scalar1=0.5, scalar2=None, op0=A.is_ge)
                sV.edge(v, r)
                r = v.tensor_tensor(out=B["xs"], in0=B["xs"], in1=B["sgn2"], op=A.add)
                sV.edge(v, r)
                r = v.tensor_scalar(out=B["q1"], in0=B["xs"], scalar1=0.5, scalar2=None, op0=A.mult)
                r = v.tensor_scalar(out=B["q2"], in0=B["xs"], scalar1=1.0, scalar2=-2.0, op0=A.mult, op1=A.add)
                r = v.tensor_scalar(out=B["a2"], in0=B["xs"], scalar1=2.0, scalar2=-8.0, op0=A.mult, op1=A.add)
                sV.edge(v, r)
                r = v.tensor_tensor(out=B["q1"], in0=B["q1"], in1=B["q2"], op=A.max)
                sV.edge(v, r)
                r = v.tensor_tensor(out=B["q1"], in0=B["q1"], in1=B["a2"], op=A.max)
                sV.edge(v, r)
                r = v.tensor_tensor(out=B["q1"], in0=B["q1"], in1=B["sgn"], op=A.mult)
                sV.edge(v, r)
                r = v.tensor_tensor(out=out_ap.rearrange("p (b s) -> p b s", s=16),
                                    in0=B["q1"].rearrange("p (b s) -> p b s", s=16),
                                    in1=B["bmd"].unsqueeze(2).broadcast_to((128, nb, 16)),
                                    op=A.mult)
                sV.tag(r)
            _emit("vector", qv2)

        es6 = ExitStack()
        def sb6(name, shape, dt=F32):
            return es6.enter_context(nc.sbuf_tensor(name, list(shape), dt))
        qw_in = sb6("qw_in", (128, FFI))
        qw_xs = sb6("qw_xs", (128, FFI))
        qw_a2 = sb6("qw_a2", (128, FFI))
        qw_q1 = sb6("qw_q1", (128, FFI))
        qw_q2 = sb6("qw_q2", (128, FFI))
        qw_sgn = sb6("qw_sgn", (128, FFI))
        qw_sg2 = sb6("qw_sg2", (128, FFI))
        qw_li = sb6("qw_li", (128, FFI), I32)
        qw_bm = sb6("qw_bm", (128, FFI // 16))
        qw_bmr = sb6("qw_bmr", (128, FFI // 16))
        qw_bmd = sb6("qw_bmd", (128, FFI // 16))
        qw_out = sb6("qw_out", (128, FFI), BF16)
        qw_tr = sb6("qw_tr", (128, FFI), BF16)
        pq6 = es6.enter_context(nc.psum_tensor("pq6", [128, 1024], BF16))
        pq6b = es6.enter_context(nc.psum_tensor("pq6b", [128, 1024], BF16))
        with es6:
            def qbufs(F):
                return {"xs": qw_xs[:, 0:F], "a2": qw_a2[:, 0:F], "q1": qw_q1[:, 0:F],
                        "q2": qw_q2[:, 0:F], "sgn": qw_sgn[:, 0:F], "sgn2": qw_sg2[:, 0:F],
                        "li": qw_li[:, 0:F],
                        "bm": qw_bm[:, 0:F // 16], "bmr": qw_bmr[:, 0:F // 16],
                        "bmd": qw_bmd[:, 0:F // 16]}

            for it in range(NG):
                def q6_ld2(sy, it=it, sv_n=sV.n):
                    sy.wait_ge(sV.h, sv_n)
                    sA.inc(sy.dma_start(out=qw_in[:], in_=l1_wd[it]), 16)
                block.sync(q6_ld2)

                def q6_w2(v, sa_n=sA.n):
                    v.wait_ge(sA.h, sa_n)
                    sV.bump(v)
                _emit("vector", q6_w2)
                emit_quant(qbufs(FFI), qw_in[:], qw_out[:], FFI)

                for half in range(2):
                    def q6_t2(t, it=it, half=half, sv_n=sV.n, ss_n=sS.n):
                        t.wait_ge(sV.h, sv_n)
                        t.wait_ge(sS.h, ss_n)
                        nn = 11
                        for g in range(nn):
                            r = t.transpose(pq6[:, (g % 8) * 128:(g % 8 + 1) * 128] if False else
                                            pq6[:, g * 128:(g + 1) * 128] if g < 8 else pq6b[:, (g - 8) * 128:(g - 7) * 128],
                                            qw_out[:, (half * 11 + g) * 128:(half * 11 + g + 1) * 128], ident[:])
                        sT.tag(r)
                    _emit("tensor", q6_t2)

                    def q6_s2(s, it=it, half=half, st_n=sT.n, so_n=sO.n):
                        s.wait_ge(sT.h, st_n)
                        s.wait_ge(sO.h, so_n)
                        s.activation(out=qw_tr[:, 0:1024], in_=pq6[:], func=AF.Copy)
                        r = s.activation(out=qw_tr[:, 1024:1408], in_=pq6b[:, 0:384], func=AF.Copy)
                        sS.tag(r)
                    _emit("scalar", q6_s2)

                    def q6_st2(sy, it=it, half=half, ss_n=sS.n):
                        sy.wait_ge(sS.h, ss_n)
                        sO.inc(sy.dma_start(
                            out=wdq_spill[:, half * 11:(half + 1) * 11, it * 128:(it + 1) * 128],
                            in_=qw_tr[:, 0:1408].rearrange("p (g c) -> p g c", c=128)), 16)
                    block.sync(q6_st2)

        phase_barrier()
        es7 = ExitStack()
        def sb7(name, shape, dt=F32):
            return es7.enter_context(nc.sbuf_tensor(name, list(shape), dt))
        l1q_sb = sb7("l1q_sb", (128, NG, 512), BF16)
        l1k_sb = sb7("l1k_sb", (128, NG, 512), BF16)
        l1v_sb = sb7("l1v_sb", (128, NG, 512), BF16)
        xn1b = sb7("xn1b", (128, NG, TB), BF16)
        k1 = sb7("k1", (128, 4, S), BF16)
        v1 = sb7("v1", (128, 32, 520), BF16)
        qb7 = sb7("qb7", (128, 4, TB), BF16)
        atb7 = sb7("atb7", (128, 2, 2, TB), BF16)
        recd7 = sb7("recd7", (128, 2, TB))
        ib7 = sb7("ib7", (64, 2, TB))
        nb7 = sb7("nb7", (64, 2, TB), BF16)
        aob7 = sb7("aob7", (128, 4, TB), BF16)
        psA7 = es7.enter_context(nc.psum_tensor("psA7", [128, TB], F32))
        psB7 = es7.enter_context(nc.psum_tensor("psB7", [128, TB], F32))
        psC7 = es7.enter_context(nc.psum_tensor("psC7", [128, TB], F32))
        psD7 = es7.enter_context(nc.psum_tensor("psD7", [128, TB], F32))
        av07 = es7.enter_context(nc.psum_tensor("av07", [128, TB], F32))
        av17 = es7.enter_context(nc.psum_tensor("av17", [128, TB], F32))
        psM7 = es7.enter_context(nc.psum_tensor("psM7", [128, TB], F32))
        with es7:
            def p7_init(g):
                sWg.inc(g.dma_start(out=l1q_sb[:], in_=l1_wq.rearrange("(g p) c -> p g c", p=128)), 16)
                sWg.inc(g.dma_start(out=l1k_sb[:], in_=l1_wk.rearrange("(g p) c -> p g c", p=128)), 16)
                sWg.inc(g.dma_start(out=l1v_sb[:], in_=l1_wv.rearrange("(g p) c -> p g c", p=128)), 16)
                for ht in range(4):
                    g.memset(v1[:, :, ht * 130 + 64:ht * 130 + 65], 1.0)
                    r = g.memset(v1[:, :, ht * 130 + 129:ht * 130 + 130], 1.0)
                p_ = pid(g)
                b0 = _rv(binary_op(_rv(binary_op(p_, make_scalar_value(~1 & 0xff), A.bitwise_and), 6),
                                   make_scalar_value(D), A.mult), 6 * D)
                b1 = _rv(binary_op(_rv(binary_op(p_, make_scalar_value(1), A.bitwise_or), 7),
                                   make_scalar_value(D), A.mult), 7 * D)
                sWg.inc(g.dma_start(out=xn1_loc[:, 0:HS], in_=cc_xn1_out[ds(b0, D), :]), 16)
                sWg.inc(g.dma_start(out=xn1_loc[:, HS:S], in_=cc_xn1_out[ds(b1, D), :]), 16)
                sG.tag(r)
            block.gpsimd(p7_init)

            for i in range(NBLK):
                t0 = i * TB

                def p7_ld(sy, i=i, t0=t0, sv_n=sV.n, ss_n=sS.n, swg=sWg.n):
                    sy.wait_ge(sV.h, sv_n)
                    sy.wait_ge(sS.h, ss_n)
                    sy.wait_ge(sWg.h, swg)
                    sA.inc(sy.dma_start(out=xn1b[:], in_=xn1_loc.rearrange(
                        "(g p) t -> p g t", p=128)[:, :, t0:t0 + TB]), 16)
                block.sync(p7_ld)

                for hp in range(4):
                    pq = psA7 if hp % 2 == 0 else psB7
                    def p7_q_t(t, hp=hp, pq=pq, sa_n=sA.n, ss_n=sS.n, swg=sWg.n):
                        t.wait_ge(sA.h, sa_n)
                        t.wait_ge(sWg.h, swg)
                        t.wait_ge(sS.h, ss_n)
                        for g in range(NG):
                            r = t.matmul(pq[:], lhsT=l1q_sb[:, g, hp * 128:(hp + 1) * 128],
                                         rhs=xn1b[:, g, :], start=(g == 0), stop=(g == NG - 1))
                        sT.tag(r)
                    _emit("tensor", p7_q_t)

                    def p7_q_s(s, hp=hp, pq=pq, st_n=sT.n):
                        s.wait_ge(sT.h, st_n)
                        sS.tag(s.activation(out=qb7[:, hp, :], in_=pq[:], func=AF.Copy))
                    _emit("scalar", p7_q_s)

                for hp in range(4):
                    pq = psA7 if hp % 2 == 0 else psB7
                    def p7_k_t(t, hp=hp, pq=pq, ss_n=sS.n):
                        t.wait_ge(sS.h, ss_n)
                        for g in range(NG):
                            r = t.matmul(pq[:], lhsT=l1k_sb[:, g, hp * 128:(hp + 1) * 128],
                                         rhs=xn1b[:, g, :], start=(g == 0), stop=(g == NG - 1))
                        sT.tag(r)
                    _emit("tensor", p7_k_t)

                    def p7_k_s(s, hp=hp, pq=pq, t0=t0, st_n=sT.n):
                        s.wait_ge(sT.h, st_n)
                        sS.tag(s.activation(out=k1[:, hp, t0:t0 + TB], in_=pq[:], func=AF.Copy, scale=0.125))
                    _emit("scalar", p7_k_s)

                for sub in range(4):
                    pv = psC7 if sub % 2 == 0 else psD7
                    def p7_v_t(t, i=i, sub=sub, pv=pv, ss_n=sS.n):
                        t.wait_ge(sS.h, ss_n)
                        for hp in range(4):
                            for g in range(NG):
                                r = t.matmul(pv[:, hp * 128:(hp + 1) * 128],
                                             lhsT=xn1b[:, g, sub * 128:(sub + 1) * 128],
                                             rhs=l1v_sb[:, g, hp * 128:(hp + 1) * 128],
                                             start=(g == 0), stop=(g == NG - 1))
                        sT.tag(r)
                    _emit("tensor", p7_v_t)

                    def p7_v_s(s, i=i, sub=sub, pv=pv, st_n=sT.n):
                        s.wait_ge(sT.h, st_n)
                        kt = i * 4 + sub
                        for hp in range(4):
                            s.activation(out=v1[:, kt, hp * 130:hp * 130 + 64],
                                         in_=pv[:, hp * 128:hp * 128 + 64], func=AF.Copy)
                            r = s.activation(out=v1[:, kt, hp * 130 + 65:hp * 130 + 129],
                                             in_=pv[:, hp * 128 + 64:hp * 128 + 128], func=AF.Copy)
                        sS.tag(r)
                    _emit("scalar", p7_v_s)
                kv_done7 = sS.n

                nkt = 4 * i + 4
                for ht in range(4):
                    exp_done = {}
                    msk_done = {}
                    sc_done = {}

                    def mk_sc(kt, i=i, ht=ht, exp_done=exp_done, sc_done=sc_done,
                              kv_done7=kv_done7):
                        par = kt % 2
                        scA = psA7 if par == 0 else psC7
                        scB = psB7 if par == 0 else psD7

                        def p7_sc(t, kt=kt, scA=scA, scB=scB, ss_n=sS.n):
                            t.wait_ge(sS.h, kv_done7)
                            if kt >= 2:
                                t.wait_ge(sS.h, exp_done[kt - 2])
                            else:
                                t.wait_ge(sS.h, ss_n)
                            r = t.matmul(scA[:], lhsT=k1[0:64, ht, kt * 128:(kt + 1) * 128],
                                         rhs=qb7[0:64, ht, :], start=True, stop=True)
                            r = t.matmul(scB[:], lhsT=k1[64:128, ht, kt * 128:(kt + 1) * 128],
                                         rhs=qb7[64:128, ht, :], start=True, stop=True)
                            sT.tag(r)
                        _emit("tensor", p7_sc)
                        sc_done[kt] = sT.n

                    def mk_exp(kt, i=i, exp_done=exp_done, msk_done=msk_done, sc_done=sc_done):
                        par = kt % 2
                        scA = psA7 if par == 0 else psC7
                        scB = psB7 if par == 0 else psD7
                        diag = (kt >= 4 * i)

                        def p7_exp(s, kt=kt, par=par, scA=scA, scB=scB):
                            s.wait_ge(sT.h, sc_done[kt])
                            s.activation(out=atb7[:, 0, par, :], in_=scA[:], func=AF.Exp)
                            r = s.activation(out=atb7[:, 1, par, :], in_=scB[:], func=AF.Exp)
                            sS.tag(r)
                            exp_done[kt] = sS.n
                        _emit("scalar", p7_exp)

                        if diag:
                            j = kt - 4 * i

                            def p7_msk(v, kt=kt, par=par, j=j, e_n=sS.n):
                                v.wait_ge(sS.h, e_n)
                                for hh in range(2):
                                    if j > 0:
                                        v.memset(atb7[:, hh, par, 0:j * 128], 0.0)
                                    r = v.tensor_tensor(out=atb7[:, hh, par, j * 128:(j + 1) * 128],
                                                        in0=atb7[:, hh, par, j * 128:(j + 1) * 128],
                                                        in1=trib[:], op=A.mult)
                                sV.tag(r)
                                msk_done[kt] = sV.n
                            _emit("vector", p7_msk)

                    def mk_av(kt, i=i, ht=ht, nkt=nkt, exp_done=exp_done, msk_done=msk_done):
                        par = kt % 2
                        diag = (kt >= 4 * i)

                        def p7_av(t, kt=kt, par=par, diag=diag, sg_n=sG.n):
                            if kt == 0:
                                t.wait_ge(sG.h, sg_n)
                            t.wait_ge(sS.h, exp_done[kt])
                            if diag:
                                t.wait_ge(sV.h, msk_done[kt])
                            r = t.matmul(av07[0:65, :], lhsT=v1[:, kt, ht * 130:ht * 130 + 65],
                                         rhs=atb7[:, 0, par, :], start=(kt == 0), stop=(kt == nkt - 1))
                            r = t.matmul(av17[0:65, :], lhsT=v1[:, kt, ht * 130 + 65:ht * 130 + 130],
                                         rhs=atb7[:, 1, par, :], start=(kt == 0), stop=(kt == nkt - 1))
                            sT.tag(r)
                        _emit("tensor", p7_av)

                    for kt in range(nkt + 1):
                        if kt < nkt:
                            mk_sc(kt)
                        if kt >= 1:
                            mk_exp(kt - 1)
                            mk_av(kt - 1)

                    def p7_ep_v(v, ht=ht, st_n=sT.n, so_n=sO.n):
                        v.wait_ge(sT.h, st_n)
                        if ht == 0:
                            v.wait_ge(sO.h, so_n)
                        v.reciprocal(out=recd7[64:65, 0, :], in_=av07[64:65, :])
                        r = v.reciprocal(out=recd7[64:65, 1, :], in_=av17[64:65, :])
                        sV.tag(r)
                    _emit("vector", p7_ep_v)

                    def p7_ep_t(t, ht=ht, sv_n=sV.n):
                        t.wait_ge(sV.h, sv_n)
                        t.matmul(psM7[0:64, :], lhsT=onesf[64:65, 0:64], rhs=recd7[64:65, 0, :],
                                 start=True, stop=True)
                        r = t.matmul(psC7[0:64, :], lhsT=onesf[64:65, 0:64], rhs=recd7[64:65, 1, :],
                                     start=True, stop=True)
                        sT.tag(r)
                    _emit("tensor", p7_ep_t)

                    def p7_ep_sx(s, ht=ht, st_n=sT.n):
                        s.wait_ge(sT.h, st_n)
                        s.activation(out=ib7[:, 0, :], in_=psM7[0:64, :], func=AF.Copy)
                        r = s.activation(out=ib7[:, 1, :], in_=psC7[0:64, :], func=AF.Copy)
                        sS.tag(r)
                    _emit("scalar", p7_ep_sx)

                    def p7_ep_v2(v, ht=ht, ss_n=sS.n):
                        v.wait_ge(sS.h, ss_n)
                        v.tensor_tensor(out=nb7[:, 0, :], in0=av07[0:64, :], in1=ib7[:, 0, :], op=A.mult)
                        r = v.tensor_tensor(out=nb7[:, 1, :], in0=av17[0:64, :], in1=ib7[:, 1, :], op=A.mult)
                        sV.tag(r)
                    _emit("vector", p7_ep_v2)

                    def p7_ep_t2(t, ht=ht, sv_n=sV.n):
                        t.wait_ge(sV.h, sv_n)
                        t.matmul(psB7[:], lhsT=selb[:, 0:128], rhs=nb7[:, 0, :], start=True, stop=False)
                        r = t.matmul(psB7[:], lhsT=selb[:, 128:256], rhs=nb7[:, 1, :], start=False, stop=True)
                        sT.tag(r)
                    _emit("tensor", p7_ep_t2)

                    def p7_ep_s2(s, ht=ht, st_n=sT.n):
                        s.wait_ge(sT.h, st_n)
                        sS.tag(s.activation(out=aob7[:, ht, :], in_=psB7[:], func=AF.Copy))
                    _emit("scalar", p7_ep_s2)

                def p7_st(sy, i=i, t0=t0, ss_n=sS.n):
                    sy.wait_ge(sS.h, ss_n)
                    sO.inc(sy.dma_start(out=cc_ao1_in[:, :, t0:t0 + TB], in_=aob7[:]), 16)
                block.sync(p7_st)

            def p7_cc(g):
                sO.wait(g)
                sG.inc(g.collective_compute("AllGather", A.bypass, replica_groups=GROUPS_ALL,
                                            ins=[cc_ao1_in[:]], outs=[cc_ao1_out[:]]), 1)
            block.gpsimd(p7_cc)

            def p7_restage(sc):
                sG.wait(sc)
                p_ = pid(sc)
                role_ = _rv(binary_op(p_, make_scalar_value(1), A.bitwise_and), 1)
                hoff = _rv(binary_op(role_, make_scalar_value(HS), A.mult), HS)
                mybase = _rv(binary_op(p_, make_scalar_value(128), A.mult), 7 * 128)
                pbase = _rv(binary_op(_rv(binary_op(p_, make_scalar_value(1), A.bitwise_xor), 7),
                                      make_scalar_value(128), A.mult), 7 * 128)
                sD.inc(sc.dma_start(out=ao1_my[:], in_=cc_ao1_out[ds(mybase, 128), :, ds(hoff, HS)]), 16)
                sD.inc(sc.dma_start(out=ao1_pt[:], in_=cc_ao1_out[ds(pbase, 128), :, ds(hoff, HS)]), 16)
            block.scalar(p7_restage)

        phase_barrier()
        dbg_out("g1", (128, NG, HS), BF16)
        dbg_out("xq", (128, 16, D), BF16)
        es8 = ExitStack()
        def sb8(name, shape, dt=F32):
            return es8.enter_context(nc.sbuf_tensor(name, list(shape), dt))
        l1wo_sb = sb8("l1wo_sb", (128, NG, D), BF16)
        wdq_sb = sb8("wdq_sb", (128, 2, NFI, 128), BF16)
        xin8 = sb8("xin8", (128, D))
        q8_xs = sb8("q8_xs", (128, D))
        q8_a2 = sb8("q8_a2", (128, D))
        q8_q1 = sb8("q8_q1", (128, D))
        q8_q2 = sb8("q8_q2", (128, D))
        q8_sgn = sb8("q8_sgn", (128, D))
        q8_sg2 = sb8("q8_sg2", (128, D))
        q8_li = sb8("q8_li", (128, D), I32)
        q8_bm = sb8("q8_bm", (128, D // 16))
        q8_bmr = sb8("q8_bmr", (128, D // 16))
        q8_bmd = sb8("q8_bmd", (128, D // 16))
        xq8 = sb8("xq8", (128, 4, D), BF16)
        xqf = sb8("xqf", (128, NG, TB), BF16)
        wg8 = sb8("wg8", (128, 2, NG, TB), BF16)
        wu8 = sb8("wu8", (128, 2, NG, TB), BF16)
        h8s = sb8("h8s", (128, 2, TB))
        hq8 = sb8("hq8", (128, 2, TB), BF16)
        hqf = sb8("hqf", (128, NFI, TB), BF16)
        g18 = sb8("g18", (128, NG, TB), BF16)
        ao18 = sb8("ao18", (128, NG, TB), BF16)
        xn1fm8 = sb8("xn1fm8", (128, NG, TB), BF16)
        s1s = sb8("s1s", (128, TB))
        sg18 = sb8("sg18", (128, NG, TB), BF16)
        x1t8 = sb8("x1t8", (128, D))
        x2t8 = sb8("x2t8", (128, D))
        pg8 = es8.enter_context(nc.psum_tensor("pg8", [128, TB], F32))
        pu8 = es8.enter_context(nc.psum_tensor("pu8", [128, TB], F32))
        pt8 = es8.enter_context(nc.psum_tensor("pt8", [128, TB], BF16))
        pd8 = es8.enter_context(nc.psum_tensor("pd8", [128, TB], F32))
        pd8b = es8.enter_context(nc.psum_tensor("pd8b", [128, TB], F32))
        ppk8 = es8.enter_context(nc.psum_tensor("ppk8", [128, 1024], BF16))
        with es8:
            def q8bufs(F):
                return {"xs": q8_xs[:, 0:F], "a2": q8_a2[:, 0:F], "q1": q8_q1[:, 0:F],
                        "q2": q8_q2[:, 0:F], "sgn": q8_sgn[:, 0:F], "sgn2": q8_sg2[:, 0:F],
                        "li": q8_li[:, 0:F],
                        "bm": q8_bm[:, 0:F // 16], "bmr": q8_bmr[:, 0:F // 16],
                        "bmd": q8_bmd[:, 0:F // 16]}

            def p8_init(g):
                sWg.inc(g.dma_start(out=l1wo_sb[:], in_=l1_wo.rearrange("(g p) c -> p g c", p=128)), 16)
            block.gpsimd(p8_init)

            FBLK = [(0, 512), (512, 512), (1024, 512), (1536, 512), (2048, 512), (2560, 256)]
            hm_last = [0]
            qd_par = {0: 0, 1: 0}
            for ob in range(OBLK):
                ot = ob * TB

                def p8_ld(sy, ob=ob, ot=ot, sv_n=sV.n, ss_n=sS.n, st_n=sT.n, so_n=sO.n):
                    sy.wait_ge(sV.h, sv_n)
                    sy.wait_ge(sS.h, ss_n)
                    sy.wait_ge(sT.h, st_n)
                    sy.wait_ge(sO.h, so_n)
                    sA.inc(sy.dma_start(out=ao18[:, 0:4, :], in_=ao1_my[:, :, ot:ot + TB]), 16)
                    sA.inc(sy.dma_start(out=ao18[:, 4:8, :], in_=ao1_pt[:, :, ot:ot + TB]), 16)
                    sA.inc(sy.dma_start(out=xn1fm8[:], in_=cc_xn1_in.rearrange(
                        "(g p) t -> p g t", p=128)[:, :, ot:ot + TB]), 16)
                block.sync(p8_ld)

                def p8_ldw(g, ob=ob, st_n=sT.n):
                    g.wait_ge(sT.h, st_n)
                    pass
                block.gpsimd(p8_ldw)

                for sub in range(4):
                    def p8_qld(sy, ob=ob, sub=sub, sv_n=sV.n):
                        sy.wait_ge(sV.h, sv_n)
                        sA.inc(sy.dma_start(out=xin8[:], in_=xn1f_spill[:, ob * 4 + sub, :]), 16)
                    block.sync(p8_qld)

                    def p8_qw(v, sub=sub, sa_n=sA.n):
                        v.wait_ge(sA.h, sa_n)
                        sV.bump(v)
                    _emit("vector", p8_qw)
                    emit_quant(q8bufs(D), xin8[:], xq8[:, sub, :], D)

                    if "xq" in debug:
                        def dbg_xq(sy, ob=ob, sub=sub, sv_n=sV.n):
                            sy.wait_ge(sV.h, sv_n)
                            sO.inc(sy.dma_start(out=dbg["xq"][:, ob * 4 + sub, :], in_=xq8[:, sub, :]), 16)
                        block.sync(dbg_xq)

                    def p8_xt(t, sub=sub, sv_n=sV.n, ss_n=sS.n):
                        t.wait_ge(sV.h, sv_n)
                        t.wait_ge(sS.h, ss_n)
                        for g in range(NG):
                            r = t.transpose(ppk8[:, g * 128:(g + 1) * 128],
                                            xq8[:, sub, g * 128:(g + 1) * 128], ident[:])
                        sT.tag(r)
                    _emit("tensor", p8_xt)

                    def p8_xs(s, sub=sub, st_n=sT.n):
                        s.wait_ge(sT.h, st_n)
                        for g in range(NG):
                            r = s.activation(out=xqf[:, g, sub * 128:(sub + 1) * 128],
                                             in_=ppk8[:, g * 128:(g + 1) * 128], func=AF.Copy)
                        sS.tag(r)
                    _emit("scalar", p8_xs)

                def p8_front(sub, fb, f0, fw, par8, qch_done):
                    def p8_gw(g2, ob=ob, sub=sub, fb=fb, f0=f0, fw=fw, st_n=sT.n):
                        g2.wait_ge(sT.h, st_n)
                        sWg.inc(g2.dma_start(out=wg8[:, fb % 2, :, 0:fw],
                                             in_=wgq_spill[:, :, f0:f0 + fw]), 16)
                        sWg.inc(g2.dma_start(out=wu8[:, fb % 2, :, 0:fw],
                                             in_=wuq_spill[:, :, f0:f0 + fw]), 16)
                    block.gpsimd(p8_gw)

                    def p8_g_t(t, sub=sub, fb=fb, f0=f0, fw=fw, swg=sWg.n, ss_n=sS.n,
                               hm_n=hm_last[0]):
                        t.wait_ge(sWg.h, swg)
                        t.wait_ge(sS.h, ss_n)
                        if hm_n:
                            t.wait_ge(sV.h, hm_n)
                        for g in range(NG):
                            t.matmul(pg8[:, 0:fw], lhsT=xqf[:, g, sub * 128:(sub + 1) * 128],
                                     rhs=wg8[:, fb % 2, g, 0:fw], start=(g == 0), stop=(g == NG - 1))
                        for g in range(NG):
                            r = t.matmul(pu8[:, 0:fw], lhsT=xqf[:, g, sub * 128:(sub + 1) * 128],
                                         rhs=wu8[:, fb % 2, g, 0:fw], start=(g == 0), stop=(g == NG - 1))
                        sT.tag(r)
                    _emit("tensor", p8_g_t)

                    def p8_sil(s, fb=fb, fw=fw, par8=par8, st_n=sT.n, qd_n=qd_par[par8]):
                        s.wait_ge(sT.h, st_n)
                        if qd_n:
                            s.wait_ge(sV.h, qd_n)
                        sS.tag(s.activation(out=h8s[:, par8, 0:fw], in_=pg8[:, 0:fw], func=AF.Silu))
                    _emit("scalar", p8_sil)

                    def p8_hm(v, fb=fb, fw=fw, par8=par8, ss_n=sS.n):
                        v.wait_ge(sS.h, ss_n)
                        r = v.tensor_tensor(out=h8s[:, par8, 0:fw], in0=h8s[:, par8, 0:fw],
                                            in1=pu8[:, 0:fw], op=A.mult)
                        sV.tag(r)
                    _emit("vector", p8_hm)
                    hm_last[0] = sV.n
                    emit_quant(q8bufs(fw), h8s[:, par8, 0:fw], hq8[:, par8, 0:fw], fw)
                    qd_par[par8] = sV.n
                    qch_done[0] = sV.n

                def p8_back(sub, fb, f0, fw, par8, qch_n):
                    nsubt = fw // 128
                    def p8_ht(t, sub=sub, fb=fb, f0=f0, par8=par8, nsubt=nsubt,
                              qch_n=qch_n, ss_n=sS.n):
                        t.wait_ge(sV.h, qch_n)
                        t.wait_ge(sS.h, ss_n)
                        for cch in range(nsubt):
                            r = t.transpose(pt8[:, cch * 128:(cch + 1) * 128],
                                            hq8[:, par8, cch * 128:(cch + 1) * 128], ident[:])
                        sT.tag(r)
                    _emit("tensor", p8_ht)

                    def p8_hs(s, sub=sub, fb=fb, f0=f0, nsubt=nsubt, st_n=sT.n):
                        s.wait_ge(sT.h, st_n)
                        for cch in range(nsubt):
                            r = s.activation(out=hqf[:, f0 // 128 + cch, sub * 128:(sub + 1) * 128],
                                             in_=pt8[:, cch * 128:(cch + 1) * 128], func=AF.Copy)
                        sS.tag(r)
                    _emit("scalar", p8_hs)

                steps = [(sub, fb, f0, fw) for sub in range(4)
                         for fb, (f0, fw) in enumerate(FBLK)]
                pend = None
                for sub, fb, f0, fw in steps:
                    qch_done = [0]
                    p8_front(sub, fb, f0, fw, fb % 2, qch_done)
                    if pend is not None:
                        p8_back(*pend)
                    pend = (sub, fb, f0, fw, fb % 2, qch_done[0])
                p8_back(*pend)

                for oc in range(NG):
                    pdd = pd8 if oc % 2 == 0 else pd8b
                    def p8_dw(g2, oc=oc, st_n=sT.n):
                        g2.wait_ge(sT.h, st_n)
                        sWg.inc(g2.dma_start(out=wdq_sb[:, oc % 2, :, :],
                                             in_=wdq_spill[:, :, oc * 128:(oc + 1) * 128]), 16)
                    block.gpsimd(p8_dw)

                    def p8_d_t(t, oc=oc, pdd=pdd, ss_n=sS.n, swg=sWg.n):
                        t.wait_ge(sS.h, ss_n)
                        t.wait_ge(sWg.h, swg)
                        for g in range(NFI):
                            r = t.matmul(pdd[:], lhsT=wdq_sb[:, oc % 2, g, :],
                                         rhs=hqf[:, g, :], start=(g == 0), stop=(g == NFI - 1))
                        sT.tag(r)
                    _emit("tensor", p8_d_t)

                    def p8_d_s(s, oc=oc, pdd=pdd, st_n=sT.n):
                        s.wait_ge(sT.h, st_n)
                        sS.tag(s.activation(out=g18[:, oc, :], in_=pdd[:], func=AF.Silu))
                    _emit("scalar", p8_d_s)

                if "g1" in debug:
                    def dbg_g1(sy, ob=ob, ot=ot, ss_n=sS.n):
                        sy.wait_ge(sS.h, ss_n)
                        sO.inc(sy.dma_start(out=dbg["g1"][:, :, ot:ot + TB], in_=g18[:]), 16)
                    block.sync(dbg_g1)

                for g in range(NG):
                    pdd = pd8 if g % 2 == 0 else pd8b
                    def p8_s1_t(t, g=g, pdd=pdd, sa_n=sA.n, swg=sWg.n, sv_n=sV.n, ss_n=sS.n):
                        t.wait_ge(sA.h, sa_n)
                        t.wait_ge(sWg.h, swg)
                        t.wait_ge(sS.h, ss_n)
                        if g >= 2:
                            t.wait_ge(sV.h, sv_n)
                        for j in range(NG):
                            r = t.matmul(pdd[:], lhsT=l1wo_sb[:, j, g * 128:(g + 1) * 128],
                                         rhs=ao18[:, j, :], start=(j == 0), stop=(j == NG - 1))
                        sT.tag(r)
                    _emit("tensor", p8_s1_t)

                    def p8_s1_v(v, g=g, pdd=pdd, st_n=sT.n, ss_n=sS.n):
                        v.wait_ge(sT.h, st_n)
                        v.wait_ge(sS.h, ss_n)
                        r = v.tensor_tensor(out=s1s[:], in0=pdd[:], in1=xn1fm8[:, g, :], op=A.add)
                        sV.edge(v, r)
                        r = v.tensor_tensor(out=sg18[:, g, :], in0=s1s[:], in1=g18[:, g, :], op=A.mult)
                        sV.tag(r)
                    _emit("vector", p8_s1_v)

                for sub in range(4):
                    def p8_tr_t(t, sub=sub, sv_n=sV.n, so_n=sO.n):
                        t.wait_ge(sV.h, sv_n)
                        for g in range(NG):
                            r = t.transpose(ppk8[:, g * 128:(g + 1) * 128],
                                            sg18[:, g, sub * 128:(sub + 1) * 128], ident[:])
                        sT.tag(r)
                    _emit("tensor", p8_tr_t)

                    def p8_x1ld(sy, ob=ob, sub=sub, sv_n=sV.n):
                        sy.wait_ge(sV.h, sv_n)
                        sA.inc(sy.dma_start(out=x1t8[:], in_=x1_spill[:, ob * 4 + sub, :]), 16)
                    block.sync(p8_x1ld)

                    def p8_x2_v(v, sub=sub, st_n=sT.n, so_n=sO.n, sa_n=sA.n):
                        v.wait_ge(sT.h, st_n)
                        v.wait_ge(sO.h, so_n)
                        v.wait_ge(sA.h, sa_n)
                        r = v.tensor_tensor(out=x2t8[:], in0=x1t8[:], in1=ppk8[:], op=A.add)
                        sV.tag(r)
                    _emit("vector", p8_x2_v)

                    def p8_y(sy, ob=ob, sub=sub, sv_n=sV.n):
                        sy.wait_ge(sV.h, sv_n)
                        tt = ob * 4 + sub
                        sO.inc(sy.dma_start(out=y_out.rearrange("(n p) d -> p n d", p=128)[:, tt, :],
                                            in_=x2t8[:]), 16)
                    block.sync(p8_y)

        phase_barrier()
        def fin(sy):
            sO.wait(sy)
            sG.wait(sy)
        block.sync(fin)

    return nc, dbg



def _shard_inputs(inputs, c):
    b, r = c // 2, c % 2
    I = inputs
    t = lambda a: np.ascontiguousarray(a)
    x = I["L0_attn_x"][b]
    m = {}
    m["x_half"] = t(x[r * HS:(r + 1) * HS])
    ocols = list(range(r * DH, (r + 1) * DH)) + list(range(2048 + r * DH, 2048 + (r + 1) * DH))
    m["w_in"] = t(I["L0_attn_W_in"][:, ocols])
    m["conv_w"] = t(I["L0_attn_conv_w"][r * DH:(r + 1) * DH, 0, :].reshape(NG, 128, 4).transpose(1, 0, 2))
    m["conv_b"] = t(I["L0_attn_conv_b"][r * DH:(r + 1) * DH].reshape(NG, 128).T)
    m["w_x"] = t(I["L0_attn_W_x"][r * DH:(r + 1) * DH])
    m["w_dt"] = t(I["L0_attn_W_dt"][:, r * DH:(r + 1) * DH])
    m["b_dt"] = t(I["L0_attn_b_dt"][r * DH:(r + 1) * DH].reshape(NG, 128).T)
    m["a_log"] = t(I["L0_attn_A_log"][r * DH:(r + 1) * DH].reshape(NG, 128, NST).transpose(1, 0, 2))
    m["d_vec"] = t(I["L0_attn_D"][r * DH:(r + 1) * DH].reshape(NG, 128).T)
    m["w_out"] = t(I["L0_attn_W_out"][r * DH:(r + 1) * DH])
    m["anorm_w"] = t(I["L0_ffn_attn_norm_w"].reshape(NG, 128).T)
    m["fnorm_w"] = t(I["L0_ffn_ffn_norm_w"].reshape(NG, 128).T)
    pair_cols = []
    for ht in range(4):
        for hl in (ht, ht + 4):
            gh = 8 * r + hl
            pair_cols.extend(range(gh * 64, (gh + 1) * 64))
    m["wqT"] = t(I["L0_ffn_wq"].T[:, pair_cols])
    m["wkT"] = t(I["L0_ffn_wk"].T[:, r * 128:(r + 1) * 128])
    m["wvT"] = t(I["L0_ffn_wv"].T[:, r * 128:(r + 1) * 128])
    woTfull = I["L0_ffn_wo"].T
    pair_rows = []
    for rr in (r, 1 - r):
        for ht in range(4):
            for hl in (ht, ht + 4):
                gh = 8 * rr + hl
                pair_rows.extend(range(gh * 64, (gh + 1) * 64))
    m["woT"] = t(woTfull[pair_rows])
    m["gate_wT"] = t(I["L0_ffn_gate_w"].T)
    m["expert_wT"] = t(np.stack([I["L0_ffn_expert_w"][e].T for e in range(8)]))
    m["l1_wq"] = t(I["L1_attn_Wq"][:, pair_cols])
    m["l1_wk"] = t(I["L1_attn_Wk"][:, pair_cols])
    m["l1_wv"] = t(I["L1_attn_Wv"][:, pair_cols])
    m["l1_wo"] = t(I["L1_attn_Wo"][pair_rows])
    m["l1_wg"] = t(I["L1_ffn_w_gate"])
    m["l1_wu"] = t(I["L1_ffn_w_up"])
    m["l1_wd"] = t(I["L1_ffn_w_down"].reshape(NG, 128, FFI))
    m["ident_in"] = np.eye(128, dtype=np.float32)
    m["ones_in"] = np.ones((128, 128), np.float32)
    kk, qq = np.meshgrid(np.arange(128), np.arange(128), indexing="ij")
    m["tri_in"] = (qq >= kk).astype(np.float32)
    h8 = np.zeros((8, 1024), np.float32)
    for j in range(8):
        h8[j, j * 128:(j + 1) * 128] = 1.0
    m["hot8_in"] = h8
    sel = np.zeros((64, 256), np.float32)
    sel[:, 0:64] = np.eye(64)
    sel[:, 128 + 64:256] = np.eye(64)
    m["sel_in"] = sel
    h16 = np.zeros((16, 2048), np.float32)
    for j in range(16):
        h16[j, j * 128:(j + 1) * 128] = 1.0
    m["hot16_in"] = h16
    return m


_BUILD_CACHE = {}


def kernel(**inputs):
    debug = tuple(os.environ.get("MK_DEBUG", "").split(",")) if os.environ.get("MK_DEBUG") else ()
    key = debug
    if key not in _BUILD_CACHE:
        _BUILD_CACHE[key] = build_nc(debug)
    nc, dbg = _BUILD_CACHE[key]
    in_maps = [_shard_inputs(inputs, c) for c in range(8)]
    res = run_bass_kernel_spmd(nc, in_maps, core_ids=list(range(8)))
    kernel.dbg_results = res.results
    out = np.empty((4, S, D), np.float32)
    for c in range(8):
        b, r = c // 2, c % 2
        out[b, r * HS:(r + 1) * HS] = res.results[c]["y_out"]
    return out
